# version 02
# baseline (speedup 1.0000x reference)
import sys, os
if "/opt/trn_rl_repo" not in sys.path:
    sys.path.insert(0, "/opt/trn_rl_repo")

import numpy as np
import concourse.bass as bass
import concourse.mybir as mybir
from concourse.bass_utils import run_bass_kernel_spmd
from concourse.bass_isa import ReduceOp
import concourse.library_config as libcfg

F32 = mybir.dt.float32
BF16 = mybir.dt.bfloat16
I32 = mybir.dt.int32
U8 = mybir.dt.uint8
ALU = mybir.AluOpType
AF = mybir.ActivationFunctionType
AX = mybir.AxisListType

B, S, D = 4, 4096, 1024
NCORES = 8
TOK = 2048
H0, DK0, CH0 = 8, 64, 512
FF = 3072
CHUNK = 128
NCH = TOK // CHUNK
H1, DK1, CH1 = 8, 64, 512
QMAX = 32767.0
EPS_LN = 1e-5
EPS_ATT = 1e-6
PAIRS = [[0, 1], [2, 3], [4, 5], [6, 7]]
ALL8 = [list(range(8))]
NSEG_DEFAULT = int(os.environ.get("KSEG", "99"))
SUB3 = int(os.environ.get("KSUB3", "9"))
KPHI = int(os.environ.get("KPHI", "1"))
KGRP = int(os.environ.get("KGRP", "99"))
KBANK0 = int(os.environ.get("KBANK0", "0"))
KNONOP = int(os.environ.get("KNONOP", "0"))
KSC4 = int(os.environ.get("KSC4", "9"))
KODD = int(os.environ.get("KODD", "1"))


def rowsz(t):
    r = 1
    for s in t.shape[1:]:
        r *= s
    return r


def ap3(t, p0, np_, base, dims):
    return bass.AP(t, p0 * rowsz(t) + base, [[rowsz(t), np_]] + [list(d) for d in dims])


class Sem:
    __slots__ = ("name", "h", "v", "maxwait")

    def __init__(self, name, h):
        self.name, self.h, self.v, self.maxwait = name, h, 0, 0


class Arena:

    def __init__(self, nc, nbytes):
        self.nc = nc
        self.res = nc.alloc_sbuf_tensor("ARENA", [128, nbytes], U8)
        self.base = nc.lookup_mloc(self.res).addr
        self.off = 0
        self.size = nbytes
        self.n = 0

    def alloc(self, name, shape, dtype):
        per = 1
        for s in shape[1:]:
            per *= s
        per *= mybir.dt.size(dtype)
        self.off = (self.off + 63) & ~63
        assert self.off + per <= self.size, f"arena overflow at {name}: {self.off}+{per}"
        self.n += 1
        h = self.nc.virtual_sbuf_tensor(f"{name}_{self.n}", list(shape), dtype)
        self.nc._tensor(h.name, h.shape, h.dtype, type="SB")
        mloc = self.nc.lookup_mloc(h)
        mloc.allocated = True
        mloc.addr = self.base + self.off
        self.off += per
        return h

    def mark(self):
        return self.off

    def reset(self, m):
        self.off = m


class KB:
    ENGS = ("tensor", "vector", "scalar", "sync", "gpsimd")

    def __init__(self, nc):
        self.nc = nc
        self.prog = []
        self.meta = []
        self.sems = {}
        self.tapinfo = {}
        self.auto = {}

    def sem(self, name):
        s = Sem(name, self.nc.alloc_semaphore(name=name))
        self.sems[name] = s
        return s

    def op(self, eng, fn, waits=(), inc=None, inc_by=1, dbg=""):
        if inc is None:
            inc = self.auto.get(eng)
        waits = list(waits)
        if eng in ("vector", "scalar", "gpsimd") and eng in self.auto:
            s_ = self.auto[eng]
            waits.append((s_, s_.v))
        waits = tuple((s, v) for (s, v) in waits if s is not None and v > 0)
        for s, v in waits:
            s.maxwait = max(s.maxwait, v)

        def cl(e, fn=fn, waits=waits, inc=inc, inc_by=inc_by):
            for s, v in waits:
                e.wait_ge(s.h, v)
            ins = fn(e)
            if inc is not None:
                assert ins is not None, "inc on an op that emitted no instruction"
                ins.then_inc(inc.h, inc_by)
        self.prog.append((eng, cl))
        self.meta.append((eng, waits, inc, inc_by, dbg))
        if inc is not None:
            inc.v += inc_by
            return inc.v
        return None

    def dma_raw(self, q, dst, src, sem, waits=(), transpose=False, dbg="dma"):
        def fn(e, dst=dst, src=src, tp=transpose):
            if tp:
                return e.dma_start(out=dst, in_=src, transpose=True)
            return e.dma_start(out=dst, in_=src)
        return self.op(q, fn, waits=waits, inc=sem, inc_by=16, dbg=dbg)


    def wait_only(self, eng, waits, dbg="w"):
        waits = [(s, v) for (s, v) in waits if s is not None and v > 0]
        for s, v in waits:
            s.maxwait = max(s.maxwait, v)

        def cl(e, waits=tuple(waits)):
            for s, v in waits:
                e.wait_ge(s.h, v)
        self.prog.append((eng, cl))
        self.meta.append((eng, tuple(waits), None, 0, dbg))

    def barrier(self, dbg="barrier"):
        waits = [(s, s.v) for s in self.sems.values() if s.v > 0]
        for e in self.ENGS:
            self.wait_only(e, waits, dbg=dbg + "_" + e)

    def mm(self, out, lhsT, rhs, start, stop, waits=(), inc=None, dbg="mm", tp=None):
        return self.op(
            "tensor",
            lambda e, o=out, l=lhsT, r=rhs, st=start, sp=stop, tp=tp:
                e.matmul(o, l, r, start=st, stop=sp, skip_group_check=True,
                         tile_position=tp),
            waits=waits, inc=inc, dbg=dbg)

    def check(self):
        for s in self.sems.values():
            assert s.maxwait <= s.v, f"sem {s.name}: wait {s.maxwait} > final {s.v}"
        streams = {e: [] for e in self.ENGS}
        for (e, w, inc, ib, dbg) in self.meta:
            streams[e].append((w, inc, ib, dbg))
        cnt = {name: 0 for name in self.sems}
        ptr = {e: 0 for e in self.ENGS}
        progress = True
        while progress:
            progress = False
            for e in self.ENGS:
                while ptr[e] < len(streams[e]):
                    w, inc, ib, dbg = streams[e][ptr[e]]
                    if all(cnt[s.name] >= v for s, v in w):
                        if inc is not None:
                            cnt[inc.name] += ib
                        ptr[e] += 1
                        progress = True
                    else:
                        break
        stuck = {e: ptr[e] for e in self.ENGS if ptr[e] < len(streams[e])}
        if stuck:
            msg = []
            for e, p in stuck.items():
                w, inc, ib, dbg = streams[e][p]
                need = [(s.name, v, cnt[s.name]) for s, v in w if cnt[s.name] < v]
                msg.append(f"{e}@{p} '{dbg}' needs {need}")
            raise AssertionError("DEADLOCK in static check: " + "; ".join(msg))


class Stream:

    def __init__(self, kb, q, name, R=4):
        self.kb, self.q, self.R = kb, q, R
        self.sems = [kb.sem(f"{name}{i}") for i in range(R)]
        self.g = 0
        self._open = False

    def dma(self, dst, src, waits=(), transpose=False, dbg="dma", group=False):
        s = self.sems[self.g % self.R]
        if not group or not self._open:
            if not group:
                self.g += 1
                s = self.sems[self.g % self.R]
            gate = [(s, s.v)]
            self._open = True
        else:
            gate = []
        v = self.kb.dma_raw(self.q, dst, src, s, waits=list(waits) + gate,
                            transpose=transpose, dbg=dbg)
        return (s, v)

    def new_group(self):
        self._open = False


def build(nseg=NSEG_DEFAULT, taps=()):
    taps = set(taps)
    nc = bass.Bass("TRN2", target_bir_lowering=False)
    kb = KB(nc)

    def inp(name, shape, dtype=F32):
        return nc.dram_tensor(name, list(shape), dtype, kind="ExternalInput")

    xT = inp("xT", [D, TOK])
    dlog_c = inp("dlog_c", [8, 1]); dlog_r = inp("dlog_r", [1, 8])
    wqT = inp("wqT", [D, CH0]); wkT = inp("wkT", [D, CH0])
    wvT = inp("wvT", [D, CH0]); wgT = inp("wgT", [D, CH0])
    wgb_row = inp("wgb_row", [1, CH0])
    woT = inp("woT", [CH0, D])
    wup = inp("wup", [FF, D]); wgate = inp("wgate", [FF, D])
    wdown = inp("wdown", [D, FF])
    l1_wqT = inp("l1_wqT", [D, CH1]); l1_wkT = inp("l1_wkT", [D, CH1])
    l1_wvT = inp("l1_wvT", [D, CH1]); l1_woT = inp("l1_woT", [CH1, D])
    l1_wfT = inp("l1_wfT", [D, D])
    l1_sb = inp("l1_sb", [128, 16])
    flags = inp("flags", [1, 8])
    hsel_ch = inp("hsel_ch", [8, CH0])
    hsel65 = inp("hsel65", [8, 4 * 65])
    psel = inp("psel", [8, 128])

    out_T = nc.dram_tensor("out_T", [D, TOK], F32, kind="ExternalOutput")

    dram = lambda n, s, dt=BF16: nc.dram_tensor(n, list(s), dt)
    d_wq = dram("d_wq", [D, CH0]); d_wk = dram("d_wk", [D, CH0])
    d_wv = dram("d_wv", [D, CH0]); d_wg = dram("d_wg", [D, CH0])
    d_wo = dram("d_wo", [CH0, D])
    d_wupq = dram("d_wupq", [FF, D]); d_wgtq = dram("d_wgtq", [FF, D])
    d_wdnq = dram("d_wdnq", [D, FF])
    d_wupqT = dram("d_wupqT", [D, FF]); d_wgtqT = dram("d_wgtqT", [D, FF])
    d_wdnqT = dram("d_wdnqT", [FF, D])
    d_t5 = dram("d_t5", [FF, TOK])
    d_u = dram("d_u", [FF, TOK]); d_g = dram("d_g", [FF, TOK]); d_t4 = dram("d_t4", [FF, TOK])
    d_x1 = dram("d_x1", [D, TOK], F32)
    d_g1 = dram("d_g1", [D, TOK])
    d_l1wq = dram("d_l1wq", [D, CH1]); d_l1wk = dram("d_l1wk", [D, CH1])
    d_l1wv = dram("d_l1wv", [D, CH1]); d_l1wo = dram("d_l1wo", [CH1, D])
    c_ar_i = [dram(f"c_ar_i{i}", [1, 16], F32) for i in range(4)]
    c_ar_o = [dram(f"c_ar_o{i}", [1, 16], F32) for i in range(4)]
    c_st_i = dram("c_st_i", [128, 4 * 65], F32)
    c_st_o = dram("c_st_o", [128, 4 * 65], F32)
    c_ag_i = dram("c_ag_i", [D, TOK])
    c_ag_o = dram("c_ag_o", [2 * D, TOK])
    c_rs_i = [dram(f"c_rs_i{i}", [1024, D]) for i in range(4)]
    c_rs_o = [dram(f"c_rs_o{i}", [512, D]) for i in range(4)]

    SV = kb.sem("SV"); SS = kb.sem("SS"); ST = kb.sem("ST")
    SG = kb.sem("SG")
    SC = kb.sem("SC")
    kb.auto = {"vector": SV, "scalar": SS, "tensor": ST, "gpsimd": SG}
    LD = Stream(kb, "sync", "LD", 4)
    TP = Stream(kb, "sync", "TP", 4)
    GD = Stream(kb, "gpsimd", "GD", 4)

    pb = [nc.alloc_psum_tensor(f"pb{i}", [128, 512], F32) for i in range(8)]
    bank_free = {i: (None, 0) for i in range(8)}

    def bw(*idx):
        out = []
        for i in idx:
            s, v = bank_free[i]
            if s is not None:
                out.append((s, v))
        return out

    def brel(i, sem, val):
        bank_free[i] = (sem, val)

    AR = Arena(nc, ((nc.sbuf_bytes_remaining - 2048) // 64) * 64)

    ones128 = AR.alloc("ones128", [128, 1], F32)
    ones1r = AR.alloc("ones1r", [1, 128], F32)
    ones8x64 = AR.alloc("ones8x64", [8, 64], F32)
    mask_all = AR.alloc("mask_all", [128, H0 * 128], F32)
    gammaQ = AR.alloc("gammaQ", [128, 4 * 128], F32)
    gammaK = AR.alloc("gammaK", [128, CH0], F32)
    gammaC = AR.alloc("gammaC", [128, 4 * 65], F32)
    t_brep = AR.alloc("t_brep", [128, CH0], F32)
    t_flag = AR.alloc("t_flag", [1, 8], F32)
    fl_contrib = AR.alloc("fl_contrib", [128, 1], F32)
    fl_use = AR.alloc("fl_use", [128, 1], F32)
    t_l1sb = AR.alloc("t_l1sb", [128, 16], F32)
    p_eps = AR.alloc("p_eps", [1, 2], F32)
    t_s = [AR.alloc(f"t_s{i}", [128, 4], F32) for i in range(1, 6)]
    t_locmax = AR.alloc("t_locmax", [128, 2], F32)
    t_ar = AR.alloc("t_ar", [1, 16], F32)
    t_arstage = AR.alloc("t_arstage", [1, 16], F32)
    t_ident = AR.alloc("t_ident", [128, 128], F32)
    t_row128 = AR.alloc("t_row128", [1, 128], F32)
    PONE2 = AR.alloc("PONE2", [128, 2], F32)
    PH1 = AR.alloc("PH1", [128, 512], F32)
    PH2 = AR.alloc("PH2", [128, 512], F32)

    XN2 = AR.alloc("XN2", [128, 8 * TOK], BF16)
    YGT = AR.alloc("YGT", [128, 4 * TOK], BF16)

    MK_SETUP = AR.mark()
    t_dlc = AR.alloc("t_dlc", [8, 1], F32)
    t_dlr = AR.alloc("t_dlr", [1, 8], F32)
    t_g8 = AR.alloc("t_g8", [8, 1], F32)
    t_lg8 = AR.alloc("t_lg8", [8, 1], F32)
    t_lgr = AR.alloc("t_lgr", [1, 8], F32)
    t_lgr2 = AR.alloc("t_lgr2", [1, 8], F32)
    t_lgbc = AR.alloc("t_lgbc", [128, 8], F32)
    t_gc8 = AR.alloc("t_gc8", [8, 1], F32)
    t_hsel = AR.alloc("t_hsel", [8, CH0], F32)
    t_hsel65 = AR.alloc("t_hsel65", [8, 4 * 65], F32)
    t_psel = AR.alloc("t_psel", [8, 128], F32)
    t_iotaD = AR.alloc("t_iotaD", [128, 128], F32)
    t_reluD = AR.alloc("t_reluD", [128, 128], F32)
    t_causal = AR.alloc("t_causal", [128, 128], F32)
    t_jcol = AR.alloc("t_jcol", [128, 1], F32)
    t_rev = AR.alloc("t_rev", [128, 1], F32)
    t_jp1 = AR.alloc("t_jp1", [128, 1], F32)
    t_ramp = AR.alloc("t_ramp", [128, 128], F32)
    t_lnch = AR.alloc("t_lnch", [1, CH0], F32)
    t_lnchb = AR.alloc("t_lnchb", [128, CH0], F32)
    t_lnG = AR.alloc("t_lnG", [128, 4], F32)
    t_wgb = AR.alloc("t_wgb", [1, CH0], F32)
    p_128 = AR.alloc("p_128", [8, 2], F32)
    t_scr8 = AR.alloc("t_scr8", [8, 4 * 65], F32)

    def tap(name, src_ap, shape, waits, dtype=F32):
        if name not in taps:
            return
        dt_ = nc.dram_tensor("tap_" + name, list(shape), dtype, kind="ExternalOutput")
        LD.dma(dt_.ap(), src_ap, waits=waits, dbg=f"tap_{name}")
        kb.tapinfo[name] = (list(shape), dtype)

    for i, (dst, src) in enumerate([
            (t_dlc[:, :], dlog_c.ap()), (t_dlr[:, :], dlog_r.ap()),
            (t_hsel[:, :], hsel_ch.ap()), (t_hsel65[:, :], hsel65.ap()), (t_psel[:, :], psel.ap()),
            (t_flag[:, :], flags.ap()), (t_l1sb[:, :], l1_sb.ap()),
            (t_wgb[:, :], wgb_row.ap())]):
        k_ld0 = LD.dma(dst, src, dbg="ld_small", group=(i > 0))
    for i, (dd, ss_) in enumerate([(d_wq, wqT), (d_wk, wkT), (d_wv, wvT),
                                   (d_wg, wgT), (d_wo, woT)]):
        k_cast = GD.dma(dd.ap(), ss_.ap(), dbg="cast_w0", group=(i > 0))

    def g_iota(t_ap, pattern, base, chm):
        return lambda e: e.iota(t_ap, pattern, base=base, channel_multiplier=chm,
                                allow_small_or_imprecise_dtypes=True)

    kb.op("gpsimd", g_iota(t_iotaD[:, :], [[1, 128]], 0, -1), dbg="iotaD")
    v_g_iota = kb.op("gpsimd", g_iota(t_jcol[:, 0:1], [[0, 1]], 0, 1), dbg="iotaJ") or SG.v

    for t_, val in [(ones128, 1.0), (ones1r, 1.0), (ones8x64, 1.0)]:
        kb.op("vector", lambda e, t=t_, v=val: e.memset(t[:, :], v), dbg="memset")
    kb.op("vector", lambda e: e.memset(p_eps[0:1, 0:1], 1.0), dbg="ms")
    kb.op("vector", lambda e: e.memset(p_eps[0:1, 1:2], EPS_LN), dbg="ms")
    kb.op("vector", lambda e: e.memset(p_128[:, 0:1], float(CHUNK)), dbg="ms")
    kb.op("vector", lambda e: e.memset(p_128[:, 1:2], 0.0), dbg="ms")
    kb.op("vector", lambda e: e.memset(t_arstage[:, :], 0.0), dbg="ms")
    for _ts in t_s:
        kb.op("vector", lambda e, t=_ts: e.memset(t[:, :], 0.0), dbg="ms_ts")
    v_vconst = SV.v
    v_ident = kb.op("vector", lambda e: e.tensor_single_scalar(
        t_ident[:, :], t_iotaD[:, :], 0.0, ALU.is_equal),
        waits=[(SG, v_g_iota)], inc=SV, dbg="ident")

    def bcast_rows(src_row_ap, dst_ap, ncols, bank, waits, dbg):
        v_mm = kb.mm(pb[bank][0:128, 0:ncols], ones1r[:, :], src_row_ap, True, True,
                     waits=list(waits) + bw(bank) + [(SV, v_vconst)], inc=ST, dbg=dbg + "m")
        v_cp = kb.op("vector", lambda e, d=dst_ap, b=bank, n=ncols: e.tensor_copy(
            d, pb[b][0:128, 0:n]), waits=[(ST, v_mm)], inc=SV, dbg=dbg + "c")
        brel(bank, SV, v_cp)
        return v_cp

    def preduce(col_ap, out11_ap, bank, waits, op, dbg):
        v_mm = kb.op("tensor", lambda e, b=bank, c=col_ap: e.transpose(
            pb[b][0:1, 0:128], c, t_ident[:, :]),
            waits=list(waits) + bw(bank) + [(SV, v_ident)], inc=ST, dbg=dbg + "t")
        kb.op("vector", lambda e, b=bank: e.tensor_copy(t_row128[:, :], pb[b][0:1, 0:128]),
              waits=[(ST, v_mm)], dbg=dbg + "c")
        v_rd = kb.op("vector", lambda e, o=out11_ap, op=op: e.tensor_reduce(
            o, t_row128[0:1, :], AX.X, op), inc=SV, dbg=dbg + "r")
        brel(bank, SV, v_rd)
        return v_rd

    v_tmp = kb.op("scalar", lambda e: e.activation(t_g8[:, :], t_dlc[:, :], AF.Sigmoid),
                  waits=[k_ld0], inc=SS, dbg="gam")
    v_s_lg8 = kb.op("scalar", lambda e: e.activation(t_lg8[:, :], t_g8[:, :], AF.Ln),
                    waits=[(SS, v_tmp)], inc=SS, dbg="lngam")
    v_tmp = kb.op("scalar", lambda e: e.activation(t_lgr[:, :], t_dlr[:, :], AF.Sigmoid),
                  waits=[k_ld0], inc=SS, dbg="gamr")
    v_s_lgr = kb.op("scalar", lambda e: e.activation(t_lgr2[:, :], t_lgr[:, :], AF.Ln),
                    waits=[(SS, v_tmp)], inc=SS, dbg="lngamr")
    v_s_gc8 = kb.op("scalar", lambda e: e.activation(
        t_gc8[:, :], t_lg8[:, :], AF.Exp, scale=p_128[:, 0:1], bias=p_128[:, 1:2]),
        waits=[(SV, v_vconst), (SS, v_s_lg8)], inc=SS, dbg="gc8")
    v_g_lgbc = bcast_rows(t_lgr2[0:1, :], t_lgbc[:, 0:8], 8, 6, [(SS, v_s_lgr)], "lgbc")

    v_v_caus = kb.op("vector", lambda e: e.tensor_single_scalar(
        t_causal[:, :], t_iotaD[:, :], 0.0, ALU.is_ge),
        waits=[(SG, v_g_iota)], inc=SV, dbg="caus")
    v_s_relu = kb.op("scalar", lambda e: e.activation(t_reluD[:, :], t_iotaD[:, :], AF.Relu),
                     waits=[(SG, v_g_iota)], inc=SS, dbg="reluD")
    kb.op("vector", lambda e: e.tensor_scalar(
        t_rev[:, 0:1], t_jcol[:, 0:1], -1.0, 127.0, ALU.mult, ALU.add),
        waits=[(SG, v_g_iota)], dbg="rev")
    kb.op("vector", lambda e: e.tensor_scalar_add(t_jp1[:, 0:1], t_jcol[:, 0:1], 1.0), dbg="jp1")
    kb.op("vector", lambda e: e.tensor_single_scalar(
        PONE2[:, 0:1], t_jcol[:, 0:1], 64.0, ALU.is_lt), dbg="pmb0")
    kb.op("vector", lambda e: e.tensor_single_scalar(
        PONE2[:, 1:2], t_jcol[:, 0:1], 64.0, ALU.is_ge), dbg="pmb1")
    v_v_rev = kb.op("vector", lambda e: e.tensor_scalar(
        t_ramp[:, :], t_iotaD[:, :], t_jp1[:, 0:1], None, ALU.add), inc=SV, dbg="ramp")

    v_ss = v_s_relu
    for h in range(H0):
        v_vv = kb.op("vector", lambda e, h=h: e.tensor_scalar_mul(
            PH1[:, 0:128], t_reluD[:, :], t_lgbc[:, h:h + 1]),
            waits=[(SS, v_ss), (SV, v_g_lgbc)], inc=SV, dbg=f"mk{h}a")
        v_ss = kb.op("scalar", lambda e: e.activation(PH2[:, 0:128], PH1[:, 0:128], AF.Exp),
                     waits=[(SV, v_vv)], inc=SS, dbg=f"mk{h}b")
        kb.op("vector", lambda e, h=h: e.tensor_mul(
            mask_all[:, h * 128:(h + 1) * 128], PH2[:, 0:128], t_causal[:, :]),
            waits=[(SS, v_ss), (SV, v_v_caus)], dbg=f"mk{h}c")
    v_masks = SV.v

    v_mm_lnch = kb.mm(pb[0][0:1, 0:512], t_lg8[:, :], t_hsel[:, :], True, True,
                      waits=[(SS, v_s_lg8), k_ld0] + bw(0), inc=ST, dbg="lnch")
    v_v_lnch = kb.op("vector", lambda e: e.tensor_copy(t_lnch[:, :], pb[0][0:1, 0:512]),
                     waits=[(ST, v_mm_lnch)], inc=SV, dbg="lnch_ev")
    brel(0, SV, v_v_lnch)
    v_g_lnchb = bcast_rows(t_lnch[0:1, :], t_lnchb[:, :], 512, 7, [(SV, v_v_lnch)], "lnchb")
    v_v_gk1 = kb.op("vector", lambda e: e.tensor_scalar_mul(
        gammaK[:, :], t_lnchb[:, :], t_rev[:, 0:1]),
        waits=[(SV, max(v_g_lnchb, v_v_rev))], inc=SV, dbg="gK1")
    v_s_gK = kb.op("scalar", lambda e: e.activation(gammaK[:, :], gammaK[:, :], AF.Exp),
                   waits=[(SV, v_v_gk1)], inc=SS, dbg="gK2")

    v_mmg = None
    for cc in range(4):
        v_mmg = kb.mm(pb[1][0:128, cc:cc + 1], t_hsel[:, cc * 128:(cc + 1) * 128], t_lg8[:, :],
                      cc == 0, cc == 3,
                      waits=(([(SS, v_s_lg8)] + bw(1)) if cc == 0 else []),
                      inc=(ST if cc == 3 else None), dbg=f"lnG{cc}")
    v_v_lnG = kb.op("vector", lambda e: e.tensor_copy(t_lnG[:, :], pb[1][0:128, 0:4]),
                    waits=[(ST, v_mmg)], inc=SV, dbg="lnG_ev")
    brel(1, SV, v_v_lnG)
    for cc in range(4):
        kb.op("vector", lambda e, cc=cc: e.tensor_scalar_mul(
            gammaQ[:, cc * 128:(cc + 1) * 128], t_ramp[:, :], t_lnG[:, cc:cc + 1]),
            waits=[(SV, max(v_v_lnG, v_v_rev))], dbg=f"gQ{cc}")
    v_v_gq = SV.v
    v_s_gQ = kb.op("scalar", lambda e: e.activation(gammaQ[:, :], gammaQ[:, :], AF.Exp),
                   waits=[(SV, v_v_gq)], inc=SS, dbg="gQ2")

    v_v_gca = kb.op("vector", lambda e: e.tensor_scalar_mul(
        t_scr8[:, :], t_hsel65[:, :], t_gc8[:, 0:1]),
        waits=[(SS, v_s_gc8), k_ld0], inc=SV, dbg="gCa")
    v_mm_gc = kb.mm(pb[2][0:128, 0:260], t_psel[:, :], t_scr8[:, 0:260], True, True,
                    waits=[(SV, v_v_gca), k_ld0] + bw(2), inc=ST, dbg="gCmm0")
    v_v_gC = kb.op("vector", lambda e: e.tensor_copy(gammaC[:, 0:260], pb[2][0:128, 0:260]),
                   waits=[(ST, v_mm_gc)], inc=SV, dbg="gCe0")
    brel(2, SV, v_v_gC)

    v_g_brep = bcast_rows(t_wgb[0:1, :], t_brep[:, :], 512, 6, [k_ld0], "brep")
    bcast_rows(t_flag[0:1, 0:1], fl_contrib[:, 0:1], 1, 7, [k_ld0], "flc")
    v_g_fl = bcast_rows(t_flag[0:1, 1:2], fl_use[:, 0:1], 1, 7, [k_ld0], "flu")

    if nseg < 1:
        return _finish(kb, nc)

    AR.reset(MK_SETUP)
    kb.barrier("rst_setup")
    MK_LN = AR.mark()
    XN1 = AR.alloc("XN1", [128, 8 * TOK], BF16)
    MK_B = AR.mark()
    XC = [AR.alloc(f"XC{i}", [128, TOK], F32) for i in range(2)]
    SQ = AR.alloc("SQ", [128, TOK], F32)
    MK_ROWS = AR.mark()
    m_row = AR.alloc("m_row", [1, TOK], F32)
    s_row = AR.alloc("s_row", [1, TOK], F32)
    mn_row = AR.alloc("mn_row", [1, TOK], F32)
    var_row = AR.alloc("var_row", [1, TOK], F32)
    sd_row = AR.alloc("sd_row", [1, TOK], F32)
    rstd_row = AR.alloc("rstd_row", [1, TOK], F32)
    m_rep = AR.alloc("m_rep", [128, TOK], F32)
    r_rep = AR.alloc("r_rep", [128, TOK], F32)

    v_t = 0
    v_tA = {}
    w_first = bw(0, 1, 2, 3, 4, 5, 6, 7)
    for cc in range(8):
        xc = XC[cc % 2]
        k_ld = LD.dma(xc[:, :], xT.ap()[cc * 128:(cc + 1) * 128, :],
                      waits=[(ST, v_tA.get(cc - 2, 0))], dbg=f"ld_x{cc}")
        v_sq = kb.op("vector", lambda e, xc=xc: e.tensor_mul(SQ[:, :], xc[:, :], xc[:, :]),
                     waits=([k_ld, (ST, v_t)] + (w_first if cc == 0 else [])),
                     inc=SV, dbg=f"lnsq{cc}")
        for tb in range(4):
            kb.mm(pb[tb][0:1, 0:512], ones128[:, :],
                  xc[:, tb * 512:(tb + 1) * 512],
                  cc == 0, cc == 7,
                  waits=([(SV, v_sq)] if tb == 0 else []), dbg=f"lnm{cc}{tb}")
            kb.mm(pb[4 + tb][0:1, 0:512], ones128[:, :], SQ[:, tb * 512:(tb + 1) * 512],
                  cc == 0, cc == 7, dbg=f"lns{cc}{tb}")
        v_t = ST.v
        v_tA[cc] = v_t
    for tb in range(4):
        kb.op("vector", lambda e, tb=tb: e.tensor_copy(
            m_row[:, tb * 512:(tb + 1) * 512], pb[tb][0:1, 0:512]),
            waits=[(ST, v_t)], dbg=f"lnem{tb}")
        kb.op("vector", lambda e, tb=tb: e.tensor_copy(
            s_row[:, tb * 512:(tb + 1) * 512], pb[4 + tb][0:1, 0:512]), dbg=f"lnes{tb}")
    v_ln_ev = SV.v
    for i in range(8):
        brel(i, SV, v_ln_ev)
    kb.op("vector", lambda e: e.tensor_scalar_mul(mn_row[:, :], m_row[:, :], 1.0 / D), dbg="lnr1")
    kb.op("vector", lambda e: e.tensor_scalar_mul(s_row[:, :], s_row[:, :], 1.0 / D), dbg="lnr2")
    kb.op("vector", lambda e: e.tensor_mul(var_row[:, :], mn_row[:, :], mn_row[:, :]), dbg="lnr3")
    v_v_var = kb.op("vector", lambda e: e.tensor_sub(var_row[:, :], s_row[:, :], var_row[:, :]),
                    inc=SV, dbg="lnr4")
    v_s_sd = kb.op("scalar", lambda e: e.activation(
        sd_row[:, :], var_row[:, :], AF.Sqrt, scale=p_eps[0:1, 0:1], bias=p_eps[0:1, 1:2]),
        waits=[(SV, v_v_var)], inc=SS, dbg="lnsqrt")
    v_v_rstd = kb.op("vector", lambda e: e.reciprocal(rstd_row[:, :], sd_row[:, :]),
                     waits=[(SS, v_s_sd)], inc=SV, dbg="lnrecip")
    for tb in range(4):
        b0, b1 = tb % 2, 2 + tb % 2
        kb.mm(pb[b0][0:128, 0:512], ones1r[:, :], mn_row[:, tb * 512:(tb + 1) * 512],
              True, True, waits=[(SV, v_v_rstd)] + bw(b0), dbg=f"lnrm{tb}")
        v_mm_rep = kb.mm(pb[b1][0:128, 0:512], ones1r[:, :], rstd_row[:, tb * 512:(tb + 1) * 512],
                         True, True, waits=bw(b1), inc=ST, dbg=f"lnrr{tb}")
        kb.op("vector", lambda e, tb=tb, b0=b0: e.tensor_copy(
            m_rep[:, tb * 512:(tb + 1) * 512], pb[b0][0:128, 0:512]),
            waits=[(ST, v_mm_rep)], dbg=f"lnre{tb}")
        v_v_rep = kb.op("vector", lambda e, tb=tb, b1=b1: e.tensor_copy(
            r_rep[:, tb * 512:(tb + 1) * 512], pb[b1][0:128, 0:512]), inc=SV, dbg=f"lnrf{tb}")
        brel(b0, SV, v_v_rep); brel(b1, SV, v_v_rep)
    v_vB = {}
    for cc in range(8):
        xc = XC[cc % 2]
        k_ld = LD.dma(xc[:, :], xT.ap()[cc * 128:(cc + 1) * 128, :],
                      waits=[(SV, v_vB.get(cc - 2, 0)),
                             (ST, v_tA[7] if cc < 2 else 0)], dbg=f"ld2_x{cc}")
        kb.op("vector", lambda e, xc=xc: e.tensor_sub(SQ[:, :], xc[:, :], m_rep[:, :]),
              waits=[k_ld], dbg=f"xna{cc}")
        v_vB[cc] = kb.op("vector", lambda e, cc=cc: e.tensor_mul(
            XN1[:, cc * TOK:(cc + 1) * TOK], SQ[:, :], r_rep[:, :]), inc=SV, dbg=f"xnb{cc}")
    v_xn = SV.v
    tap("xn", XN1[:, 0:8 * TOK], [128, 8 * TOK], [(SV, v_xn)], BF16)

    if nseg < 2:
        return _finish(kb, nc)

    AR.reset(MK_ROWS)
    kb.barrier("rst_rows")
    I32S = AR.alloc("I32S", [128, TOK], I32)
    v_vm = kb.op("vector", lambda e: e.tensor_reduce(
        t_locmax[:, 0:1], XN1[:, 0:8 * TOK], AX.X, ALU.max, apply_absolute_value=True),
        waits=[(SV, v_xn)], inc=SV, dbg="xnmax")
    v_gm = preduce(t_locmax[:, 0:1], t_arstage[0:1, 0:1], 6, [(SV, v_vm)], ALU.max, "xnmax_p")
    k_d1 = GD.dma(c_ar_i[0].ap(), t_arstage[:, :], waits=[(SV, v_gm)], dbg="ar1w")
    v_cc1 = kb.op("gpsimd", lambda e: e.collective_compute(
        "AllReduce", ALU.max, replica_groups=ALL8,
        ins=[c_ar_i[0].ap().opt()], outs=[c_ar_o[0].ap().opt()]),
        waits=[k_d1], inc=SC, dbg="AR1")
    k_d1o = GD.dma(t_ar[:, :], c_ar_o[0].ap(), waits=[(SC, v_cc1)], dbg="ar1r")
    v_g_s1 = bcast_rows(t_ar[0:1, 0:1], t_s[0][:, 0:1], 1, 6, [k_d1o], "s1bc")
    kb.op("vector", lambda e: e.tensor_scalar(
        t_s[0][:, 0:1], t_s[0][:, 0:1], 1.0 / QMAX, 1e-12, ALU.mult, ALU.max),
        waits=[(SV, v_g_s1)], dbg="s1a")
    v_v_s1 = kb.op("vector", lambda e: e.reciprocal(t_s[0][:, 1:2], t_s[0][:, 0:1]),
                   inc=SV, dbg="s1b")
    for cc in range(8):
        kb.op("vector", lambda e, cc=cc: e.tensor_scalar_mul(
            SQ[:, :], XN1[:, cc * TOK:(cc + 1) * TOK], t_s[0][:, 1:2]),
            waits=([(SV, v_v_s1)] if cc == 0 else []), dbg=f"xqa{cc}")
        kb.op("vector", lambda e: e.tensor_copy(I32S[:, :], SQ[:, :]), dbg=f"xqb{cc}")
        kb.op("vector", lambda e: e.tensor_copy(SQ[:, :], I32S[:, :]), dbg=f"xqc{cc}")
        kb.op("vector", lambda e, cc=cc: e.tensor_scalar_mul(
            XN2[:, cc * TOK:(cc + 1) * TOK], SQ[:, :], t_s[0][:, 0:1]), dbg=f"xqd{cc}")
    v_xq = SV.v
    tap("xq", XN2[:, 0:8 * TOK], [128, 8 * TOK], [(SV, v_xq)], BF16)
    tap("s1", t_s[0][:, 0:2], [128, 2], [(SV, v_xq)], F32)

    if nseg < 3:
        return _finish(kb, nc)

    AR.reset(MK_B)
    kb.barrier("rst_ln")
    QF = AR.alloc("QF", [128, 4 * TOK], BF16)
    KF = AR.alloc("KF", [128, 4 * TOK], BF16)
    KT = AR.alloc("KT", [128, NCH * CH0], BF16)
    VA = AR.alloc("VA", [128, NCH * H0 * 65], BF16)
    GT = AR.alloc("GT", [128, NCH * CH0], BF16)
    MK_WR = AR.mark()
    WRA = AR.alloc("WRA", [128, 8 * CH0], BF16)
    WRB = AR.alloc("WRB", [128, 8 * CH0], BF16)
    WRC = AR.alloc("WRC", [128, 8 * CH0], BF16)

    v_vone = kb.op("vector", lambda e: e.memset(
        ap3(VA, 0, 128, 64, [[H0 * 65, NCH], [65, H0]]), 1.0), inc=SV, dbg="vone")

    for cc in range(8):
        k_wq = LD.dma(WRA[:, cc * CH0:(cc + 1) * CH0],
                      d_wq.ap()[cc * 128:(cc + 1) * 128, :],
                      waits=([k_cast] if cc == 0 else []), dbg="ldwq", group=(cc > 0))
    for cc in range(8):
        k_wk = LD.dma(WRB[:, cc * CH0:(cc + 1) * CH0],
                      d_wk.ap()[cc * 128:(cc + 1) * 128, :], dbg="ldwk", group=(cc > 0))

    def phi_evac(bank, dst_ap, w_extra, dbg):
        if not KPHI:
            v3 = kb.op("vector", lambda e, b=bank, d=dst_ap: e.tensor_copy(
                d, pb[b][0:128, 0:512]), waits=w_extra, inc=SV, dbg=dbg + "cp")
            return v3
        v1 = kb.op("vector", lambda e, b=bank: e.tensor_scalar_min(
            PH1[:, 0:512], pb[b][0:128, 0:512], 0.0), waits=w_extra, inc=SV, dbg=dbg + "a")
        v2 = kb.op("scalar", lambda e: e.activation(PH2[:, 0:512], PH1[:, 0:512], AF.Exp),
                   waits=[(SV, v1)], inc=SS, dbg=dbg + "b")
        v3 = kb.op("vector", lambda e, b=bank, d=dst_ap: e.scalar_tensor_tensor(
            d, pb[b][0:128, 0:512], 0.0, PH2[:, 0:512], ALU.max, ALU.add),
            waits=[(SS, v2)], inc=SV, dbg=dbg + "c")
        return v3

    bq = 0
    for (wring, k_w, dst, nm) in ([(WRA, k_wq, QF, "q"), (WRB, k_wk, KF, "k")][:SUB3]):
        for cc in range(4):
            for tb in range(4):
                if cc * 4 + tb >= KGRP:
                    continue
                bank = 0 if KBANK0 else bq % 4
                bq += 1
                for dk in range(8):
                    kb.mm(pb[bank][0:128, 0:512],
                          wring[:, dk * CH0 + cc * 128:dk * CH0 + (cc + 1) * 128],
                          XN1[:, dk * TOK + tb * 512:dk * TOK + (tb + 1) * 512],
                          dk == 0, dk == 7,
                          waits=(([k_w, (SV, v_xn)] + bw(bank)) if dk == 0 else []),
                          inc=(ST if (KNONOP and dk == 7) else None),
                          dbg=f"{nm}{cc}{tb}")
                v_mmx = ST.v
                v3 = phi_evac(bank, dst[:, cc * TOK + tb * 512:cc * TOK + (tb + 1) * 512],
                              [(ST, v_mmx)], f"{nm}e{cc}{tb}")
                brel(bank, SV, v3)
    v_guard = SV.v
    for cc in range(8):
        k_wv = LD.dma(WRA[:, cc * CH0:(cc + 1) * CH0],
                      d_wv.ap()[cc * 128:(cc + 1) * 128, :],
                      waits=([(SV, v_guard)] if cc == 0 else []), dbg="ldwv", group=(cc > 0))
    for cc in range(8):
        k_wg = LD.dma(WRC[:, cc * CH0:(cc + 1) * CH0],
                      d_wg.ap()[cc * 128:(cc + 1) * 128, :], dbg="ldwg", group=(cc > 0))

    for tc in (range(NCH) if SUB3 >= 3 else []):
        bank = bq % 4; bq += 1
        for dk in range(8):
            kb.mm(pb[bank][0:128, 0:512],
                  XN1[:, dk * TOK + tc * 128:dk * TOK + (tc + 1) * 128],
                  WRB[:, dk * CH0:(dk + 1) * CH0],
                  dk == 0, dk == 7, waits=(bw(bank) if dk == 0 else []), dbg=f"kt{tc}")
        v_mmx = ST.v
        v3 = phi_evac(bank, KT[:, tc * CH0:(tc + 1) * CH0], [(ST, v_mmx)], f"kte{tc}")
        brel(bank, SV, v3)

        bank = bq % 4; bq += 1
        for dk in range(8):
            kb.mm(pb[bank][0:128, 0:512],
                  XN1[:, dk * TOK + tc * 128:dk * TOK + (tc + 1) * 128],
                  WRA[:, dk * CH0:(dk + 1) * CH0],
                  dk == 0, dk == 7,
                  waits=(([k_wv] + bw(bank)) if dk == 0 else []), dbg=f"v{tc}")
        v_mmx = ST.v
        v3 = kb.op("vector", lambda e, b=bank, tc=tc: e.tensor_copy(
            ap3(VA, 0, 128, tc * H0 * 65, [[65, H0], [1, 64]]), pb[b][0:128, 0:512]),
            waits=[(ST, v_mmx), (SV, v_vone)], inc=SV, dbg=f"ve{tc}")
        brel(bank, SV, v3)

        bank = bq % 4; bq += 1
        for dk in range(8):
            kb.mm(pb[bank][0:128, 0:512],
                  XN1[:, dk * TOK + tc * 128:dk * TOK + (tc + 1) * 128],
                  WRC[:, dk * CH0:(dk + 1) * CH0],
                  dk == 0, dk == 7,
                  waits=(([k_wg] + bw(bank)) if dk == 0 else []), dbg=f"g{tc}")
        v_mmx = ST.v
        v1 = kb.op("vector", lambda e, b=bank: e.tensor_add(
            PH1[:, 0:512], pb[b][0:128, 0:512], t_brep[:, :]),
            waits=[(ST, v_mmx), (SV, v_g_brep)], inc=SV, dbg=f"ge{tc}a")
        brel(bank, SV, v1)
        v2 = kb.op("scalar", lambda e, tc=tc: e.activation(
            GT[:, tc * CH0:(tc + 1) * CH0], PH1[:, 0:512], AF.Sigmoid),
            waits=[(SV, v1)], inc=SS, dbg=f"ge{tc}b")
        kb.wait_only("vector", [(SS, v2)], dbg=f"ge{tc}c")
    v_proj = SV.v
    v_proj_s = SS.v
    tap("qf", QF[:, 0:4 * TOK], [128, 4 * TOK], [(SV, v_proj)], BF16)
    tap("kf", KF[:, 0:4 * TOK], [128, 4 * TOK], [(SV, v_proj)], BF16)
    tap("kt", KT[:, 0:NCH * CH0], [128, NCH * CH0], [(SV, v_proj)], BF16)
    tap("va", VA[:, 0:NCH * H0 * 65], [128, NCH * H0 * 65], [(SV, v_proj)], BF16)
    tap("gt", GT[:, 0:NCH * CH0], [128, NCH * CH0], [(SS, v_proj_s)], BF16)

    if nseg < 4:
        return _finish(kb, nc)

    AR.reset(MK_WR)
    kb.barrier("rst_wr")
    MK_SCAN = AR.mark()
    QG = AR.alloc("QG", [128, 512], BF16)
    KG = AR.alloc("KG", [128, 512], BF16)
    QZE = AR.alloc("QZE", [128, 512], BF16)
    QZO = AR.alloc("QZO", [128, 512], BF16)
    QGE = AR.alloc("QGE", [128, 512], BF16)
    QGO = AR.alloc("QGO", [128, 512], BF16)
    MATB = AR.alloc("MATB", [128, H0 * 128], BF16)
    SSUM = AR.alloc("SSUM", [128, 4 * 65], F32)
    SEA = AR.alloc("SEA", [128, 4 * 65], F32)
    SEB = AR.alloc("SEB", [128, 4 * 65], F32)
    SRV = AR.alloc("SRV", [128, 4 * 65], F32)
    DENB = AR.alloc("DENB", [128, 8], F32)
    DENR = AR.alloc("DENR", [128, 8], F32)
    YG = AR.alloc("YG", [128, CH0], BF16)
    TMPY = AR.alloc("TMPY", [128, CH0], F32)
    SE16 = AR.alloc("SE16", [128, 4 * 65], BF16)

    last_state_mm = [0]

    def state_mms(tc, w_extra, dbg):
        v_kg = kb.op("vector", lambda e, tc=tc: e.tensor_mul(
            KG[:, 0:512], KT[:, tc * CH0:(tc + 1) * CH0], gammaK[:, :]),
            waits=list(w_extra) + [(SS, v_s_gK), (ST, last_state_mm[0]), (SV, v_proj)],
            inc=SV, dbg=dbg + "kg")
        for h in range(H0):
            pp = (h % 2) * 64
            bank = 4 + h % 2
            cg = (h // 2) * 65
            kb.mm(pb[bank][pp:pp + 64, cg:cg + 65],
                  KG[:, h * 64:(h + 1) * 64],
                  VA[:, tc * H0 * 65 + h * 65:tc * H0 * 65 + (h + 1) * 65],
                  h < 2, h >= 6,
                  waits=(([(SV, v_kg)] + bw(4, 5)) if h == 0 else []),
                  dbg=dbg + f"s{h}")
        v_mm = ST.v
        last_state_mm[0] = v_mm
        kb.op("vector", lambda e: e.tensor_copy(SSUM[0:64, 0:260], pb[4][0:64, 0:260]),
              waits=[(ST, v_mm)], dbg=dbg + "e0")
        v_ev = kb.op("vector", lambda e: e.tensor_copy(SSUM[64:128, 0:260], pb[5][64:128, 0:260]),
                     inc=SV, dbg=dbg + "e1")
        brel(4, SV, v_ev); brel(5, SV, v_ev)
        return v_ev

    v_z = kb.op("vector", lambda e: e.memset(SEA[:, 0:260], 0.0), inc=SV, dbg="seaz")
    cur, nxt = SEA, SEB
    v_ch = v_z
    for tc in range(NCH):
        v_ev = state_mms(tc, [], f"c1_{tc}")
        kb.op("vector", lambda e, c=cur, n=nxt: e.tensor_mul(
            n[:, 0:260], c[:, 0:260], gammaC[:, 0:260]),
            waits=[(SV, max(v_ev, v_ch)), (SV, v_v_gC)], dbg=f"ch{tc}a")
        v_ch = kb.op("vector", lambda e, n=nxt: e.tensor_add(
            n[:, 0:260], n[:, 0:260], SSUM[:, 0:260]), inc=SV, dbg=f"ch{tc}b")
        cur, nxt = nxt, cur
    v_ctr = kb.op("vector", lambda e, c=cur: e.tensor_scalar_mul(
        SRV[:, 0:260], c[:, 0:260], fl_contrib[:, 0:1]),
        waits=[(SV, max(v_ch, v_g_fl))], inc=SV, dbg="ctr")
    k_dst = GD.dma(c_st_i.ap(), SRV[0:128, 0:260], waits=[(SV, v_ctr)], dbg="stw")
    v_ccs = kb.op("gpsimd", lambda e: e.collective_compute(
        "AllReduce", ALU.add, replica_groups=PAIRS,
        ins=[c_st_i.ap().opt()], outs=[c_st_o.ap().opt()]),
        waits=[k_dst], inc=SC, dbg="ARst")
    k_dsr = GD.dma(SRV[0:128, 0:260], c_st_o.ap(), waits=[(SC, v_ccs)], dbg="str")
    v_se0 = kb.op("vector", lambda e: e.tensor_scalar_mul(
        SEA[:, 0:260], SRV[:, 0:260], fl_use[:, 0:1]),
        waits=[k_dsr, (SV, v_g_fl)], inc=SV, dbg="se0")

    cur, nxt = SEA, SEB
    v_chain = v_se0
    v_num_prev = 0
    k_tp = (None, 0)
    for tc in range(NCH):
        kb.op("vector", lambda e, tc=tc: e.tensor_scalar_mul(
            QZE[:, 0:512], ap3(QF, 0, 128, tc * 128, [[TOK, 4], [1, 128]]), PONE2[:, 0:1]),
            waits=[(SV, v_proj)], dbg=f"qze{tc}")
        v_qz = kb.op("vector", lambda e, tc=tc: e.tensor_scalar_mul(
            QZO[:, 0:512], ap3(QF, 0, 128, tc * 128, [[TOK, 4], [1, 128]]), PONE2[:, 1:2]),
            inc=SV, dbg=f"qzo{tc}")
        for h in range(H0):
            bank = h // 4
            cc = h // 2
            QZ = QZE if h % 2 == 0 else QZO
            kb.mm(pb[bank][0:128, (h % 4) * 128:(h % 4) * 128 + 128],
                  KF[0:128, cc * TOK + tc * 128:cc * TOK + (tc + 1) * 128],
                  QZ[0:128, cc * 128:(cc + 1) * 128],
                  True, True,
                  waits=(([(SV, v_qz)] + bw(bank)) if h % 4 == 0 else []),
                  dbg=f"at{tc}_{h}")
        v_at = ST.v
        kb.op("vector", lambda e: e.tensor_mul(
            MATB[:, 0:512], pb[0][0:128, 0:512], mask_all[:, 0:512]),
            waits=[(ST, v_at), (SV, v_masks)], dbg=f"ms{tc}0")
        v_msk = kb.op("vector", lambda e: e.tensor_mul(
            MATB[:, 512:1024], pb[1][0:128, 0:512], mask_all[:, 512:1024]),
            inc=SV, dbg=f"ms{tc}1")
        brel(0, SV, v_msk); brel(1, SV, v_msk)
        if KSC4 < 3:
            continue
        kb.op("vector", lambda e, tc=tc: e.tensor_mul(
            QG[:, 0:512], ap3(QF, 0, 128, tc * 128, [[TOK, 4], [1, 128]]), gammaQ[:, :]),
            waits=[(SS, v_s_gQ), (ST, v_num_prev)], dbg=f"qg{tc}")
        kb.op("vector", lambda e: e.tensor_scalar_mul(
            QGE[:, 0:512], QG[:, 0:512], PONE2[:, 0:1]), dbg=f"qge{tc}")
        v_qg = kb.op("vector", lambda e: e.tensor_scalar_mul(
            QGO[:, 0:512], QG[:, 0:512], PONE2[:, 1:2]), inc=SV, dbg=f"qgo{tc}")
        v_se16 = kb.op("vector", lambda e, c=cur: e.tensor_copy(SE16[:, 0:260], c[:, 0:260]),
                       waits=[(SV, v_chain), (ST, v_num_prev)], inc=SV, dbg=f"se16_{tc}")
        for h in range(H0):
            bank = 2 + h // 4
            kb.mm(pb[bank][0:128, (h % 4) * 65:(h % 4) * 65 + 65],
                  MATB[:, h * 128:(h + 1) * 128],
                  VA[:, tc * H0 * 65 + h * 65:tc * H0 * 65 + (h + 1) * 65],
                  h % 4 == 0, False,
                  waits=(([(SV, v_msk)] + bw(bank)) if h % 4 == 0 else []), dbg=f"in{tc}_{h}")
        for h in range(H0):
            bank = 2 + h // 4
            QGZ = QGE if h % 2 == 0 else QGO
            kb.mm(pb[bank][0:128, (h % 4) * 65:(h % 4) * 65 + 65],
                  QGZ[0:128, (h // 2) * 128:(h // 2 + 1) * 128],
                  SE16[0:128, (h // 2) * 65:(h // 2) * 65 + 65],
                  False, h % 4 == 3,
                  waits=([(SV, max(v_qg, v_se16))] if h == 0 else []), dbg=f"ir{tc}_{h}")
        v_num = ST.v
        v_num_prev = v_num
        if KSC4 < 4:
            brel(2, ST, v_num); brel(3, ST, v_num)
            continue
        for bi in range(2):
            kb.op("vector", lambda e, b=2 + bi, bi=bi: e.tensor_scalar_add(
                DENB[:, bi * 4:bi * 4 + 4],
                bass.AP(pb[b], 64, [[512, 128], [65, 4]]), EPS_ATT),
                waits=[(ST, v_num)], dbg=f"den{tc}{bi}")
        v_den = kb.op("vector", lambda e: e.reciprocal(DENR[:, 0:8], DENB[:, 0:8]),
                      inc=SV, dbg=f"denr{tc}")
        for bi in range(2):
            kb.op("vector", lambda e, b=2 + bi, bi=bi: e.tensor_mul(
                TMPY[:, bi * 256:bi * 256 + 256],
                bass.AP(pb[b], 0, [[512, 128], [65, 4], [1, 64]]),
                bass.AP(DENR, bi * 4, [[8, 128], [1, 4], [0, 64]])),
                waits=[(SV, v_den)], dbg=f"y{tc}{bi}")
        v_y = SV.v
        brel(2, SV, v_y); brel(3, SV, v_y)
        v_yg = kb.op("vector", lambda e, tc=tc: e.tensor_mul(
            YG[:, 0:512], TMPY[:, 0:512], GT[:, tc * CH0:(tc + 1) * CH0]),
            waits=[(SV, v_y), (SS, v_proj_s), k_tp], inc=SV, dbg=f"yg{tc}")
        if KSC4 < 5:
            continue
        for cc in range(4):
            k_tp = TP.dma(YGT[:, cc * TOK + tc * 128:cc * TOK + (tc + 1) * 128],
                          YG[:, cc * 128:(cc + 1) * 128],
                          waits=([(SV, v_yg)] if cc == 0 else []), transpose=True,
                          dbg=f"tp{tc}{cc}", group=(cc > 0))
        v_ev = state_mms(tc, [], f"c2_{tc}")
        kb.op("vector", lambda e, c=cur, n=nxt: e.tensor_mul(
            n[:, 0:260], c[:, 0:260], gammaC[:, 0:260]), waits=[(SV, v_ev)], dbg=f"cb{tc}a")
        v_chain = kb.op("vector", lambda e, n=nxt: e.tensor_add(
            n[:, 0:260], n[:, 0:260], SSUM[:, 0:260]), inc=SV, dbg=f"cb{tc}b")
        cur, nxt = nxt, cur
    v_scan = SV.v
    tap("ygt", YGT[:, 0:4 * TOK], [128, 4 * TOK], [k_tp, (SV, v_scan)], BF16)

    if nseg < 5:
        return _finish(kb, nc)

    AR.reset(MK_LN)
    kb.barrier("rst_s5")
    MK_S5 = AR.mark()
    GFF = AR.alloc("GFF", [128, 8 * TOK], BF16)
    MK_S5B = AR.mark()
    WQ0 = AR.alloc("WQ0", [128, FF], F32)
    WQ1 = AR.alloc("WQ1", [128, FF], F32)
    SWT = AR.alloc("SWT", [128, 96], F32)
    SWR = AR.alloc("SWR", [128, 96], F32)
    WQB = AR.alloc("WQB", [128, FF], BF16)
    RND = AR.alloc("RND", [128, FF], F32)
    SQA = AR.alloc("SQA", [128, FF], F32)
    SQB = AR.alloc("SQB", [128, FF], F32)
    I32B = AR.alloc("I32B", [128, FF], I32)
    UBB = AR.alloc("UBB", [128, TOK], BF16)
    TBB = AR.alloc("TBB", [128, TOK], BF16)
    TB5 = AR.alloc("TB5", [128, 512], BF16)
    OB = AR.alloc("OB", [128, 512], BF16)
    WUT = AR.alloc("WUT", [128, 8 * 128], BF16)
    WUT2 = AR.alloc("WUT2", [128, 8 * 128], BF16)
    WTT = AR.alloc("WTT", [128, FF], BF16)
    t_mx = AR.alloc("t_mx", [128, 4], F32)
    t_mxs = AR.alloc("t_mxs", [128, 1], F32)
    t_c1 = AR.alloc("t_c1", [128, 1], F32)

    kb.op("vector", lambda e: e.memset(t_mx[:, :], 0.0), dbg="mx0")

    def lpbq(w_in, d_out, nch, dbg, prev_tok=None):
        wid = w_in.shape[1]
        nblk = wid // 32
        toks = []
        v_done = {}
        for cch in range(nch):
            wq_t = WQ0 if cch % 2 == 0 else WQ1
            k_l = LD.dma(wq_t[:, 0:wid], w_in.ap()[cch * 128:(cch + 1) * 128, :],
                         waits=[(SV, v_done.get(cch - 2, SV.v if cch < 2 else 0))],
                         dbg=dbg + f"l{cch}")
            kb.op("vector", lambda e, w=wq_t, nb=nblk, wid=wid: e.tensor_reduce(
                SWT[:, 0:nb], ap3(w, 0, 128, 0, [[32, nb], [1, 32]]), AX.X, ALU.max,
                apply_absolute_value=True), waits=[k_l], dbg=dbg + f"a{cch}")
            kb.op("vector", lambda e, nb=nblk: e.tensor_scalar(
                SWT[:, 0:nb], SWT[:, 0:nb], 1.0 / 7.0, 1e-12, ALU.mult, ALU.max),
                dbg=dbg + f"s{cch}")
            kb.op("vector", lambda e, nb=nblk: e.reciprocal(SWR[:, 0:nb], SWT[:, 0:nb]),
                  dbg=dbg + f"r{cch}")
            kb.op("vector", lambda e, w=wq_t, nb=nblk, wid=wid: e.tensor_mul(
                SQA[:, 0:wid], w[:, 0:wid], ap3(SWR, 0, 128, 0, [[1, nb], [0, 32]])),
                dbg=dbg + f"d{cch}")
            kb.op("vector", lambda e, wid=wid: e.tensor_copy(I32B[:, 0:wid], SQA[:, 0:wid]),
                  dbg=dbg + f"i{cch}")
            kb.op("vector", lambda e, wid=wid: e.tensor_copy(SQB[:, 0:wid], I32B[:, 0:wid]),
                  dbg=dbg + f"j{cch}")
            kb.op("vector", lambda e, wid=wid: e.tensor_sub(
                RND[:, 0:wid], SQA[:, 0:wid], SQB[:, 0:wid]), dbg=dbg + f"j2{cch}")
            kb.op("vector", lambda e, wid=wid: e.scalar_tensor_tensor(
                SQA[:, 0:wid], RND[:, 0:wid], 0.5, SQB[:, 0:wid], ALU.is_gt, ALU.add),
                dbg=dbg + f"j3{cch}")
            kb.op("vector", lambda e, wid=wid: e.tensor_scalar(
                RND[:, 0:wid], RND[:, 0:wid], -0.5, -1.0, ALU.is_lt, ALU.mult),
                dbg=dbg + f"j4{cch}")
            kb.op("vector", lambda e, wid=wid: e.tensor_add(
                SQA[:, 0:wid], SQA[:, 0:wid], RND[:, 0:wid]), dbg=dbg + f"j5{cch}")
            v_q = kb.op("vector", lambda e, nb=nblk, wid=wid: e.tensor_mul(
                WQB[:, 0:wid], SQA[:, 0:wid], ap3(SWT, 0, 128, 0, [[1, nb], [0, 32]])),
                waits=([toks[-1]] if toks else ([prev_tok] if prev_tok else [])),
                dbg=dbg + f"q{cch}")
            v_done[cch] = v_q
            toks.append(GD.dma(d_out.ap()[cch * 128:(cch + 1) * 128, :], WQB[:, 0:wid],
                               waits=[(SV, v_q)], dbg=dbg + f"w{cch}"))
        return toks

    tok_up = lpbq(wup, d_wupq, 24, "lu")
    tok_gt = lpbq(wgate, d_wgtq, 24, "lg", prev_tok=tok_up[-1])
    tok_dn = lpbq(wdown, d_wdnq, 8, "ld", prev_tok=tok_gt[-1])

    def pretrans(src_d, dst_d, R, C, wtok, dbg, prev=None):
        ks = prev
        for j in range(C // 128):
            k_t = TP.dma(WTT[:, 0:R], src_d.ap()[:, j * 128:(j + 1) * 128],
                         waits=[wtok] + ([ks] if ks else []), transpose=True,
                         dbg=dbg + f"t{j}")
            ks = LD.dma(dst_d.ap()[j * 128:(j + 1) * 128, :], WTT[:, 0:R],
                        waits=[k_t], dbg=dbg + f"s{j}")
        return ks

    tok_upT = pretrans(d_wupq, d_wupqT, FF, D, tok_up[-1], "ptu")
    tok_gtT = pretrans(d_wgtq, d_wgtqT, FF, D, tok_gt[-1], "ptg", prev=tok_upT)
    tok_dnT = pretrans(d_wdnq, d_wdnqT, D, FF, tok_dn[-1], "ptd", prev=tok_gtT)

    def ffn_proj(dT, spill, mx_col, wtok, dbg, prev_sp=None):
        sp_toks = []
        v_last_mx = SV.v
        for cch in range(24):
            wr = WUT if cch % 2 == 0 else WUT2
            kws = None
            for dk in range(8):
                kws = LD.dma(wr[:, dk * 128:(dk + 1) * 128],
                             dT.ap()[dk * 128:dk * 128 + 128, cch * 128:(cch + 1) * 128],
                             waits=([wtok, (ST, ST.v)] if dk == 0 else []),
                             dbg=dbg + f"w{cch}", group=(dk > 0))
            for tb in range(4):
                bank = (cch * 4 + tb) % 4
                for dk in range(8):
                    kb.mm(pb[bank][0:128, 0:512],
                          wr[:, dk * 128:(dk + 1) * 128],
                          XN2[:, dk * TOK + tb * 512:dk * TOK + (tb + 1) * 512],
                          dk == 0, dk == 7,
                          waits=(([kws, (SV, v_xq)] + bw(bank)) if dk == 0 else []),
                          dbg=dbg + f"m{cch}{tb}")
                v_mmu = ST.v
                v_ev = kb.op("vector", lambda e, b=bank, tb=tb: e.tensor_copy(
                    UBB[:, tb * 512:(tb + 1) * 512], pb[b][0:128, 0:512]),
                    waits=[(ST, v_mmu)] +
                          ([sp_toks[cch - 1], (SV, v_last_mx)] if (tb == 0 and cch > 0) else []) +
                          ([prev_sp, (SV, v_last_mx)] if (tb == 0 and cch == 0 and prev_sp) else []),
                    dbg=dbg + f"e{cch}{tb}")
                brel(bank, SV, v_ev)
            sp_toks.append(GD.dma(spill.ap()[cch * 128:(cch + 1) * 128, :], UBB[:, :],
                                  waits=[(SV, v_ev)], dbg=dbg + f"sp{cch}"))
            kb.op("vector", lambda e: e.tensor_reduce(
                t_mxs[:, 0:1], UBB[:, :], AX.X, ALU.max, apply_absolute_value=True),
                dbg=dbg + f"mr{cch}")
            v_last_mx = kb.op("vector", lambda e, c=mx_col: e.tensor_max(
                t_mx[:, c:c + 1], t_mx[:, c:c + 1], t_mxs[:, 0:1]), dbg=dbg + f"mx{cch}")
        return sp_toks

    tok_u = ffn_proj(d_wupqT, d_u, 0, tok_upT, "U")
    tok_g = ffn_proj(d_wgtqT, d_g, 1, tok_gtT, "G", prev_sp=tok_u[-1])
    v_maxug = SV.v

    v_p1 = preduce(t_mx[:, 0:1], t_arstage[0:1, 0:1], 6, [(SV, v_maxug)], ALU.max, "p_u")
    v_p2 = preduce(t_mx[:, 1:2], t_arstage[0:1, 1:2], 6, [(SV, v_p1)], ALU.max, "p_g")
    k_a2 = GD.dma(c_ar_i[1].ap(), t_arstage[:, :], waits=[(SV, v_p2)], dbg="ar2w")
    v_cc2 = kb.op("gpsimd", lambda e: e.collective_compute(
        "AllReduce", ALU.max, replica_groups=ALL8,
        ins=[c_ar_i[1].ap().opt()], outs=[c_ar_o[1].ap().opt()]),
        waits=[k_a2], inc=SC, dbg="AR2")
    k_a2r = GD.dma(t_ar[:, :], c_ar_o[1].ap(), waits=[(SC, v_cc2)], dbg="ar2r")

    def mk_scale(col, tile, wtok, dbg):
        v_b = bcast_rows(t_ar[0:1, col:col + 1], tile[:, 0:1], 1, 6, [wtok], dbg + "b")
        kb.op("vector", lambda e, t=tile: e.tensor_scalar(
            t[:, 0:1], t[:, 0:1], 1.0 / QMAX, 1e-12, ALU.mult, ALU.max),
            waits=[(SV, v_b)], dbg=dbg + "s")
        kb.op("vector", lambda e, t=tile: e.reciprocal(t[:, 1:2], t[:, 0:1]), dbg=dbg + "r")
        kb.op("vector", lambda e, t=tile: e.tensor_copy(t[:, 2:3], t[:, 0:1]), dbg=dbg + "c")
        return kb.op("vector", lambda e, t=tile: e.memset(t[:, 3:4], 0.0), dbg=dbg + "z")

    v_s2 = mk_scale(0, t_s[1], k_a2r, "s2")
    v_s3 = mk_scale(1, t_s[2], k_a2r, "s3")

    v_c34 = kb.op("vector", lambda e: e.tensor_scalar_mul(
        t_c1[:, 0:1], t_s[2][:, 0:1], 1.0 / 65536.0), waits=[(SV, v_s3)], dbg="c34")
    v_t4last = 0
    t4_toks = []
    for cch in range(24):
        k_g = LD.dma(UBB[:, :], d_g.ap()[cch * 128:(cch + 1) * 128, :],
                     waits=[tok_g[cch], (SV, v_t4last)] +
                           ([tok_g[-1], (SV, SV.v)] if cch == 0 else []), dbg=f"Cg{cch}")
        kb.op("vector", lambda e: e.tensor_scalar_mul(SQA[:, 0:TOK], UBB[:, :], t_s[2][:, 1:2]),
              waits=[k_g, (SV, v_s3)], dbg=f"Ca{cch}")
        kb.op("vector", lambda e: e.tensor_copy(I32B[:, 0:TOK], SQA[:, 0:TOK]), dbg=f"Cb{cch}")
        v_r3 = kb.op("vector", lambda e: e.tensor_copy(SQA[:, 0:TOK], I32B[:, 0:TOK]), dbg=f"Cc{cch}")
        v_sg = kb.op("scalar", lambda e: e.activation(
            SQB[:, 0:TOK], SQA[:, 0:TOK], AF.Sigmoid, scale=t_s[2][:, 2:3], bias=t_s[2][:, 3:4]),
            waits=[(SV, v_r3)], dbg=f"Cs{cch}")
        kb.op("vector", lambda e: e.tensor_scalar_mul(SQB[:, 0:TOK], SQB[:, 0:TOK], 65536.0),
              waits=[(SS, v_sg)], dbg=f"Cd{cch}")
        kb.op("vector", lambda e: e.tensor_copy(I32B[:, 0:TOK], SQB[:, 0:TOK]), dbg=f"Ce{cch}")
        kb.op("vector", lambda e: e.tensor_copy(SQB[:, 0:TOK], I32B[:, 0:TOK]), dbg=f"Cf{cch}")
        kb.op("vector", lambda e: e.tensor_scalar_min(SQB[:, 0:TOK], SQB[:, 0:TOK], 65535.0),
              dbg=f"Cg2{cch}")
        v_t4 = kb.op("vector", lambda e: e.scalar_tensor_tensor(
            TBB[:, :], SQA[:, 0:TOK], t_c1[:, 0:1], SQB[:, 0:TOK], ALU.mult, ALU.mult),
            waits=[(SV, v_c34)] + ([t4_toks[cch - 1]] if cch > 0 else []), dbg=f"Ct{cch}")
        t4_toks.append(GD.dma(d_t4.ap()[cch * 128:(cch + 1) * 128, :], TBB[:, :],
                              waits=[(SV, v_t4)], dbg=f"Cw{cch}"))
        kb.op("vector", lambda e: e.tensor_reduce(
            t_mxs[:, 0:1], TBB[:, :], AX.X, ALU.max, apply_absolute_value=True), dbg=f"Cm{cch}")
        v_t4last = kb.op("vector", lambda e: e.tensor_max(
            t_mx[:, 2:3], t_mx[:, 2:3], t_mxs[:, 0:1]), dbg=f"Cx{cch}")

    v_p3 = preduce(t_mx[:, 2:3], t_arstage[0:1, 0:1], 6, [(SV, v_t4last)], ALU.max, "p_t4")
    k_a3 = GD.dma(c_ar_i[2].ap(), t_arstage[:, :], waits=[(SV, v_p3)], dbg="ar3w")
    v_cc3 = kb.op("gpsimd", lambda e: e.collective_compute(
        "AllReduce", ALU.max, replica_groups=ALL8,
        ins=[c_ar_i[2].ap().opt()], outs=[c_ar_o[2].ap().opt()]),
        waits=[k_a3], inc=SC, dbg="AR3")
    k_a3r = GD.dma(t_ar[:, :], c_ar_o[2].ap(), waits=[(SC, v_cc3)], dbg="ar3r")
    v_s4 = mk_scale(0, t_s[3], k_a3r, "s4")

    v_c24 = kb.op("vector", lambda e: e.tensor_mul(
        t_c1[:, 0:1], t_s[1][:, 0:1], t_s[3][:, 0:1]),
        waits=[(SV, max(v_s2, v_s4))], dbg="c24")
    v_t5last = 0
    t5_toks = []
    for cch in range(24):
        k_u = LD.dma(UBB[:, :], d_u.ap()[cch * 128:(cch + 1) * 128, :],
                     waits=[tok_u[cch], (SV, v_t5last)] +
                           ([(SV, SV.v)] if cch == 0 else []), dbg=f"Du{cch}")
        k_t = LD.dma(TBB[:, :], d_t4.ap()[cch * 128:(cch + 1) * 128, :],
                     waits=[t4_toks[cch], (SV, v_t5last)] +
                           ([t5_toks[cch - 1]] if cch > 0 else [(SV, SV.v), t4_toks[-1]]),
                     dbg=f"Dt{cch}")
        kb.op("vector", lambda e: e.tensor_scalar_mul(SQA[:, 0:TOK], UBB[:, :], t_s[1][:, 1:2]),
              waits=[k_u, (SV, v_c24)], dbg=f"Da{cch}")
        kb.op("vector", lambda e: e.tensor_copy(I32B[:, 0:TOK], SQA[:, 0:TOK]), dbg=f"Db{cch}")
        kb.op("vector", lambda e: e.tensor_copy(SQA[:, 0:TOK], I32B[:, 0:TOK]), dbg=f"Dc{cch}")
        kb.op("vector", lambda e: e.tensor_scalar_mul(SQB[:, 0:TOK], TBB[:, :], t_s[3][:, 1:2]),
              waits=[k_t], dbg=f"Dd{cch}")
        kb.op("vector", lambda e: e.tensor_copy(I32B[:, 0:TOK], SQB[:, 0:TOK]), dbg=f"De{cch}")
        kb.op("vector", lambda e: e.tensor_copy(SQB[:, 0:TOK], I32B[:, 0:TOK]), dbg=f"Df{cch}")
        v_t5 = kb.op("vector", lambda e: e.scalar_tensor_tensor(
            TBB[:, :], SQA[:, 0:TOK], t_c1[:, 0:1], SQB[:, 0:TOK], ALU.mult, ALU.mult), dbg=f"Dg{cch}")
        t5_toks.append(GD.dma(d_t5.ap()[cch * 128:(cch + 1) * 128, :], TBB[:, :],
                              waits=[(SV, v_t5)], dbg=f"Dw{cch}"))
        kb.op("vector", lambda e: e.tensor_reduce(
            t_mxs[:, 0:1], TBB[:, :], AX.X, ALU.max, apply_absolute_value=True), dbg=f"Dm{cch}")
        v_t5last = kb.op("vector", lambda e: e.tensor_max(
            t_mx[:, 3:4], t_mx[:, 3:4], t_mxs[:, 0:1]), dbg=f"Dx{cch}")

    v_p4 = preduce(t_mx[:, 3:4], t_arstage[0:1, 0:1], 6, [(SV, v_t5last)], ALU.max, "p_t5")
    k_a4 = GD.dma(c_ar_i[3].ap(), t_arstage[:, :], waits=[(SV, v_p4)], dbg="ar4w")
    v_cc4 = kb.op("gpsimd", lambda e: e.collective_compute(
        "AllReduce", ALU.max, replica_groups=ALL8,
        ins=[c_ar_i[3].ap().opt()], outs=[c_ar_o[3].ap().opt()]),
        waits=[k_a4], inc=SC, dbg="AR4")
    k_a4r = GD.dma(t_ar[:, :], c_ar_o[3].ap(), waits=[(SC, v_cc4)], dbg="ar4r")
    v_s5 = mk_scale(0, t_s[4], k_a4r, "s5")

    for tb in range(4):
        v_sil_prev = SS.v
        for cch in range(24):
            wr = WUT if cch % 2 == 0 else WUT2
            k_w = LD.dma(wr[:, :], d_wdnqT.ap()[cch * 128:(cch + 1) * 128, :],
                         waits=[tok_dnT, (ST, ST.v)], dbg=f"Ew{tb}_{cch}")
            k_5 = LD.dma(TB5[:, :],
                         d_t5.ap()[cch * 128:(cch + 1) * 128, tb * 512:(tb + 1) * 512],
                         waits=[t5_toks[cch], (SV, SV.v)], dbg=f"E5{tb}_{cch}")
            kb.op("vector", lambda e: e.tensor_scalar_mul(
                SQA[:, 0:512], TB5[:, :], t_s[4][:, 1:2]),
                waits=[k_5, (SV, v_s5), (ST, ST.v)], dbg=f"Ea{tb}_{cch}")
            kb.op("vector", lambda e: e.tensor_copy(I32B[:, 0:512], SQA[:, 0:512]),
                  dbg=f"Eb{tb}_{cch}")
            kb.op("vector", lambda e: e.tensor_copy(SQA[:, 0:512], I32B[:, 0:512]),
                  dbg=f"Ec{tb}_{cch}")
            v_o = kb.op("vector", lambda e: e.tensor_scalar_mul(
                OB[:, :], SQA[:, 0:512], t_s[4][:, 0:1]), dbg=f"Ed{tb}_{cch}")
            for dk in range(8):
                w_ = []
                if dk == 0:
                    w_ = [k_w, (SV, v_o), (SS, v_sil_prev)]
                if cch == 0:
                    w_ = w_ + bw(dk)
                kb.mm(pb[dk][0:128, 0:512],
                      wr[:, dk * 128:(dk + 1) * 128], OB[:, :],
                      cch == 0, cch == 23, waits=w_, dbg=f"Em{tb}_{cch}_{dk}")
            v_mme = ST.v
        for dk in range(8):
            v_sil = kb.op("scalar", lambda e, dk=dk, tb=tb: e.activation(
                GFF[:, dk * TOK + tb * 512:dk * TOK + (tb + 1) * 512],
                pb[dk][0:128, 0:512], AF.Silu),
                waits=[(ST, v_mme)], dbg=f"Es{tb}_{dk}")
            brel(dk, SS, v_sil)
    v_gff = SS.v
    tap("gff", GFF[:, 0:8 * TOK], [128, 8 * TOK], [(SS, v_gff)], BF16)
    if "uu" in taps:
        dt_u = nc.dram_tensor("tap_uu", [FF, TOK], BF16, kind="ExternalOutput")
        LD.dma(dt_u.ap(), d_u.ap(), waits=list(tok_u), dbg="tap_uu")
        kb.tapinfo["uu"] = ([FF, TOK], BF16)
    if "t4t" in taps:
        dt_t4 = nc.dram_tensor("tap_t4t", [FF, TOK], BF16, kind="ExternalOutput")
        LD.dma(dt_t4.ap(), d_t4.ap(), waits=list(t4_toks), dbg="tap_t4t")
        kb.tapinfo["t4t"] = ([FF, TOK], BF16)
    if "wupq" in taps:
        dt_wq = nc.dram_tensor("tap_wupq", [FF, D], BF16, kind="ExternalOutput")
        LD.dma(dt_wq.ap(), d_wupq.ap(), waits=[tok_up[-1]], dbg="tap_wupq")
        kb.tapinfo["wupq"] = ([FF, D], BF16)
    if "sc" in taps:
        dt_sc = nc.dram_tensor("tap_sc", [128, 20], F32, kind="ExternalOutput")
        v_scw = SV.v
        for i in range(5):
            LD.dma(dt_sc.ap()[:, 4 * i:4 * (i + 1)], t_s[i][:, 0:4],
                   waits=[(SV, v_scw)], dbg=f"tap_sc{i}", group=(i > 0))
        kb.tapinfo["sc"] = ([128, 20], F32)

    if nseg < 7:
        return _finish(kb, nc)

    AR.reset(MK_S5B)
    kb.barrier("rst_s7")
    WOT = AR.alloc("WOT", [128, 4 * 1024], BF16)
    XR0 = AR.alloc("XR0", [128, TOK], F32)
    XR1 = AR.alloc("XR1", [128, TOK], F32)
    X1B = AR.alloc("X1B", [128, 512], F32)
    k_wo = None
    for cc in range(4):
        k_wo = LD.dma(WOT[:, cc * 1024:(cc + 1) * 1024],
                      d_wo.ap()[cc * 128:(cc + 1) * 128, :],
                      waits=([k_cast] if cc == 0 else []), dbg="ldwo", group=(cc > 0))
    v_x1_done = {}
    x1_toks = []
    for dk in range(8):
        xr = XR0 if dk % 2 == 0 else XR1
        k_x = LD.dma(xr[:, :], xT.ap()[dk * 128:(dk + 1) * 128, :],
                     waits=[(SV, v_x1_done.get(dk - 2, 0))], dbg=f"x2{dk}")
        for tb in range(4):
            bank = (dk * 4 + tb) % 4
            for cc in range(4):
                kb.mm(pb[bank][0:128, 0:512],
                      WOT[:, cc * 1024 + dk * 128:cc * 1024 + (dk + 1) * 128],
                      YGT[:, cc * TOK + tb * 512:cc * TOK + (tb + 1) * 512],
                      cc == 0, cc == 3,
                      waits=(([k_wo, k_tp, (SV, v_scan)] + bw(bank)) if cc == 0 else []),
                      dbg=f"wo{dk}{tb}")
            v_mmo = ST.v
            kb.op("vector", lambda e, b=bank, dk=dk, tb=tb: e.scalar_tensor_tensor(
                X1B[:, 0:512], pb[b][0:128, 0:512], 1.0,
                GFF[:, dk * TOK + tb * 512:dk * TOK + (tb + 1) * 512],
                ALU.mult, ALU.mult),
                waits=[(ST, v_mmo), (SS, v_gff), k_x] +
                      ([x1_toks[-1]] if x1_toks else []), dbg=f"cmb{dk}{tb}a")
            v_x1 = kb.op("vector", lambda e, xr=xr, tb=tb: e.tensor_add(
                X1B[:, 0:512], X1B[:, 0:512], xr[:, tb * 512:(tb + 1) * 512]),
                dbg=f"cmb{dk}{tb}b")
            brel(bank, SV, v_x1)
            x1_toks.append(GD.dma(
                d_x1.ap()[dk * 128:(dk + 1) * 128, tb * 512:(tb + 1) * 512],
                X1B[:, 0:512], waits=[(SV, v_x1)], dbg=f"x1w{dk}{tb}"))
        v_x1_done[dk] = SV.v
    tok_x1 = x1_toks[-1]
    if "x1" in taps:
        dt_x1 = nc.dram_tensor("tap_x1", [D, TOK], F32, kind="ExternalOutput")
        LD.dma(dt_x1.ap(), d_x1.ap(), waits=list(x1_toks), dbg="tap_x1")
        kb.tapinfo["x1"] = ([D, TOK], F32)

    if nseg < 8:
        return _finish(kb, nc)

    AR.reset(MK_S5)
    kb.barrier("rst_s8")
    MK_S8 = AR.mark()
    XC2 = [AR.alloc(f"XC2{i}", [128, TOK], F32) for i in range(2)]
    SQ2b = AR.alloc("SQ2b", [128, TOK], F32)
    m2_row = AR.alloc("m2_row", [1, TOK], F32)
    s2_row = AR.alloc("s2_row", [1, TOK], F32)
    mn2_row = AR.alloc("mn2_row", [1, TOK], F32)
    var2_row = AR.alloc("var2_row", [1, TOK], F32)
    sd2_row = AR.alloc("sd2_row", [1, TOK], F32)
    rstd2_row = AR.alloc("rstd2_row", [1, TOK], F32)
    m2_rep = AR.alloc("m2_rep", [128, TOK], F32)
    r2_rep = AR.alloc("r2_rep", [128, TOK], F32)
    SGN = AR.alloc("SGN", [128, 8 * 1024], BF16)
    WS0 = AR.alloc("WS0", [128, 1024], F32)
    WS1 = AR.alloc("WS1", [128, 1024], F32)
    G1B = AR.alloc("G1B", [128, 512], BF16)

    v_t2 = 0
    v_t2A = {}
    for cc in range(8):
        xc = XC2[cc % 2]
        k_ld = LD.dma(xc[:, :], d_x1.ap()[cc * 128:(cc + 1) * 128, :],
                      waits=[(ST, v_t2A.get(cc - 2, 0))] + list(x1_toks), dbg=f"l2x{cc}")
        v_sq = kb.op("vector", lambda e, xc=xc: e.tensor_mul(SQ2b[:, :], xc[:, :], xc[:, :]),
                     waits=([k_ld, (ST, v_t2)] + (bw(0, 1, 2, 3, 4, 5, 6, 7) if cc == 0 else [])),
                     inc=SV, dbg=f"l2sq{cc}")
        for tb in range(4):
            kb.mm(pb[tb][0:1, 0:512], ones128[:, :], xc[:, tb * 512:(tb + 1) * 512],
                  cc == 0, cc == 7, waits=([(SV, v_sq)] if tb == 0 else []), dbg=f"l2m{cc}{tb}")
            kb.mm(pb[4 + tb][0:1, 0:512], ones128[:, :], SQ2b[:, tb * 512:(tb + 1) * 512],
                  cc == 0, cc == 7, dbg=f"l2s{cc}{tb}")
        v_t2 = ST.v
        v_t2A[cc] = v_t2
    for tb in range(4):
        kb.op("vector", lambda e, tb=tb: e.tensor_copy(
            m2_row[:, tb * 512:(tb + 1) * 512], pb[tb][0:1, 0:512]),
            waits=[(ST, v_t2)], dbg=f"l2em{tb}")
        kb.op("vector", lambda e, tb=tb: e.tensor_copy(
            s2_row[:, tb * 512:(tb + 1) * 512], pb[4 + tb][0:1, 0:512]), dbg=f"l2es{tb}")
    v_ln2ev = SV.v
    for i in range(8):
        brel(i, SV, v_ln2ev)
    kb.op("vector", lambda e: e.tensor_scalar_mul(mn2_row[:, :], m2_row[:, :], 1.0 / D), dbg="l2r1")
    kb.op("vector", lambda e: e.tensor_scalar_mul(s2_row[:, :], s2_row[:, :], 1.0 / D), dbg="l2r2")
    kb.op("vector", lambda e: e.tensor_mul(var2_row[:, :], mn2_row[:, :], mn2_row[:, :]), dbg="l2r3")
    v_v_var2 = kb.op("vector", lambda e: e.tensor_sub(var2_row[:, :], s2_row[:, :], var2_row[:, :]),
                     inc=SV, dbg="l2r4")
    v_s_sd2 = kb.op("scalar", lambda e: e.activation(
        sd2_row[:, :], var2_row[:, :], AF.Sqrt, scale=p_eps[0:1, 0:1], bias=p_eps[0:1, 1:2]),
        waits=[(SV, v_v_var2)], inc=SS, dbg="l2sqrt")
    v_v_rstd2 = kb.op("vector", lambda e: e.reciprocal(rstd2_row[:, :], sd2_row[:, :]),
                      waits=[(SS, v_s_sd2)], inc=SV, dbg="l2recip")
    for tb in range(4):
        b0, b1 = tb % 2, 2 + tb % 2
        kb.mm(pb[b0][0:128, 0:512], ones1r[:, :], mn2_row[:, tb * 512:(tb + 1) * 512],
              True, True, waits=[(SV, v_v_rstd2)] + bw(b0), dbg=f"l2rm{tb}")
        v_mm_rep2 = kb.mm(pb[b1][0:128, 0:512], ones1r[:, :], rstd2_row[:, tb * 512:(tb + 1) * 512],
                          True, True, waits=bw(b1), inc=ST, dbg=f"l2rr{tb}")
        kb.op("vector", lambda e, tb=tb, b0=b0: e.tensor_copy(
            m2_rep[:, tb * 512:(tb + 1) * 512], pb[b0][0:128, 0:512]),
            waits=[(ST, v_mm_rep2)], dbg=f"l2re{tb}")
        v_v_rep2 = kb.op("vector", lambda e, tb=tb, b1=b1: e.tensor_copy(
            r2_rep[:, tb * 512:(tb + 1) * 512], pb[b1][0:128, 0:512]), inc=SV, dbg=f"l2rf{tb}")
        brel(b0, SV, v_v_rep2); brel(b1, SV, v_v_rep2)
    v_vB2 = {}
    for cc in range(8):
        xc = XC2[cc % 2]
        k_ld = LD.dma(xc[:, :], d_x1.ap()[cc * 128:(cc + 1) * 128, :],
                      waits=[(SV, v_vB2.get(cc - 2, 0)), (ST, v_t2A[7] if cc < 2 else 0)],
                      dbg=f"l2b{cc}")
        kb.op("vector", lambda e, xc=xc: e.tensor_sub(SQ2b[:, :], xc[:, :], m2_rep[:, :]),
              waits=[k_ld], dbg=f"x2a{cc}")
        v_vB2[cc] = kb.op("vector", lambda e, cc=cc: e.tensor_mul(
            XN2[:, cc * TOK:(cc + 1) * TOK], SQ2b[:, :], r2_rep[:, :]), inc=SV, dbg=f"x2b{cc}")
    v_xn1 = SV.v
    tap("xn1", XN2[:, 0:8 * TOK], [128, 8 * TOK], [(SV, v_xn1)], BF16)

    k_agw = None
    for cc in range(8):
        k_agw = GD.dma(c_ag_i.ap()[cc * 128:(cc + 1) * 128, :], XN2[:, cc * TOK:(cc + 1) * TOK],
                       waits=([(SV, v_xn1)] if cc == 0 else []), dbg=f"agw{cc}", group=(cc > 0))
    v_ccag = kb.op("gpsimd", lambda e: e.collective_compute(
        "AllGather", ALU.bypass, replica_groups=PAIRS,
        ins=[c_ag_i.ap().opt()], outs=[c_ag_o.ap().opt()]),
        waits=[k_agw], inc=SC, dbg="AG")

    v_sgn_done = {}
    for dk in range(8):
        ws = WS0 if dk % 2 == 0 else WS1
        k_wf = LD.dma(ws[:, :], l1_wfT.ap()[dk * 128:(dk + 1) * 128, :],
                      waits=[(SV, v_sgn_done.get(dk - 2, 0))], dbg=f"sgl{dk}")
        kb.op("vector", lambda e, ws=ws: e.tensor_single_scalar(
            SQ2b[:, 0:1024], ws[:, :], 0.0, ALU.is_ge), waits=[k_wf], dbg=f"sga{dk}")
        v_sgn_done[dk] = kb.op("vector", lambda e, dk=dk: e.tensor_scalar(
            SGN[:, dk * 1024:(dk + 1) * 1024], SQ2b[:, 0:1024], 2.0, -1.0,
            ALU.mult, ALU.add), inc=SV, dbg=f"sgb{dk}")
    v_sgn = SV.v
    g1_toks = []
    for cho in range(8):
        for tb in range(4):
            bank = (cho * 4 + tb) % 4
            for dk in range(8):
                kb.mm(pb[bank][0:128, 0:512],
                      SGN[:, dk * 1024 + cho * 128:dk * 1024 + (cho + 1) * 128],
                      XN2[:, dk * TOK + tb * 512:dk * TOK + (tb + 1) * 512],
                      dk == 0, dk == 7,
                      waits=(([(SV, max(v_sgn, v_xn1))] + bw(bank)) if dk == 0 else []),
                      dbg=f"f1{cho}{tb}")
            v_mmf = ST.v
            kb.op("vector", lambda e, b=bank, cho=cho: e.tensor_scalar(
                SQ2b[:, 0:512], pb[b][0:128, 0:512],
                t_l1sb[:, 2 * cho:2 * cho + 1], t_l1sb[:, 2 * cho + 1:2 * cho + 2],
                ALU.mult, ALU.add),
                waits=[(ST, v_mmf), (SS, SS.v)], dbg=f"f1e{cho}{tb}a")
            v_cl = kb.op("vector", lambda e: e.tensor_scalar(
                SQ2b[:, 0:512], SQ2b[:, 0:512], 100.0, -100.0, ALU.min, ALU.max),
                inc=SV, dbg=f"f1e{cho}{tb}b")
            brel(bank, SV, v_cl)
            v_si = kb.op("scalar", lambda e: e.activation(G1B[:, :], SQ2b[:, 0:512], AF.Silu),
                         waits=[(SV, v_cl)] + ([g1_toks[-1]] if g1_toks else []),
                         inc=SS, dbg=f"f1e{cho}{tb}c")
            g1_toks.append(GD.dma(
                d_g1.ap()[cho * 128:(cho + 1) * 128, tb * 512:(tb + 1) * 512],
                G1B[:, :], waits=[(SS, v_si)], dbg=f"g1w{cho}{tb}"))
    tok_g1 = g1_toks[-1]

    if nseg < 9:
        return _finish(kb, nc)

    AR.reset(MK_S8)
    kb.barrier("rst_s9")
    MK_S9 = AR.mark()
    QF1 = AR.alloc("QF1", [128, 4 * 4096], BF16)
    VA1 = AR.alloc("VA1", [128, 32 * 520], BF16)
    MK_W1R = AR.mark()
    XNF = AR.alloc("XNF", [128, 16 * TOK], BF16)
    W1R = AR.alloc("W1R", [128, 8 * 512], BF16)
    KF1 = XN2

    k_agl = None
    for hf in range(2):
        for dk in range(8):
            k_agl = LD.dma(XNF[:, hf * 8 * TOK + dk * TOK:hf * 8 * TOK + (dk + 1) * TOK],
                           c_ag_o.ap()[hf * D + dk * 128:hf * D + (dk + 1) * 128, :],
                           waits=([(SC, v_ccag)] if (hf == 0 and dk == 0) else []),
                           dbg=f"agl{hf}{dk}", group=not (hf == 0 and dk == 0))
    v_vone1 = kb.op("vector", lambda e: e.memset(
        ap3(VA1, 0, 128, 64, [[65, 8 * 32]]), 1.0), inc=SV, dbg="vone1")

    k_cast1 = None
    for dd, ss_ in [(d_l1wq, l1_wqT), (d_l1wk, l1_wkT), (d_l1wv, l1_wvT), (d_l1wo, l1_woT)]:
        k_cast1 = GD.dma(dd.ap(), ss_.ap(), dbg="cast_w1", group=(dd is not d_l1wq))

    def l1_proj_fm(d_w, dst, nm, scale=None):
        k_w = None
        for dk in range(8):
            k_w = LD.dma(W1R[:, dk * 512:(dk + 1) * 512],
                         d_w.ap()[dk * 128:(dk + 1) * 128, :],
                         waits=([k_cast1, (ST, ST.v)] if dk == 0 else []),
                         dbg=nm + f"w{dk}", group=(dk > 0))
        for cc in range(4):
            for hf in range(2):
                for tb in range(4):
                    bank = (cc * 8 + hf * 4 + tb) % 4
                    for dk in range(8):
                        kb.mm(pb[bank][0:128, 0:512],
                              W1R[:, dk * 512 + cc * 128:dk * 512 + (cc + 1) * 128],
                              XNF[:, hf * 8 * TOK + dk * TOK + tb * 512:
                                  hf * 8 * TOK + dk * TOK + (tb + 1) * 512],
                              dk == 0, dk == 7,
                              waits=(([k_w, k_agl] + bw(bank)) if dk == 0 else []),
                              dbg=nm + f"{cc}{hf}{tb}")
                    v_mm1 = ST.v
                    col = cc * 4096 + hf * TOK + tb * 512
                    if scale is None:
                        v_e1 = kb.op("vector", lambda e, b=bank, col=col: e.tensor_copy(
                            dst[:, col:col + 512], pb[b][0:128, 0:512]),
                            waits=[(ST, v_mm1)], inc=SV, dbg=nm + f"e{cc}{hf}{tb}")
                    else:
                        v_e1 = kb.op("vector", lambda e, b=bank, col=col: e.tensor_scalar_mul(
                            dst[:, col:col + 512], pb[b][0:128, 0:512], scale),
                            waits=[(ST, v_mm1)], inc=SV, dbg=nm + f"e{cc}{hf}{tb}")
                    brel(bank, SV, v_e1)

    l1_proj_fm(d_l1wq, QF1, "q1", scale=0.125)
    l1_proj_fm(d_l1wk, KF1, "k1")
    k_wv1 = None
    for dk in range(8):
        k_wv1 = LD.dma(W1R[:, dk * 512:(dk + 1) * 512],
                       d_l1wv.ap()[dk * 128:(dk + 1) * 128, :],
                       waits=([k_cast1, (ST, ST.v)] if dk == 0 else []),
                       dbg=f"v1w{dk}", group=(dk > 0))
    for tcg in range(32):
        hf, tc = tcg // 16, tcg % 16
        bank = tcg % 4
        for dk in range(8):
            kb.mm(pb[bank][0:128, 0:512],
                  XNF[:, hf * 8 * TOK + dk * TOK + tc * 128:hf * 8 * TOK + dk * TOK + (tc + 1) * 128],
                  W1R[:, dk * 512:(dk + 1) * 512],
                  dk == 0, dk == 7,
                  waits=(([k_wv1, k_agl] + bw(bank)) if dk == 0 else []), dbg=f"v1{tcg}")
        v_mm1 = ST.v
        v_e1 = kb.op("vector", lambda e, b=bank, tcg=tcg: e.tensor_copy(
            ap3(VA1, 0, 128, tcg * 520, [[65, 8], [1, 64]]), pb[b][0:128, 0:512]),
            waits=[(ST, v_mm1), (SV, v_vone1)], inc=SV, dbg=f"v1e{tcg}")
        brel(bank, SV, v_e1)
    v_proj1 = SV.v
    tap("qf1", QF1[:, 0:4 * 4096], [128, 4 * 4096], [(SV, v_proj1)], BF16)
    tap("va1", VA1[:, 0:32 * 520], [128, 32 * 520], [(SV, v_proj1)], BF16)

    AR.reset(MK_W1R)
    kb.barrier("rst_nrm")
    PC = AR.alloc("PC", [128, 16], F32)
    MK_NRM = AR.mark()
    SQW = AR.alloc("SQW", [128, 4096], F32)
    nrm_row = AR.alloc("nrm_row", [1, 512], F32)
    mq8 = AR.alloc("mq8", [1, 16], F32)
    crow = AR.alloc("crow", [1, 8], F32)
    kb.op("vector", lambda e: e.memset(mq8[:, :], 0.0), dbg="nq0")
    for src_t, base_col in [(QF1, 0), (KF1, 8)]:
        for cc in range(4):
            v_sqw = kb.op("vector", lambda e, s=src_t, cc=cc: e.tensor_mul(
                SQW[:, :], s[:, cc * 4096:(cc + 1) * 4096], s[:, cc * 4096:(cc + 1) * 4096]),
                waits=[(SV, v_proj1), (ST, ST.v)], inc=SV, dbg=f"nsq{base_col}{cc}")
            for par in range(2):
                h = cc * 2 + par
                pp = par * 64
                for qb in range(8):
                    kb.mm(pb[6][0:1, 0:512], PONE2[0:128, par:par + 1],
                          SQW[0:128, qb * 512:(qb + 1) * 512],
                          True, True, waits=[(SV, v_sqw)] + bw(6), dbg=f"nm{base_col}{h}{qb}")
                    v_nm = ST.v
                    v_ncp = kb.op("vector", lambda e: e.tensor_copy(nrm_row[:, :], pb[6][0:1, 0:512]),
                          waits=[(ST, v_nm)], dbg=f"ne{base_col}{h}{qb}")
                    brel(6, SV, v_ncp)
                    kb.op("vector", lambda e: e.tensor_reduce(
                        nrm_row[0:1, 0:1], nrm_row[0:1, :], AX.X, ALU.max), dbg=f"nr{base_col}{h}{qb}")
                    kb.op("vector", lambda e, h=h, b=base_col: e.tensor_max(
                        mq8[0:1, b + h:b + h + 1], mq8[0:1, b + h:b + h + 1],
                        nrm_row[0:1, 0:1]), dbg=f"nx{base_col}{h}{qb}")
    v_nrm = SV.v
    v_sq8 = kb.op("scalar", lambda e: e.activation(mq8[0:1, :], mq8[0:1, :], AF.Sqrt),
                  waits=[(SV, v_nrm)], inc=SS, dbg="nsqrt")
    kb.op("vector", lambda e: e.tensor_mul(crow[0:1, :], mq8[0:1, 0:8], mq8[0:1, 8:16]),
          waits=[(SS, v_sq8)], dbg="ncr")
    v_crow = kb.op("vector", lambda e: e.tensor_scalar_mul(crow[0:1, :], crow[0:1, :], -1.0),
                   inc=SV, dbg="ncr2")
    kb.op("vector", lambda e: e.memset(ap3(PC, 0, 128, 0, [[2, 8]]), 1.0), dbg="pc1")
    v_pcz = SV.v
    v_pc = v_pcz
    for h in range(8):
        v_pc = bcast_rows(crow[0:1, h:h + 1], PC[:, 2 * h + 1:2 * h + 2], 1, 6,
                          [(SV, v_crow), (SV, v_pc)], f"pc{h}")
    tap("crow", PC[:, 0:16], [128, 16], [(SV, v_pc)], F32)

    if nseg < 10:
        return _finish(kb, nc)

    AR.reset(MK_NRM)
    kb.barrier("rst_s10")
    Y1 = AR.alloc("Y1", [128, 4 * 4096], BF16)
    PBUF = [AR.alloc(f"PBUF{i}", [128, 512], BF16) for i in range(3)]
    CMI = AR.alloc("CMI", [128, 512], F32)
    CM = AR.alloc("CM", [128, 4 * 512], BF16)
    RLrow = AR.alloc("RLrow", [1, 512], F32)
    RLrep = AR.alloc("RLrep", [64, 512], BF16)
    YS = AR.alloc("YS", [64, 512], BF16)
    QZ1 = AR.alloc("QZ1", [128, 512], BF16)

    v_cm = 0
    for mloc in range(4):
        kb.op("gpsimd", g_iota(CMI[:, :], [[1, 512]], -128 * mloc, -1),
              waits=[(SV, SV.v)], dbg=f"cmi{mloc}")
        v_cm = kb.op("vector", lambda e, m=mloc: e.tensor_single_scalar(
            CM[:, m * 512:(m + 1) * 512], CMI[:, :], 0.0, ALU.is_ge),
            waits=[(SG, SG.v)], inc=SV, dbg=f"cm{mloc}")

    sb_toks = []
    v_pv_prev = 0
    for h in range(8):
        cc, par = h // 2, h % 2
        for qb in range(8):
            nkc = 4 * qb + 4
            v_qz1 = kb.op("vector", lambda e, cc=cc, qb=qb, par=par: e.tensor_scalar_mul(
                QZ1[:, 0:512], QF1[:, cc * 4096 + qb * 512:cc * 4096 + (qb + 1) * 512],
                PONE2[:, par:par + 1]),
                waits=[(ST, v_pv_prev)] + ([(SV, v_proj1), (SV, v_pc)]
                                           if (h == 0 and qb == 0) else []),
                inc=SV, dbg=f"qz1_{h}{qb}")
            v_exp_prev = SS.v
            for kc in range(nkc):
                stb = 6 + kc % 2
                kb.op("tensor", lambda e, b=stb, kc=kc, cc=cc: e.matmul(
                    pb[b][0:128, 0:512],
                    KF1[0:128, cc * 4096 + kc * 128:cc * 4096 + (kc + 1) * 128],
                    QZ1[:, 0:512],
                    start=True, stop=True, skip_group_check=True),
                    waits=bw(stb) + ([(SV, v_qz1)] if kc == 0 else []),
                    inc=ST, dbg=f"st{h}{qb}{kc}")
                v_st = ST.v
                pbuf = PBUF[kc % 3]
                v_exp = kb.op("scalar", lambda e, b=stb, p=pbuf, h=h: e.activation(
                    p[:, :], pb[b][0:128, 0:512], AF.Exp,
                    scale=PC[:, 2 * h:2 * h + 1], bias=PC[:, 2 * h + 1:2 * h + 2]),
                    waits=[(ST, v_st), (SV, SV.v)], inc=SS, dbg=f"ex{h}{qb}{kc}")
                brel(stb, SS, v_exp)
                w_pv = [(SS, v_exp)]
                if kc >= 4 * qb:
                    v_mk = kb.op("vector", lambda e, p=pbuf, m=kc - 4 * qb: e.tensor_mul(
                        p[:, :], p[:, :], CM[:, m * 512:(m + 1) * 512]),
                        waits=[(SS, v_exp), (SV, v_cm)], inc=SV, dbg=f"msk{h}{qb}{kc}")
                    w_pv = [(SV, v_mk)]
                kb.op("tensor", lambda e, p=pbuf, h=h, kc=kc: e.matmul(
                    pb[5][0:65, 0:512],
                    VA1[:, kc * 520 + h * 65:kc * 520 + (h + 1) * 65],
                    p[:, :], start=(kc == 0), stop=(kc == nkc - 1), skip_group_check=True),
                    waits=w_pv + (bw(5) if kc == 0 else []), inc=ST, dbg=f"pv{h}{qb}{kc}")
            v_pv = ST.v
            v_pv_prev = v_pv
            v_rl = kb.op("vector", lambda e: e.reciprocal(RLrow[0:1, :], pb[5][64:65, 0:512]),
                         waits=[(ST, v_pv)], inc=SV, dbg=f"rl{h}{qb}")
            v_rep = kb.mm(pb[4][0:64, 0:512], ones1r[0:1, 0:64], RLrow[0:1, :], True, True,
                          waits=[(SV, v_rl)] + bw(4), inc=ST, dbg=f"rlr{h}{qb}")
            v_repc = kb.op("vector", lambda e: e.tensor_copy(RLrep[:, :], pb[4][0:64, 0:512]),
                           waits=[(ST, v_rep)], inc=SV, dbg=f"rlc{h}{qb}")
            brel(4, SV, v_repc)
            ycol = cc * 4096 + qb * 512
            if par == 0:
                v_y1 = kb.op("vector", lambda e, ycol=ycol: e.tensor_mul(
                    Y1[0:64, ycol:ycol + 512], pb[5][0:64, 0:512], RLrep[:, :]),
                    waits=[(SV, v_repc)], inc=SV, dbg=f"y{h}{qb}")
                brel(5, SV, v_y1)
            else:
                v_ys = kb.op("vector", lambda e: e.tensor_mul(
                    YS[:, :], pb[5][0:64, 0:512], RLrep[:, :]),
                    waits=[(SV, v_repc)] + ([sb_toks[-1]] if sb_toks else []),
                    inc=SV, dbg=f"ys{h}{qb}")
                brel(5, SV, v_ys)
                sb_toks.append(GD.dma(
                    bass.AP(Y1, 64 * (4 * 4096) + ycol, [[4 * 4096, 64], [1, 512]]),
                    YS[0:64, :], waits=[(SV, v_ys)], dbg=f"ysc{h}{qb}"))
    v_flash = SV.v
    tap("y1", Y1[:, 0:4 * 4096], [128, 4 * 4096], [(SV, v_flash)] + sb_toks[-1:], BF16)

    if nseg < 11:
        return _finish(kb, nc)

    WO1 = AR.alloc("WO1", [128, 4 * 1024], BF16)
    RSB = AR.alloc("RSB", [128, 1024], BF16)
    XF0 = AR.alloc("XF0", [128, 512], F32)
    GF0 = AR.alloc("GF0", [128, 512], BF16)
    SF0 = AR.alloc("SF0", [128, 512], BF16)
    OUTB = AR.alloc("OUTB", [128, 512], F32)

    k_wo1 = None
    for cc in range(4):
        k_wo1 = LD.dma(WO1[:, cc * 1024:(cc + 1) * 1024],
                       d_l1wo.ap()[cc * 128:(cc + 1) * 128, :],
                       waits=([k_cast1] if cc == 0 else []), dbg="ldwo1", group=(cc > 0))
    rs_w_toks = [[] for _ in range(4)]
    v_rsb_free = 0
    last_tk = None
    for tcg in range(32):
        kbuf = (tcg % 16) // 4
        arow = (tcg % 4) * 128 + (512 if tcg >= 16 else 0)
        for db in range(2):
            bank = (tcg * 2 + db) % 4
            for cc in range(4):
                kb.mm(pb[bank][0:128, 0:512],
                      Y1[:, cc * 4096 + tcg * 128:cc * 4096 + (tcg + 1) * 128],
                      WO1[:, cc * 1024 + db * 512:cc * 1024 + (db + 1) * 512],
                      cc == 0, cc == 3,
                      waits=(([k_wo1, (SV, v_flash)] + sb_toks[-1:] + bw(bank))
                             if cc == 0 else []),
                      dbg=f"wp{tcg}{db}")
            v_mmw = ST.v
            v_ev = kb.op("vector", lambda e, b=bank, db=db: e.tensor_copy(
                RSB[:, db * 512:(db + 1) * 512], pb[b][0:128, 0:512]),
                waits=[(ST, v_mmw)] + ([last_tk] if (db == 0 and last_tk) else []),
                dbg=f"wpe{tcg}{db}")
            brel(bank, SV, v_ev)
        v_rsb = SV.v
        tk = GD.dma(c_rs_i[kbuf].ap()[arow:arow + 128, :], RSB[:, :],
                    waits=[(SV, v_rsb)], dbg=f"rsw{tcg}")
        rs_w_toks[kbuf].append(tk)
        last_tk = tk
    v_ccrs = {}
    for kbuf in range(4):
        v_ccrs[kbuf] = kb.op("gpsimd", lambda e, k=kbuf: e.collective_compute(
            "ReduceScatter", ALU.add, replica_groups=PAIRS,
            ins=[c_rs_i[k].ap().opt()], outs=[c_rs_o[k].ap().opt()]),
            waits=list(rs_w_toks[kbuf]), inc=SC, dbg=f"RS{kbuf}")

    out_toks = []
    v_fc = 0
    for dk in range(8):
        for kbuf in range(4):
            k_x1 = LD.dma(XF0[:, :], d_x1.ap()[dk * 128:(dk + 1) * 128,
                                               kbuf * 512:(kbuf + 1) * 512],
                          waits=[(SV, v_fc)] + list(x1_toks[-1:]), dbg=f"fx{dk}{kbuf}")
            k_g1 = LD.dma(GF0[:, :], d_g1.ap()[dk * 128:(dk + 1) * 128,
                                               kbuf * 512:(kbuf + 1) * 512],
                          waits=[tok_g1, (SV, v_fc)], dbg=f"fg{dk}{kbuf}")
            k_s1 = TP.dma(SF0[:, :], c_rs_o[kbuf].ap()[:, dk * 128:(dk + 1) * 128],
                          waits=[(SC, v_ccrs[kbuf]), (SV, v_fc)] + sb_toks[-1:], transpose=True,
                          dbg=f"fs{dk}{kbuf}")
            kb.op("vector", lambda e: e.tensor_mul(OUTB[:, :], SF0[:, :], GF0[:, :]),
                  waits=[k_x1, k_g1, k_s1] + out_toks[-1:], dbg=f"fc{dk}{kbuf}a")
            v_fc = kb.op("vector", lambda e: e.tensor_add(OUTB[:, :], OUTB[:, :], XF0[:, :]),
                         inc=SV, dbg=f"fc{dk}{kbuf}b")
            out_toks.append(LD.dma(out_T.ap()[dk * 128:(dk + 1) * 128,
                                              kbuf * 512:(kbuf + 1) * 512],
                                   OUTB[:, :], waits=[(SV, v_fc)], dbg=f"fo{dk}{kbuf}"))

    return _finish(kb, nc)


def _finish(kb, nc):
    waits = [(s, s.v) for s in kb.sems.values() if s.v > 0]
    kb.wait_only("sync", waits, dbg="final_wait")
    kb.check()
    with nc.Block() as block:
        for eng_name in KB.ENGS:
            def make(en):
                def emit(eng):
                    for e2, cl in kb.prog:
                        if e2 == en:
                            cl(eng)
                return emit
            getattr(block, eng_name)(make(eng_name))
    return nc, kb



def _host_inputs(inputs, core):
    x = np.asarray(inputs["L0_attn_x"])
    xf = np.ascontiguousarray(x.reshape(B * S, D))
    rows = slice(core * TOK, (core + 1) * TOK)
    hh = core % 2
    d = {}
    d["xT"] = np.ascontiguousarray(xf[rows].T)
    dl = np.asarray(inputs["L0_attn_decay_log"]).reshape(8)
    d["dlog_c"] = np.ascontiguousarray(dl.reshape(8, 1))
    d["dlog_r"] = np.ascontiguousarray(dl.reshape(1, 8))
    d["wqT"] = np.ascontiguousarray(np.asarray(inputs["L0_attn_W_Q"]).T)
    d["wkT"] = np.ascontiguousarray(np.asarray(inputs["L0_attn_W_K"]).T)
    d["wvT"] = np.ascontiguousarray(np.asarray(inputs["L0_attn_W_V"]).T)
    d["wgT"] = np.ascontiguousarray(np.asarray(inputs["L0_attn_W_g_w"]).T)
    d["wgb_row"] = np.ascontiguousarray(np.asarray(inputs["L0_attn_W_g_b"]).reshape(1, CH0))
    d["woT"] = np.ascontiguousarray(np.asarray(inputs["L0_attn_W_O"]).T)
    d["wup"] = np.ascontiguousarray(np.asarray(inputs["L0_ffn_w_up"]))
    d["wgate"] = np.ascontiguousarray(np.asarray(inputs["L0_ffn_w_gate"]))
    d["wdown"] = np.ascontiguousarray(np.asarray(inputs["L0_ffn_w_down"]))
    sl = slice(hh * CH1, (hh + 1) * CH1)
    d["l1_wqT"] = np.ascontiguousarray(np.asarray(inputs["L1_attn_q_heads"])[sl].T)
    d["l1_wkT"] = np.ascontiguousarray(np.asarray(inputs["L1_attn_k_heads"])[sl].T)
    d["l1_wvT"] = np.ascontiguousarray(np.asarray(inputs["L1_attn_v_heads"])[sl].T)
    d["l1_woT"] = np.ascontiguousarray(np.asarray(inputs["L1_attn_output_proj"]).T[sl])
    d["l1_wfT"] = np.ascontiguousarray(np.asarray(inputs["L1_ffn_weights"]).T)
    sc = np.asarray(inputs["L1_ffn_scale"]).reshape(8, 128)
    bi = np.asarray(inputs["L1_ffn_bias"]).reshape(8, 128)
    sb = np.empty((128, 16), np.float32)
    sb[:, 0::2] = sc.T
    sb[:, 1::2] = bi.T
    d["l1_sb"] = sb
    fl = np.zeros((1, 8), np.float32)
    fl[0, 0] = 1.0 if core % 2 == 0 else 0.0
    fl[0, 1] = 0.0 if core % 2 == 0 else 1.0
    d["flags"] = fl
    hs = np.zeros((8, CH0), np.float32)
    for ch in range(CH0):
        hs[ch // DK0, ch] = 1.0
    d["hsel_ch"] = hs
    h65 = np.zeros((8, 4 * 65), np.float32)
    ps = np.zeros((8, 128), np.float32)
    for h in range(H0):
        h65[h, (h // 2) * 65:(h // 2) * 65 + 65] = 1.0
        ps[h, (h % 2) * 64:(h % 2) * 64 + 64] = 1.0
    d["hsel65"] = h65
    d["psel"] = ps
    for k in d:
        if d[k].dtype == np.float64:
            d[k] = d[k].astype(np.float32)
    return d


def kernel(**inputs):
    nc, kb = build()
    in_maps = [_host_inputs(inputs, c) for c in range(NCORES)]
    res = run_bass_kernel_spmd(nc, in_maps, core_ids=list(range(NCORES)))
    outs = []
    for c in range(NCORES):
        outs.append(np.asarray(res.results[c]["out_T"]).T)
    return np.ascontiguousarray(
        np.concatenate(outs, axis=0).reshape(B, S, D)).astype(np.float32)


# version 03
# speedup vs baseline: 1.0923x; 1.0923x over previous
import sys, os
if "/opt/trn_rl_repo" not in sys.path:
    sys.path.insert(0, "/opt/trn_rl_repo")

import numpy as np
import concourse.bass as bass
import concourse.mybir as mybir
from concourse.bass_utils import run_bass_kernel_spmd
from concourse.bass_isa import ReduceOp
import concourse.library_config as libcfg

F32 = mybir.dt.float32
BF16 = mybir.dt.bfloat16
I32 = mybir.dt.int32
U8 = mybir.dt.uint8
ALU = mybir.AluOpType
AF = mybir.ActivationFunctionType
AX = mybir.AxisListType

B, S, D = 4, 4096, 1024
NCORES = 8
TOK = 2048
H0, DK0, CH0 = 8, 64, 512
FF = 3072
CHUNK = 128
NCH = TOK // CHUNK
H1, DK1, CH1 = 8, 64, 512
QMAX = 32767.0
EPS_LN = 1e-5
EPS_ATT = 1e-6
PAIRS = [[0, 1], [2, 3], [4, 5], [6, 7]]
ALL8 = [list(range(8))]
NSEG_DEFAULT = int(os.environ.get("KSEG", "99"))
SUB3 = int(os.environ.get("KSUB3", "9"))
KPHI = int(os.environ.get("KPHI", "1"))
KGRP = int(os.environ.get("KGRP", "99"))
KBANK0 = int(os.environ.get("KBANK0", "0"))
KNONOP = int(os.environ.get("KNONOP", "0"))
KSC4 = int(os.environ.get("KSC4", "9"))
KODD = int(os.environ.get("KODD", "1"))


def rowsz(t):
    r = 1
    for s in t.shape[1:]:
        r *= s
    return r


def ap3(t, p0, np_, base, dims):
    return bass.AP(t, p0 * rowsz(t) + base, [[rowsz(t), np_]] + [list(d) for d in dims])


class Sem:
    __slots__ = ("name", "h", "v", "maxwait")

    def __init__(self, name, h):
        self.name, self.h, self.v, self.maxwait = name, h, 0, 0


class Arena:

    def __init__(self, nc, nbytes):
        self.nc = nc
        self.res = nc.alloc_sbuf_tensor("ARENA", [128, nbytes], U8)
        self.base = nc.lookup_mloc(self.res).addr
        self.off = 0
        self.size = nbytes
        self.n = 0

    def alloc(self, name, shape, dtype):
        per = 1
        for s in shape[1:]:
            per *= s
        per *= mybir.dt.size(dtype)
        self.off = (self.off + 63) & ~63
        assert self.off + per <= self.size, f"arena overflow at {name}: {self.off}+{per}"
        self.n += 1
        h = self.nc.virtual_sbuf_tensor(f"{name}_{self.n}", list(shape), dtype)
        self.nc._tensor(h.name, h.shape, h.dtype, type="SB")
        mloc = self.nc.lookup_mloc(h)
        mloc.allocated = True
        mloc.addr = self.base + self.off
        self.off += per
        return h

    def mark(self):
        return self.off

    def reset(self, m):
        self.off = m


class KB:
    ENGS = ("tensor", "vector", "scalar", "sync", "gpsimd")

    def __init__(self, nc):
        self.nc = nc
        self.prog = []
        self.meta = []
        self.sems = {}
        self.tapinfo = {}
        self.auto = {}

    def sem(self, name):
        s = Sem(name, self.nc.alloc_semaphore(name=name))
        self.sems[name] = s
        return s

    def op(self, eng, fn, waits=(), inc=None, inc_by=1, dbg=""):
        if inc is None:
            inc = self.auto.get(eng)
        waits = list(waits)
        if eng in ("vector", "scalar", "gpsimd") and eng in self.auto:
            s_ = self.auto[eng]
            waits.append((s_, s_.v))
        waits = tuple((s, v) for (s, v) in waits if s is not None and v > 0)
        for s, v in waits:
            s.maxwait = max(s.maxwait, v)

        def cl(e, fn=fn, waits=waits, inc=inc, inc_by=inc_by):
            for s, v in waits:
                e.wait_ge(s.h, v)
            ins = fn(e)
            if inc is not None:
                assert ins is not None, "inc on an op that emitted no instruction"
                ins.then_inc(inc.h, inc_by)
        self.prog.append((eng, cl))
        self.meta.append((eng, waits, inc, inc_by, dbg))
        if inc is not None:
            inc.v += inc_by
            return inc.v
        return None

    def dma_raw(self, q, dst, src, sem, waits=(), transpose=False, dbg="dma"):
        def fn(e, dst=dst, src=src, tp=transpose):
            if tp:
                return e.dma_start(out=dst, in_=src, transpose=True)
            return e.dma_start(out=dst, in_=src)
        return self.op(q, fn, waits=waits, inc=sem, inc_by=16, dbg=dbg)


    def wait_only(self, eng, waits, dbg="w"):
        waits = [(s, v) for (s, v) in waits if s is not None and v > 0]
        for s, v in waits:
            s.maxwait = max(s.maxwait, v)

        def cl(e, waits=tuple(waits)):
            for s, v in waits:
                e.wait_ge(s.h, v)
        self.prog.append((eng, cl))
        self.meta.append((eng, tuple(waits), None, 0, dbg))

    def barrier(self, dbg="barrier"):
        waits = [(s, s.v) for s in self.sems.values() if s.v > 0]
        for e in self.ENGS:
            self.wait_only(e, waits, dbg=dbg + "_" + e)

    def mm(self, out, lhsT, rhs, start, stop, waits=(), inc=None, dbg="mm", tp=None):
        return self.op(
            "tensor",
            lambda e, o=out, l=lhsT, r=rhs, st=start, sp=stop, tp=tp:
                e.matmul(o, l, r, start=st, stop=sp, skip_group_check=True,
                         tile_position=tp),
            waits=waits, inc=inc, dbg=dbg)

    def check(self):
        for s in self.sems.values():
            assert s.maxwait <= s.v, f"sem {s.name}: wait {s.maxwait} > final {s.v}"
        streams = {e: [] for e in self.ENGS}
        for (e, w, inc, ib, dbg) in self.meta:
            streams[e].append((w, inc, ib, dbg))
        cnt = {name: 0 for name in self.sems}
        ptr = {e: 0 for e in self.ENGS}
        progress = True
        while progress:
            progress = False
            for e in self.ENGS:
                while ptr[e] < len(streams[e]):
                    w, inc, ib, dbg = streams[e][ptr[e]]
                    if all(cnt[s.name] >= v for s, v in w):
                        if inc is not None:
                            cnt[inc.name] += ib
                        ptr[e] += 1
                        progress = True
                    else:
                        break
        stuck = {e: ptr[e] for e in self.ENGS if ptr[e] < len(streams[e])}
        if stuck:
            msg = []
            for e, p in stuck.items():
                w, inc, ib, dbg = streams[e][p]
                need = [(s.name, v, cnt[s.name]) for s, v in w if cnt[s.name] < v]
                msg.append(f"{e}@{p} '{dbg}' needs {need}")
            raise AssertionError("DEADLOCK in static check: " + "; ".join(msg))


class Stream:

    def __init__(self, kb, q, name, R=4):
        self.kb, self.q, self.R = kb, q, R
        self.sems = [kb.sem(f"{name}{i}") for i in range(R)]
        self.g = 0
        self._open = False

    def dma(self, dst, src, waits=(), transpose=False, dbg="dma", group=False):
        s = self.sems[self.g % self.R]
        if not group or not self._open:
            if not group:
                self.g += 1
                s = self.sems[self.g % self.R]
            gate = [(s, s.v)]
            self._open = True
        else:
            gate = []
        v = self.kb.dma_raw(self.q, dst, src, s, waits=list(waits) + gate,
                            transpose=transpose, dbg=dbg)
        return (s, v)

    def new_group(self):
        self._open = False


def build(nseg=NSEG_DEFAULT, taps=()):
    taps = set(taps)
    nc = bass.Bass("TRN2", target_bir_lowering=False)
    kb = KB(nc)

    def inp(name, shape, dtype=F32):
        return nc.dram_tensor(name, list(shape), dtype, kind="ExternalInput")

    xT = inp("xT", [D, TOK])
    dlog_c = inp("dlog_c", [8, 1]); dlog_r = inp("dlog_r", [1, 8])
    wqT = inp("wqT", [D, CH0]); wkT = inp("wkT", [D, CH0])
    wvT = inp("wvT", [D, CH0]); wgT = inp("wgT", [D, CH0])
    wgb_row = inp("wgb_row", [1, CH0])
    woT = inp("woT", [CH0, D])
    wup = inp("wup", [FF, D]); wgate = inp("wgate", [FF, D])
    wdown = inp("wdown", [D, FF])
    l1_wqT = inp("l1_wqT", [D, CH1]); l1_wkT = inp("l1_wkT", [D, CH1])
    l1_wvT = inp("l1_wvT", [D, CH1]); l1_woT = inp("l1_woT", [CH1, D])
    l1_wfT = inp("l1_wfT", [D, D])
    l1_sb = inp("l1_sb", [128, 16])
    flags = inp("flags", [1, 8])
    hsel_ch = inp("hsel_ch", [8, CH0])
    hsel65 = inp("hsel65", [8, 4 * 65])
    psel = inp("psel", [8, 128])

    out_T = nc.dram_tensor("out_T", [D, TOK], F32, kind="ExternalOutput")

    dram = lambda n, s, dt=BF16: nc.dram_tensor(n, list(s), dt)
    d_wq = dram("d_wq", [D, CH0]); d_wk = dram("d_wk", [D, CH0])
    d_wv = dram("d_wv", [D, CH0]); d_wg = dram("d_wg", [D, CH0])
    d_wo = dram("d_wo", [CH0, D])
    d_wupq = dram("d_wupq", [FF, D]); d_wgtq = dram("d_wgtq", [FF, D])
    d_wdnq = dram("d_wdnq", [D, FF])
    d_wupqT = dram("d_wupqT", [D, FF]); d_wgtqT = dram("d_wgtqT", [D, FF])
    d_wdnqT = dram("d_wdnqT", [FF, D])
    d_t5 = dram("d_t5", [FF, TOK])
    d_u = dram("d_u", [FF, TOK]); d_g = dram("d_g", [FF, TOK]); d_t4 = dram("d_t4", [FF, TOK])
    d_x1 = dram("d_x1", [D, TOK], F32)
    d_g1 = dram("d_g1", [D, TOK])
    d_l1wq = dram("d_l1wq", [D, CH1]); d_l1wk = dram("d_l1wk", [D, CH1])
    d_l1wv = dram("d_l1wv", [D, CH1]); d_l1wo = dram("d_l1wo", [CH1, D])
    c_ar_i = [dram(f"c_ar_i{i}", [1, 16], F32) for i in range(4)]
    c_ar_o = [dram(f"c_ar_o{i}", [1, 16], F32) for i in range(4)]
    c_st_i = dram("c_st_i", [128, 4 * 65], F32)
    c_st_o = dram("c_st_o", [128, 4 * 65], F32)
    c_ag_i = dram("c_ag_i", [D, TOK])
    c_ag_o = dram("c_ag_o", [2 * D, TOK])
    c_rs_i = [dram(f"c_rs_i{i}", [1024, D]) for i in range(4)]
    c_rs_o = [dram(f"c_rs_o{i}", [512, D]) for i in range(4)]

    SV = kb.sem("SV"); SS = kb.sem("SS"); ST = kb.sem("ST")
    SG = kb.sem("SG")
    SC = kb.sem("SC")
    kb.auto = {"vector": SV, "scalar": SS, "tensor": ST, "gpsimd": SG}
    LD = Stream(kb, "sync", "LD", 4)
    TP = Stream(kb, "sync", "TP", 4)
    GD = Stream(kb, "gpsimd", "GD", 4)

    pb = [nc.alloc_psum_tensor(f"pb{i}", [128, 512], F32) for i in range(8)]
    bank_free = {i: (None, 0) for i in range(8)}

    def bw(*idx):
        out = []
        for i in idx:
            s, v = bank_free[i]
            if s is not None:
                out.append((s, v))
        return out

    def brel(i, sem, val):
        bank_free[i] = (sem, val)

    AR = Arena(nc, ((nc.sbuf_bytes_remaining - 2048) // 64) * 64)

    ones128 = AR.alloc("ones128", [128, 1], F32)
    ones1r = AR.alloc("ones1r", [1, 128], F32)
    ones8x64 = AR.alloc("ones8x64", [8, 64], F32)
    mask_all = AR.alloc("mask_all", [128, H0 * 128], F32)
    gammaQ = AR.alloc("gammaQ", [128, 4 * 128], F32)
    gammaK = AR.alloc("gammaK", [128, CH0], F32)
    gammaC = AR.alloc("gammaC", [128, 4 * 65], F32)
    t_brep = AR.alloc("t_brep", [128, CH0], F32)
    t_flag = AR.alloc("t_flag", [1, 8], F32)
    fl_contrib = AR.alloc("fl_contrib", [128, 1], F32)
    fl_use = AR.alloc("fl_use", [128, 1], F32)
    t_l1sb = AR.alloc("t_l1sb", [128, 16], F32)
    p_eps = AR.alloc("p_eps", [1, 2], F32)
    t_s = [AR.alloc(f"t_s{i}", [128, 4], F32) for i in range(1, 6)]
    t_locmax = AR.alloc("t_locmax", [128, 2], F32)
    t_ar = AR.alloc("t_ar", [1, 16], F32)
    t_arstage = AR.alloc("t_arstage", [1, 16], F32)
    t_ident = AR.alloc("t_ident", [128, 128], F32)
    t_row128 = AR.alloc("t_row128", [1, 128], F32)
    PONE2 = AR.alloc("PONE2", [128, 2], F32)
    PH1 = AR.alloc("PH1", [128, 512], F32)
    PH2 = AR.alloc("PH2", [128, 512], F32)

    XN2 = AR.alloc("XN2", [128, 8 * TOK], BF16)
    YGT = AR.alloc("YGT", [128, 4 * TOK], BF16)

    MK_SETUP = AR.mark()
    t_dlc = AR.alloc("t_dlc", [8, 1], F32)
    t_dlr = AR.alloc("t_dlr", [1, 8], F32)
    t_g8 = AR.alloc("t_g8", [8, 1], F32)
    t_lg8 = AR.alloc("t_lg8", [8, 1], F32)
    t_lgr = AR.alloc("t_lgr", [1, 8], F32)
    t_lgr2 = AR.alloc("t_lgr2", [1, 8], F32)
    t_lgbc = AR.alloc("t_lgbc", [128, 8], F32)
    t_gc8 = AR.alloc("t_gc8", [8, 1], F32)
    t_hsel = AR.alloc("t_hsel", [8, CH0], F32)
    t_hsel65 = AR.alloc("t_hsel65", [8, 4 * 65], F32)
    t_psel = AR.alloc("t_psel", [8, 128], F32)
    t_iotaD = AR.alloc("t_iotaD", [128, 128], F32)
    t_reluD = AR.alloc("t_reluD", [128, 128], F32)
    t_causal = AR.alloc("t_causal", [128, 128], F32)
    t_jcol = AR.alloc("t_jcol", [128, 1], F32)
    t_rev = AR.alloc("t_rev", [128, 1], F32)
    t_jp1 = AR.alloc("t_jp1", [128, 1], F32)
    t_ramp = AR.alloc("t_ramp", [128, 128], F32)
    t_lnch = AR.alloc("t_lnch", [1, CH0], F32)
    t_lnchb = AR.alloc("t_lnchb", [128, CH0], F32)
    t_lnG = AR.alloc("t_lnG", [128, 4], F32)
    t_wgb = AR.alloc("t_wgb", [1, CH0], F32)
    p_128 = AR.alloc("p_128", [8, 2], F32)
    t_scr8 = AR.alloc("t_scr8", [8, 4 * 65], F32)

    def tap(name, src_ap, shape, waits, dtype=F32):
        if name not in taps:
            return
        dt_ = nc.dram_tensor("tap_" + name, list(shape), dtype, kind="ExternalOutput")
        LD.dma(dt_.ap(), src_ap, waits=waits, dbg=f"tap_{name}")
        kb.tapinfo[name] = (list(shape), dtype)

    for i, (dst, src) in enumerate([
            (t_dlc[:, :], dlog_c.ap()), (t_dlr[:, :], dlog_r.ap()),
            (t_hsel[:, :], hsel_ch.ap()), (t_hsel65[:, :], hsel65.ap()), (t_psel[:, :], psel.ap()),
            (t_flag[:, :], flags.ap()), (t_l1sb[:, :], l1_sb.ap()),
            (t_wgb[:, :], wgb_row.ap())]):
        k_ld0 = LD.dma(dst, src, dbg="ld_small", group=(i > 0))
    for i, (dd, ss_) in enumerate([(d_wq, wqT), (d_wk, wkT), (d_wv, wvT),
                                   (d_wg, wgT), (d_wo, woT)]):
        k_cast = GD.dma(dd.ap(), ss_.ap(), dbg="cast_w0", group=(i > 0))

    def g_iota(t_ap, pattern, base, chm):
        return lambda e: e.iota(t_ap, pattern, base=base, channel_multiplier=chm,
                                allow_small_or_imprecise_dtypes=True)

    kb.op("gpsimd", g_iota(t_iotaD[:, :], [[1, 128]], 0, -1), dbg="iotaD")
    v_g_iota = kb.op("gpsimd", g_iota(t_jcol[:, 0:1], [[0, 1]], 0, 1), dbg="iotaJ") or SG.v

    for t_, val in [(ones128, 1.0), (ones1r, 1.0), (ones8x64, 1.0)]:
        kb.op("vector", lambda e, t=t_, v=val: e.memset(t[:, :], v), dbg="memset")
    kb.op("vector", lambda e: e.memset(p_eps[0:1, 0:1], 1.0), dbg="ms")
    kb.op("vector", lambda e: e.memset(p_eps[0:1, 1:2], EPS_LN), dbg="ms")
    kb.op("vector", lambda e: e.memset(p_128[:, 0:1], float(CHUNK)), dbg="ms")
    kb.op("vector", lambda e: e.memset(p_128[:, 1:2], 0.0), dbg="ms")
    kb.op("vector", lambda e: e.memset(t_arstage[:, :], 0.0), dbg="ms")
    for _ts in t_s:
        kb.op("vector", lambda e, t=_ts: e.memset(t[:, :], 0.0), dbg="ms_ts")
    v_vconst = SV.v
    v_ident = kb.op("vector", lambda e: e.tensor_single_scalar(
        t_ident[:, :], t_iotaD[:, :], 0.0, ALU.is_equal),
        waits=[(SG, v_g_iota)], inc=SV, dbg="ident")

    def bcast_rows(src_row_ap, dst_ap, ncols, bank, waits, dbg):
        v_mm = kb.mm(pb[bank][0:128, 0:ncols], ones1r[:, :], src_row_ap, True, True,
                     waits=list(waits) + bw(bank) + [(SV, v_vconst)], inc=ST, dbg=dbg + "m")
        v_cp = kb.op("vector", lambda e, d=dst_ap, b=bank, n=ncols: e.tensor_copy(
            d, pb[b][0:128, 0:n]), waits=[(ST, v_mm)], inc=SV, dbg=dbg + "c")
        brel(bank, SV, v_cp)
        return v_cp

    def preduce(col_ap, out11_ap, bank, waits, op, dbg):
        v_mm = kb.op("tensor", lambda e, b=bank, c=col_ap: e.transpose(
            pb[b][0:1, 0:128], c, t_ident[:, :]),
            waits=list(waits) + bw(bank) + [(SV, v_ident)], inc=ST, dbg=dbg + "t")
        kb.op("vector", lambda e, b=bank: e.tensor_copy(t_row128[:, :], pb[b][0:1, 0:128]),
              waits=[(ST, v_mm)], dbg=dbg + "c")
        v_rd = kb.op("vector", lambda e, o=out11_ap, op=op: e.tensor_reduce(
            o, t_row128[0:1, :], AX.X, op), inc=SV, dbg=dbg + "r")
        brel(bank, SV, v_rd)
        return v_rd

    v_tmp = kb.op("scalar", lambda e: e.activation(t_g8[:, :], t_dlc[:, :], AF.Sigmoid),
                  waits=[k_ld0], inc=SS, dbg="gam")
    v_s_lg8 = kb.op("scalar", lambda e: e.activation(t_lg8[:, :], t_g8[:, :], AF.Ln),
                    waits=[(SS, v_tmp)], inc=SS, dbg="lngam")
    v_tmp = kb.op("scalar", lambda e: e.activation(t_lgr[:, :], t_dlr[:, :], AF.Sigmoid),
                  waits=[k_ld0], inc=SS, dbg="gamr")
    v_s_lgr = kb.op("scalar", lambda e: e.activation(t_lgr2[:, :], t_lgr[:, :], AF.Ln),
                    waits=[(SS, v_tmp)], inc=SS, dbg="lngamr")
    v_s_gc8 = kb.op("scalar", lambda e: e.activation(
        t_gc8[:, :], t_lg8[:, :], AF.Exp, scale=p_128[:, 0:1], bias=p_128[:, 1:2]),
        waits=[(SV, v_vconst), (SS, v_s_lg8)], inc=SS, dbg="gc8")
    v_g_lgbc = bcast_rows(t_lgr2[0:1, :], t_lgbc[:, 0:8], 8, 6, [(SS, v_s_lgr)], "lgbc")

    v_v_caus = kb.op("vector", lambda e: e.tensor_single_scalar(
        t_causal[:, :], t_iotaD[:, :], 0.0, ALU.is_ge),
        waits=[(SG, v_g_iota)], inc=SV, dbg="caus")
    v_s_relu = kb.op("scalar", lambda e: e.activation(t_reluD[:, :], t_iotaD[:, :], AF.Relu),
                     waits=[(SG, v_g_iota)], inc=SS, dbg="reluD")
    kb.op("vector", lambda e: e.tensor_scalar(
        t_rev[:, 0:1], t_jcol[:, 0:1], -1.0, 127.0, ALU.mult, ALU.add),
        waits=[(SG, v_g_iota)], dbg="rev")
    kb.op("vector", lambda e: e.tensor_scalar_add(t_jp1[:, 0:1], t_jcol[:, 0:1], 1.0), dbg="jp1")
    kb.op("vector", lambda e: e.tensor_single_scalar(
        PONE2[:, 0:1], t_jcol[:, 0:1], 64.0, ALU.is_lt), dbg="pmb0")
    kb.op("vector", lambda e: e.tensor_single_scalar(
        PONE2[:, 1:2], t_jcol[:, 0:1], 64.0, ALU.is_ge), dbg="pmb1")
    v_v_rev = kb.op("vector", lambda e: e.tensor_scalar(
        t_ramp[:, :], t_iotaD[:, :], t_jp1[:, 0:1], None, ALU.add), inc=SV, dbg="ramp")

    v_ss = v_s_relu
    for h in range(H0):
        v_vv = kb.op("vector", lambda e, h=h: e.tensor_scalar_mul(
            PH1[:, 0:128], t_reluD[:, :], t_lgbc[:, h:h + 1]),
            waits=[(SS, v_ss), (SV, v_g_lgbc)], inc=SV, dbg=f"mk{h}a")
        v_ss = kb.op("scalar", lambda e: e.activation(PH2[:, 0:128], PH1[:, 0:128], AF.Exp),
                     waits=[(SV, v_vv)], inc=SS, dbg=f"mk{h}b")
        kb.op("vector", lambda e, h=h: e.tensor_mul(
            mask_all[:, h * 128:(h + 1) * 128], PH2[:, 0:128], t_causal[:, :]),
            waits=[(SS, v_ss), (SV, v_v_caus)], dbg=f"mk{h}c")
    v_masks = SV.v

    v_mm_lnch = kb.mm(pb[0][0:1, 0:512], t_lg8[:, :], t_hsel[:, :], True, True,
                      waits=[(SS, v_s_lg8), k_ld0] + bw(0), inc=ST, dbg="lnch")
    v_v_lnch = kb.op("vector", lambda e: e.tensor_copy(t_lnch[:, :], pb[0][0:1, 0:512]),
                     waits=[(ST, v_mm_lnch)], inc=SV, dbg="lnch_ev")
    brel(0, SV, v_v_lnch)
    v_g_lnchb = bcast_rows(t_lnch[0:1, :], t_lnchb[:, :], 512, 7, [(SV, v_v_lnch)], "lnchb")
    v_v_gk1 = kb.op("vector", lambda e: e.tensor_scalar_mul(
        gammaK[:, :], t_lnchb[:, :], t_rev[:, 0:1]),
        waits=[(SV, max(v_g_lnchb, v_v_rev))], inc=SV, dbg="gK1")
    v_s_gK = kb.op("scalar", lambda e: e.activation(gammaK[:, :], gammaK[:, :], AF.Exp),
                   waits=[(SV, v_v_gk1)], inc=SS, dbg="gK2")

    v_mmg = None
    for cc in range(4):
        v_mmg = kb.mm(pb[1][0:128, cc:cc + 1], t_hsel[:, cc * 128:(cc + 1) * 128], t_lg8[:, :],
                      cc == 0, cc == 3,
                      waits=(([(SS, v_s_lg8)] + bw(1)) if cc == 0 else []),
                      inc=(ST if cc == 3 else None), dbg=f"lnG{cc}")
    v_v_lnG = kb.op("vector", lambda e: e.tensor_copy(t_lnG[:, :], pb[1][0:128, 0:4]),
                    waits=[(ST, v_mmg)], inc=SV, dbg="lnG_ev")
    brel(1, SV, v_v_lnG)
    for cc in range(4):
        kb.op("vector", lambda e, cc=cc: e.tensor_scalar_mul(
            gammaQ[:, cc * 128:(cc + 1) * 128], t_ramp[:, :], t_lnG[:, cc:cc + 1]),
            waits=[(SV, max(v_v_lnG, v_v_rev))], dbg=f"gQ{cc}")
    v_v_gq = SV.v
    v_s_gQ = kb.op("scalar", lambda e: e.activation(gammaQ[:, :], gammaQ[:, :], AF.Exp),
                   waits=[(SV, v_v_gq)], inc=SS, dbg="gQ2")

    v_v_gca = kb.op("vector", lambda e: e.tensor_scalar_mul(
        t_scr8[:, :], t_hsel65[:, :], t_gc8[:, 0:1]),
        waits=[(SS, v_s_gc8), k_ld0], inc=SV, dbg="gCa")
    v_mm_gc = kb.mm(pb[2][0:128, 0:260], t_psel[:, :], t_scr8[:, 0:260], True, True,
                    waits=[(SV, v_v_gca), k_ld0] + bw(2), inc=ST, dbg="gCmm0")
    v_v_gC = kb.op("vector", lambda e: e.tensor_copy(gammaC[:, 0:260], pb[2][0:128, 0:260]),
                   waits=[(ST, v_mm_gc)], inc=SV, dbg="gCe0")
    brel(2, SV, v_v_gC)

    v_g_brep = bcast_rows(t_wgb[0:1, :], t_brep[:, :], 512, 6, [k_ld0], "brep")
    bcast_rows(t_flag[0:1, 0:1], fl_contrib[:, 0:1], 1, 7, [k_ld0], "flc")
    v_g_fl = bcast_rows(t_flag[0:1, 1:2], fl_use[:, 0:1], 1, 7, [k_ld0], "flu")

    if nseg < 1:
        return _finish(kb, nc)

    AR.reset(MK_SETUP)
    kb.barrier("rst_setup")
    MK_LN = AR.mark()
    XN1 = AR.alloc("XN1", [128, 8 * TOK], BF16)
    MK_B = AR.mark()
    XC = [AR.alloc(f"XC{i}", [128, TOK], F32) for i in range(2)]
    SQ = AR.alloc("SQ", [128, TOK], F32)
    MK_ROWS = AR.mark()
    m_row = AR.alloc("m_row", [1, TOK], F32)
    s_row = AR.alloc("s_row", [1, TOK], F32)
    mn_row = AR.alloc("mn_row", [1, TOK], F32)
    var_row = AR.alloc("var_row", [1, TOK], F32)
    sd_row = AR.alloc("sd_row", [1, TOK], F32)
    rstd_row = AR.alloc("rstd_row", [1, TOK], F32)
    m_rep = AR.alloc("m_rep", [128, TOK], F32)
    r_rep = AR.alloc("r_rep", [128, TOK], F32)

    v_t = 0
    v_tA = {}
    w_first = bw(0, 1, 2, 3, 4, 5, 6, 7)
    for cc in range(8):
        xc = XC[cc % 2]
        k_ld = LD.dma(xc[:, :], xT.ap()[cc * 128:(cc + 1) * 128, :],
                      waits=[(ST, v_tA.get(cc - 2, 0))], dbg=f"ld_x{cc}")
        v_sq = kb.op("vector", lambda e, xc=xc: e.tensor_mul(SQ[:, :], xc[:, :], xc[:, :]),
                     waits=([k_ld, (ST, v_t)] + (w_first if cc == 0 else [])),
                     inc=SV, dbg=f"lnsq{cc}")
        for tb in range(4):
            kb.mm(pb[tb][0:1, 0:512], ones128[:, :],
                  xc[:, tb * 512:(tb + 1) * 512],
                  cc == 0, cc == 7,
                  waits=([(SV, v_sq)] if tb == 0 else []), dbg=f"lnm{cc}{tb}")
            kb.mm(pb[4 + tb][0:1, 0:512], ones128[:, :], SQ[:, tb * 512:(tb + 1) * 512],
                  cc == 0, cc == 7, dbg=f"lns{cc}{tb}")
        v_t = ST.v
        v_tA[cc] = v_t
    for tb in range(4):
        kb.op("vector", lambda e, tb=tb: e.tensor_copy(
            m_row[:, tb * 512:(tb + 1) * 512], pb[tb][0:1, 0:512]),
            waits=[(ST, v_t)], dbg=f"lnem{tb}")
        kb.op("vector", lambda e, tb=tb: e.tensor_copy(
            s_row[:, tb * 512:(tb + 1) * 512], pb[4 + tb][0:1, 0:512]), dbg=f"lnes{tb}")
    v_ln_ev = SV.v
    for i in range(8):
        brel(i, SV, v_ln_ev)
    kb.op("vector", lambda e: e.tensor_scalar_mul(mn_row[:, :], m_row[:, :], 1.0 / D), dbg="lnr1")
    kb.op("vector", lambda e: e.tensor_scalar_mul(s_row[:, :], s_row[:, :], 1.0 / D), dbg="lnr2")
    kb.op("vector", lambda e: e.tensor_mul(var_row[:, :], mn_row[:, :], mn_row[:, :]), dbg="lnr3")
    v_v_var = kb.op("vector", lambda e: e.tensor_sub(var_row[:, :], s_row[:, :], var_row[:, :]),
                    inc=SV, dbg="lnr4")
    v_s_sd = kb.op("scalar", lambda e: e.activation(
        sd_row[:, :], var_row[:, :], AF.Sqrt, scale=p_eps[0:1, 0:1], bias=p_eps[0:1, 1:2]),
        waits=[(SV, v_v_var)], inc=SS, dbg="lnsqrt")
    v_v_rstd = kb.op("vector", lambda e: e.reciprocal(rstd_row[:, :], sd_row[:, :]),
                     waits=[(SS, v_s_sd)], inc=SV, dbg="lnrecip")
    for tb in range(4):
        b0, b1 = tb % 2, 2 + tb % 2
        kb.mm(pb[b0][0:128, 0:512], ones1r[:, :], mn_row[:, tb * 512:(tb + 1) * 512],
              True, True, waits=[(SV, v_v_rstd)] + bw(b0), dbg=f"lnrm{tb}")
        v_mm_rep = kb.mm(pb[b1][0:128, 0:512], ones1r[:, :], rstd_row[:, tb * 512:(tb + 1) * 512],
                         True, True, waits=bw(b1), inc=ST, dbg=f"lnrr{tb}")
        kb.op("vector", lambda e, tb=tb, b0=b0: e.tensor_copy(
            m_rep[:, tb * 512:(tb + 1) * 512], pb[b0][0:128, 0:512]),
            waits=[(ST, v_mm_rep)], dbg=f"lnre{tb}")
        v_v_rep = kb.op("vector", lambda e, tb=tb, b1=b1: e.tensor_copy(
            r_rep[:, tb * 512:(tb + 1) * 512], pb[b1][0:128, 0:512]), inc=SV, dbg=f"lnrf{tb}")
        brel(b0, SV, v_v_rep); brel(b1, SV, v_v_rep)
    v_vB = {}
    for cc in range(8):
        xc = XC[cc % 2]
        k_ld = LD.dma(xc[:, :], xT.ap()[cc * 128:(cc + 1) * 128, :],
                      waits=[(SV, v_vB.get(cc - 2, 0)),
                             (ST, v_tA[7] if cc < 2 else 0)], dbg=f"ld2_x{cc}")
        kb.op("vector", lambda e, xc=xc: e.tensor_sub(SQ[:, :], xc[:, :], m_rep[:, :]),
              waits=[k_ld], dbg=f"xna{cc}")
        v_vB[cc] = kb.op("vector", lambda e, cc=cc: e.tensor_mul(
            XN1[:, cc * TOK:(cc + 1) * TOK], SQ[:, :], r_rep[:, :]), inc=SV, dbg=f"xnb{cc}")
    v_xn = SV.v
    tap("xn", XN1[:, 0:8 * TOK], [128, 8 * TOK], [(SV, v_xn)], BF16)

    if nseg < 2:
        return _finish(kb, nc)

    AR.reset(MK_ROWS)
    kb.barrier("rst_rows")
    I32S = AR.alloc("I32S", [128, TOK], I32)
    v_vm = kb.op("vector", lambda e: e.tensor_reduce(
        t_locmax[:, 0:1], XN1[:, 0:8 * TOK], AX.X, ALU.max, apply_absolute_value=True),
        waits=[(SV, v_xn)], inc=SV, dbg="xnmax")
    v_gm = preduce(t_locmax[:, 0:1], t_arstage[0:1, 0:1], 6, [(SV, v_vm)], ALU.max, "xnmax_p")
    k_d1 = GD.dma(c_ar_i[0].ap(), t_arstage[:, :], waits=[(SV, v_gm)], dbg="ar1w")
    v_cc1 = kb.op("gpsimd", lambda e: e.collective_compute(
        "AllReduce", ALU.max, replica_groups=ALL8,
        ins=[c_ar_i[0].ap().opt()], outs=[c_ar_o[0].ap().opt()]),
        waits=[k_d1], inc=SC, dbg="AR1")
    k_d1o = GD.dma(t_ar[:, :], c_ar_o[0].ap(), waits=[(SC, v_cc1)], dbg="ar1r")
    v_g_s1 = bcast_rows(t_ar[0:1, 0:1], t_s[0][:, 0:1], 1, 6, [k_d1o], "s1bc")
    kb.op("vector", lambda e: e.tensor_scalar(
        t_s[0][:, 0:1], t_s[0][:, 0:1], 1.0 / QMAX, 1e-12, ALU.mult, ALU.max),
        waits=[(SV, v_g_s1)], dbg="s1a")
    v_v_s1 = kb.op("vector", lambda e: e.reciprocal(t_s[0][:, 1:2], t_s[0][:, 0:1]),
                   inc=SV, dbg="s1b")
    for cc in range(8):
        kb.op("vector", lambda e, cc=cc: e.tensor_scalar_mul(
            SQ[:, :], XN1[:, cc * TOK:(cc + 1) * TOK], t_s[0][:, 1:2]),
            waits=([(SV, v_v_s1)] if cc == 0 else []), dbg=f"xqa{cc}")
        kb.op("vector", lambda e: e.tensor_copy(I32S[:, :], SQ[:, :]), dbg=f"xqb{cc}")
        kb.op("vector", lambda e: e.tensor_copy(SQ[:, :], I32S[:, :]), dbg=f"xqc{cc}")
        kb.op("vector", lambda e, cc=cc: e.tensor_scalar_mul(
            XN2[:, cc * TOK:(cc + 1) * TOK], SQ[:, :], t_s[0][:, 0:1]), dbg=f"xqd{cc}")
    v_xq = SV.v
    tap("xq", XN2[:, 0:8 * TOK], [128, 8 * TOK], [(SV, v_xq)], BF16)
    tap("s1", t_s[0][:, 0:2], [128, 2], [(SV, v_xq)], F32)

    if nseg < 3:
        return _finish(kb, nc)

    AR.reset(MK_B)
    kb.barrier("rst_ln")
    QF = AR.alloc("QF", [128, 4 * TOK], BF16)
    KF = AR.alloc("KF", [128, 4 * TOK], BF16)
    KT = AR.alloc("KT", [128, NCH * CH0], BF16)
    VA = AR.alloc("VA", [128, NCH * H0 * 65], BF16)
    GT = AR.alloc("GT", [128, NCH * CH0], BF16)
    MK_WR = AR.mark()
    WRA = AR.alloc("WRA", [128, 8 * CH0], BF16)
    WRB = AR.alloc("WRB", [128, 8 * CH0], BF16)
    WRC = AR.alloc("WRC", [128, 8 * CH0], BF16)

    v_vone = kb.op("vector", lambda e: e.memset(
        ap3(VA, 0, 128, 64, [[H0 * 65, NCH], [65, H0]]), 1.0), inc=SV, dbg="vone")

    for cc in range(8):
        k_wq = LD.dma(WRA[:, cc * CH0:(cc + 1) * CH0],
                      d_wq.ap()[cc * 128:(cc + 1) * 128, :],
                      waits=([k_cast] if cc == 0 else []), dbg="ldwq", group=(cc > 0))
    for cc in range(8):
        k_wk = LD.dma(WRB[:, cc * CH0:(cc + 1) * CH0],
                      d_wk.ap()[cc * 128:(cc + 1) * 128, :], dbg="ldwk", group=(cc > 0))

    def phi_evac(bank, dst_ap, w_extra, dbg):
        if not KPHI:
            v3 = kb.op("vector", lambda e, b=bank, d=dst_ap: e.tensor_copy(
                d, pb[b][0:128, 0:512]), waits=w_extra, inc=SV, dbg=dbg + "cp")
            return v3
        v1 = kb.op("vector", lambda e, b=bank: e.tensor_scalar_min(
            PH1[:, 0:512], pb[b][0:128, 0:512], 0.0), waits=w_extra, inc=SV, dbg=dbg + "a")
        v2 = kb.op("scalar", lambda e: e.activation(PH2[:, 0:512], PH1[:, 0:512], AF.Exp),
                   waits=[(SV, v1)], inc=SS, dbg=dbg + "b")
        v3 = kb.op("vector", lambda e, b=bank, d=dst_ap: e.scalar_tensor_tensor(
            d, pb[b][0:128, 0:512], 0.0, PH2[:, 0:512], ALU.max, ALU.add),
            waits=[(SS, v2)], inc=SV, dbg=dbg + "c")
        return v3

    bq = 0
    for (wring, k_w, dst, nm) in ([(WRA, k_wq, QF, "q"), (WRB, k_wk, KF, "k")][:SUB3]):
        for cc in range(4):
            for tb in range(4):
                if cc * 4 + tb >= KGRP:
                    continue
                bank = 0 if KBANK0 else bq % 4
                bq += 1
                for dk in range(8):
                    kb.mm(pb[bank][0:128, 0:512],
                          wring[:, dk * CH0 + cc * 128:dk * CH0 + (cc + 1) * 128],
                          XN1[:, dk * TOK + tb * 512:dk * TOK + (tb + 1) * 512],
                          dk == 0, dk == 7,
                          waits=(([k_w, (SV, v_xn)] + bw(bank)) if dk == 0 else []),
                          inc=(ST if (KNONOP and dk == 7) else None),
                          dbg=f"{nm}{cc}{tb}")
                v_mmx = ST.v
                v3 = phi_evac(bank, dst[:, cc * TOK + tb * 512:cc * TOK + (tb + 1) * 512],
                              [(ST, v_mmx)], f"{nm}e{cc}{tb}")
                brel(bank, SV, v3)
    v_guard = SV.v
    for cc in range(8):
        k_wv = LD.dma(WRA[:, cc * CH0:(cc + 1) * CH0],
                      d_wv.ap()[cc * 128:(cc + 1) * 128, :],
                      waits=([(SV, v_guard)] if cc == 0 else []), dbg="ldwv", group=(cc > 0))
    for cc in range(8):
        k_wg = LD.dma(WRC[:, cc * CH0:(cc + 1) * CH0],
                      d_wg.ap()[cc * 128:(cc + 1) * 128, :], dbg="ldwg", group=(cc > 0))

    for tc in (range(NCH) if SUB3 >= 3 else []):
        bank = bq % 4; bq += 1
        for dk in range(8):
            kb.mm(pb[bank][0:128, 0:512],
                  XN1[:, dk * TOK + tc * 128:dk * TOK + (tc + 1) * 128],
                  WRB[:, dk * CH0:(dk + 1) * CH0],
                  dk == 0, dk == 7, waits=(bw(bank) if dk == 0 else []), dbg=f"kt{tc}")
        v_mmx = ST.v
        v3 = phi_evac(bank, KT[:, tc * CH0:(tc + 1) * CH0], [(ST, v_mmx)], f"kte{tc}")
        brel(bank, SV, v3)

        bank = bq % 4; bq += 1
        for dk in range(8):
            kb.mm(pb[bank][0:128, 0:512],
                  XN1[:, dk * TOK + tc * 128:dk * TOK + (tc + 1) * 128],
                  WRA[:, dk * CH0:(dk + 1) * CH0],
                  dk == 0, dk == 7,
                  waits=(([k_wv] + bw(bank)) if dk == 0 else []), dbg=f"v{tc}")
        v_mmx = ST.v
        v3 = kb.op("vector", lambda e, b=bank, tc=tc: e.tensor_copy(
            ap3(VA, 0, 128, tc * H0 * 65, [[65, H0], [1, 64]]), pb[b][0:128, 0:512]),
            waits=[(ST, v_mmx), (SV, v_vone)], inc=SV, dbg=f"ve{tc}")
        brel(bank, SV, v3)

        bank = bq % 4; bq += 1
        for dk in range(8):
            kb.mm(pb[bank][0:128, 0:512],
                  XN1[:, dk * TOK + tc * 128:dk * TOK + (tc + 1) * 128],
                  WRC[:, dk * CH0:(dk + 1) * CH0],
                  dk == 0, dk == 7,
                  waits=(([k_wg] + bw(bank)) if dk == 0 else []), dbg=f"g{tc}")
        v_mmx = ST.v
        v1 = kb.op("vector", lambda e, b=bank: e.tensor_add(
            PH1[:, 0:512], pb[b][0:128, 0:512], t_brep[:, :]),
            waits=[(ST, v_mmx), (SV, v_g_brep)], inc=SV, dbg=f"ge{tc}a")
        brel(bank, SV, v1)
        v2 = kb.op("scalar", lambda e, tc=tc: e.activation(
            GT[:, tc * CH0:(tc + 1) * CH0], PH1[:, 0:512], AF.Sigmoid),
            waits=[(SV, v1)], inc=SS, dbg=f"ge{tc}b")
        kb.wait_only("vector", [(SS, v2)], dbg=f"ge{tc}c")
    v_proj = SV.v
    v_proj_s = SS.v
    tap("qf", QF[:, 0:4 * TOK], [128, 4 * TOK], [(SV, v_proj)], BF16)
    tap("kf", KF[:, 0:4 * TOK], [128, 4 * TOK], [(SV, v_proj)], BF16)
    tap("kt", KT[:, 0:NCH * CH0], [128, NCH * CH0], [(SV, v_proj)], BF16)
    tap("va", VA[:, 0:NCH * H0 * 65], [128, NCH * H0 * 65], [(SV, v_proj)], BF16)
    tap("gt", GT[:, 0:NCH * CH0], [128, NCH * CH0], [(SS, v_proj_s)], BF16)

    if nseg < 4:
        return _finish(kb, nc)

    AR.reset(MK_WR)
    kb.barrier("rst_wr")
    MK_SCAN = AR.mark()
    QG = AR.alloc("QG", [128, 512], BF16)
    KG = AR.alloc("KG", [128, 512], BF16)
    QZE = AR.alloc("QZE", [128, 512], BF16)
    QZO = AR.alloc("QZO", [128, 512], BF16)
    QGE = AR.alloc("QGE", [128, 512], BF16)
    QGO = AR.alloc("QGO", [128, 512], BF16)
    MATB = AR.alloc("MATB", [128, H0 * 128], BF16)
    SSUM = AR.alloc("SSUM", [128, 4 * 65], F32)
    SEA = AR.alloc("SEA", [128, 4 * 65], F32)
    SEB = AR.alloc("SEB", [128, 4 * 65], F32)
    SRV = AR.alloc("SRV", [128, 4 * 65], F32)
    DENB = AR.alloc("DENB", [128, 8], F32)
    DENR = AR.alloc("DENR", [128, 8], F32)
    YG = AR.alloc("YG", [128, CH0], BF16)
    TMPY = AR.alloc("TMPY", [128, CH0], F32)
    SE16 = AR.alloc("SE16", [128, 4 * 65], BF16)

    last_state_mm = [0]

    def state_mms(tc, w_extra, dbg):
        v_kg = kb.op("vector", lambda e, tc=tc: e.tensor_mul(
            KG[:, 0:512], KT[:, tc * CH0:(tc + 1) * CH0], gammaK[:, :]),
            waits=list(w_extra) + [(SS, v_s_gK), (ST, last_state_mm[0]), (SV, v_proj)],
            inc=SV, dbg=dbg + "kg")
        for h in range(H0):
            pp = (h % 2) * 64
            bank = 4 + h % 2
            cg = (h // 2) * 65
            kb.mm(pb[bank][pp:pp + 64, cg:cg + 65],
                  KG[:, h * 64:(h + 1) * 64],
                  VA[:, tc * H0 * 65 + h * 65:tc * H0 * 65 + (h + 1) * 65],
                  h < 2, h >= 6,
                  waits=(([(SV, v_kg)] + bw(4, 5)) if h == 0 else []),
                  dbg=dbg + f"s{h}")
        v_mm = ST.v
        last_state_mm[0] = v_mm
        kb.op("vector", lambda e: e.tensor_copy(SSUM[0:64, 0:260], pb[4][0:64, 0:260]),
              waits=[(ST, v_mm)], dbg=dbg + "e0")
        v_ev = kb.op("vector", lambda e: e.tensor_copy(SSUM[64:128, 0:260], pb[5][64:128, 0:260]),
                     inc=SV, dbg=dbg + "e1")
        brel(4, SV, v_ev); brel(5, SV, v_ev)
        return v_ev

    v_z = kb.op("vector", lambda e: e.memset(SEA[:, 0:260], 0.0), inc=SV, dbg="seaz")
    cur, nxt = SEA, SEB
    v_ch = v_z
    for tc in range(NCH):
        v_ev = state_mms(tc, [], f"c1_{tc}")
        kb.op("vector", lambda e, c=cur, n=nxt: e.tensor_mul(
            n[:, 0:260], c[:, 0:260], gammaC[:, 0:260]),
            waits=[(SV, max(v_ev, v_ch)), (SV, v_v_gC)], dbg=f"ch{tc}a")
        v_ch = kb.op("vector", lambda e, n=nxt: e.tensor_add(
            n[:, 0:260], n[:, 0:260], SSUM[:, 0:260]), inc=SV, dbg=f"ch{tc}b")
        cur, nxt = nxt, cur
    v_ctr = kb.op("vector", lambda e, c=cur: e.tensor_scalar_mul(
        SRV[:, 0:260], c[:, 0:260], fl_contrib[:, 0:1]),
        waits=[(SV, max(v_ch, v_g_fl))], inc=SV, dbg="ctr")
    k_dst = GD.dma(c_st_i.ap(), SRV[0:128, 0:260], waits=[(SV, v_ctr)], dbg="stw")
    v_ccs = kb.op("gpsimd", lambda e: e.collective_compute(
        "AllReduce", ALU.add, replica_groups=PAIRS,
        ins=[c_st_i.ap().opt()], outs=[c_st_o.ap().opt()]),
        waits=[k_dst], inc=SC, dbg="ARst")
    k_dsr = GD.dma(SRV[0:128, 0:260], c_st_o.ap(), waits=[(SC, v_ccs)], dbg="str")
    v_se0 = kb.op("vector", lambda e: e.tensor_scalar_mul(
        SEA[:, 0:260], SRV[:, 0:260], fl_use[:, 0:1]),
        waits=[k_dsr, (SV, v_g_fl)], inc=SV, dbg="se0")

    cur, nxt = SEA, SEB
    v_chain = v_se0
    v_num_prev = 0
    k_tp = (None, 0)
    for tc in range(NCH):
        kb.op("vector", lambda e, tc=tc: e.tensor_scalar_mul(
            QZE[:, 0:512], ap3(QF, 0, 128, tc * 128, [[TOK, 4], [1, 128]]), PONE2[:, 0:1]),
            waits=[(SV, v_proj)], dbg=f"qze{tc}")
        v_qz = kb.op("vector", lambda e, tc=tc: e.tensor_scalar_mul(
            QZO[:, 0:512], ap3(QF, 0, 128, tc * 128, [[TOK, 4], [1, 128]]), PONE2[:, 1:2]),
            inc=SV, dbg=f"qzo{tc}")
        for h in range(H0):
            bank = h // 4
            cc = h // 2
            QZ = QZE if h % 2 == 0 else QZO
            kb.mm(pb[bank][0:128, (h % 4) * 128:(h % 4) * 128 + 128],
                  KF[0:128, cc * TOK + tc * 128:cc * TOK + (tc + 1) * 128],
                  QZ[0:128, cc * 128:(cc + 1) * 128],
                  True, True,
                  waits=(([(SV, v_qz)] + bw(bank)) if h % 4 == 0 else []),
                  dbg=f"at{tc}_{h}")
        v_at = ST.v
        kb.op("vector", lambda e: e.tensor_mul(
            MATB[:, 0:512], pb[0][0:128, 0:512], mask_all[:, 0:512]),
            waits=[(ST, v_at), (SV, v_masks)], dbg=f"ms{tc}0")
        v_msk = kb.op("vector", lambda e: e.tensor_mul(
            MATB[:, 512:1024], pb[1][0:128, 0:512], mask_all[:, 512:1024]),
            inc=SV, dbg=f"ms{tc}1")
        brel(0, SV, v_msk); brel(1, SV, v_msk)
        if KSC4 < 3:
            continue
        kb.op("vector", lambda e, tc=tc: e.tensor_mul(
            QG[:, 0:512], ap3(QF, 0, 128, tc * 128, [[TOK, 4], [1, 128]]), gammaQ[:, :]),
            waits=[(SS, v_s_gQ), (ST, v_num_prev)], dbg=f"qg{tc}")
        kb.op("vector", lambda e: e.tensor_scalar_mul(
            QGE[:, 0:512], QG[:, 0:512], PONE2[:, 0:1]), dbg=f"qge{tc}")
        v_qg = kb.op("vector", lambda e: e.tensor_scalar_mul(
            QGO[:, 0:512], QG[:, 0:512], PONE2[:, 1:2]), inc=SV, dbg=f"qgo{tc}")
        v_se16 = kb.op("vector", lambda e, c=cur: e.tensor_copy(SE16[:, 0:260], c[:, 0:260]),
                       waits=[(SV, v_chain), (ST, v_num_prev)], inc=SV, dbg=f"se16_{tc}")
        for h in range(H0):
            bank = 2 + h // 4
            kb.mm(pb[bank][0:128, (h % 4) * 65:(h % 4) * 65 + 65],
                  MATB[:, h * 128:(h + 1) * 128],
                  VA[:, tc * H0 * 65 + h * 65:tc * H0 * 65 + (h + 1) * 65],
                  h % 4 == 0, False,
                  waits=(([(SV, v_msk)] + bw(bank)) if h % 4 == 0 else []), dbg=f"in{tc}_{h}")
        for h in range(H0):
            bank = 2 + h // 4
            QGZ = QGE if h % 2 == 0 else QGO
            kb.mm(pb[bank][0:128, (h % 4) * 65:(h % 4) * 65 + 65],
                  QGZ[0:128, (h // 2) * 128:(h // 2 + 1) * 128],
                  SE16[0:128, (h // 2) * 65:(h // 2) * 65 + 65],
                  False, h % 4 == 3,
                  waits=([(SV, max(v_qg, v_se16))] if h == 0 else []), dbg=f"ir{tc}_{h}")
        v_num = ST.v
        v_num_prev = v_num
        if KSC4 < 4:
            brel(2, ST, v_num); brel(3, ST, v_num)
            continue
        for bi in range(2):
            kb.op("vector", lambda e, b=2 + bi, bi=bi: e.tensor_scalar_add(
                DENB[:, bi * 4:bi * 4 + 4],
                bass.AP(pb[b], 64, [[512, 128], [65, 4]]), EPS_ATT),
                waits=[(ST, v_num)], dbg=f"den{tc}{bi}")
        v_den = kb.op("vector", lambda e: e.reciprocal(DENR[:, 0:8], DENB[:, 0:8]),
                      inc=SV, dbg=f"denr{tc}")
        for bi in range(2):
            kb.op("vector", lambda e, b=2 + bi, bi=bi: e.tensor_mul(
                TMPY[:, bi * 256:bi * 256 + 256],
                bass.AP(pb[b], 0, [[512, 128], [65, 4], [1, 64]]),
                bass.AP(DENR, bi * 4, [[8, 128], [1, 4], [0, 64]])),
                waits=[(SV, v_den)], dbg=f"y{tc}{bi}")
        v_y = SV.v
        brel(2, SV, v_y); brel(3, SV, v_y)
        v_yg = kb.op("vector", lambda e, tc=tc: e.tensor_mul(
            YG[:, 0:512], TMPY[:, 0:512], GT[:, tc * CH0:(tc + 1) * CH0]),
            waits=[(SV, v_y), (SS, v_proj_s), k_tp], inc=SV, dbg=f"yg{tc}")
        if KSC4 < 5:
            continue
        for cc in range(4):
            k_tp = TP.dma(YGT[:, cc * TOK + tc * 128:cc * TOK + (tc + 1) * 128],
                          YG[:, cc * 128:(cc + 1) * 128],
                          waits=([(SV, v_yg)] if cc == 0 else []), transpose=True,
                          dbg=f"tp{tc}{cc}", group=(cc > 0))
        v_ev = state_mms(tc, [], f"c2_{tc}")
        kb.op("vector", lambda e, c=cur, n=nxt: e.tensor_mul(
            n[:, 0:260], c[:, 0:260], gammaC[:, 0:260]), waits=[(SV, v_ev)], dbg=f"cb{tc}a")
        v_chain = kb.op("vector", lambda e, n=nxt: e.tensor_add(
            n[:, 0:260], n[:, 0:260], SSUM[:, 0:260]), inc=SV, dbg=f"cb{tc}b")
        cur, nxt = nxt, cur
    v_scan = SV.v
    tap("ygt", YGT[:, 0:4 * TOK], [128, 4 * TOK], [k_tp, (SV, v_scan)], BF16)

    if nseg < 5:
        return _finish(kb, nc)

    AR.reset(MK_LN)
    kb.barrier("rst_s5")
    MK_S5 = AR.mark()
    GFF = AR.alloc("GFF", [128, 8 * TOK], BF16)
    MK_S5B = AR.mark()
    WQ0 = AR.alloc("WQ0", [128, FF], F32)
    WQ1 = AR.alloc("WQ1", [128, FF], F32)
    SWT = AR.alloc("SWT", [128, 96], F32)
    SWR = AR.alloc("SWR", [128, 96], F32)
    WQB = AR.alloc("WQB", [128, FF], BF16)
    RND = AR.alloc("RND", [128, FF], F32)
    SQA = AR.alloc("SQA", [128, FF], F32)
    SQB = AR.alloc("SQB", [128, FF], F32)
    I32B = AR.alloc("I32B", [128, FF], I32)
    UBB = AR.alloc("UBB", [128, TOK], BF16)
    TBB = AR.alloc("TBB", [128, TOK], BF16)
    TB5 = AR.alloc("TB5", [128, 512], BF16)
    OB = AR.alloc("OB", [128, 512], BF16)
    WUT = AR.alloc("WUT", [128, 8 * 128], BF16)
    WUT2 = AR.alloc("WUT2", [128, 8 * 128], BF16)
    WTT = AR.alloc("WTT", [128, FF], BF16)
    t_mx = AR.alloc("t_mx", [128, 4], F32)
    t_mxs = AR.alloc("t_mxs", [128, 1], F32)
    t_c1 = AR.alloc("t_c1", [128, 1], F32)

    kb.op("vector", lambda e: e.memset(t_mx[:, :], 0.0), dbg="mx0")

    def lpbq(w_in, d_out, nch, dbg, prev_tok=None):
        wid = w_in.shape[1]
        nblk = wid // 32
        toks = []
        v_done = {}
        for cch in range(nch):
            wq_t = WQ0 if cch % 2 == 0 else WQ1
            k_l = LD.dma(wq_t[:, 0:wid], w_in.ap()[cch * 128:(cch + 1) * 128, :],
                         waits=[(SV, v_done.get(cch - 2, SV.v if cch < 2 else 0))],
                         dbg=dbg + f"l{cch}")
            kb.op("vector", lambda e, w=wq_t, nb=nblk, wid=wid: e.tensor_reduce(
                SWT[:, 0:nb], ap3(w, 0, 128, 0, [[32, nb], [1, 32]]), AX.X, ALU.max,
                apply_absolute_value=True), waits=[k_l], dbg=dbg + f"a{cch}")
            kb.op("vector", lambda e, nb=nblk: e.tensor_scalar(
                SWT[:, 0:nb], SWT[:, 0:nb], 1.0 / 7.0, 1e-12, ALU.mult, ALU.max),
                dbg=dbg + f"s{cch}")
            kb.op("vector", lambda e, nb=nblk: e.reciprocal(SWR[:, 0:nb], SWT[:, 0:nb]),
                  dbg=dbg + f"r{cch}")
            kb.op("vector", lambda e, w=wq_t, nb=nblk, wid=wid: e.tensor_mul(
                SQA[:, 0:wid], w[:, 0:wid], ap3(SWR, 0, 128, 0, [[1, nb], [0, 32]])),
                dbg=dbg + f"d{cch}")
            kb.op("vector", lambda e, wid=wid: e.tensor_copy(I32B[:, 0:wid], SQA[:, 0:wid]),
                  dbg=dbg + f"i{cch}")
            kb.op("vector", lambda e, wid=wid: e.tensor_copy(SQB[:, 0:wid], I32B[:, 0:wid]),
                  dbg=dbg + f"j{cch}")
            kb.op("vector", lambda e, wid=wid: e.tensor_sub(
                RND[:, 0:wid], SQA[:, 0:wid], SQB[:, 0:wid]), dbg=dbg + f"j2{cch}")
            kb.op("vector", lambda e, wid=wid: e.scalar_tensor_tensor(
                SQA[:, 0:wid], RND[:, 0:wid], 0.5, SQB[:, 0:wid], ALU.is_gt, ALU.add),
                dbg=dbg + f"j3{cch}")
            kb.op("vector", lambda e, wid=wid: e.tensor_scalar(
                RND[:, 0:wid], RND[:, 0:wid], -0.5, -1.0, ALU.is_lt, ALU.mult),
                dbg=dbg + f"j4{cch}")
            kb.op("vector", lambda e, wid=wid: e.tensor_add(
                SQA[:, 0:wid], SQA[:, 0:wid], RND[:, 0:wid]), dbg=dbg + f"j5{cch}")
            v_q = kb.op("vector", lambda e, nb=nblk, wid=wid: e.tensor_mul(
                WQB[:, 0:wid], SQA[:, 0:wid], ap3(SWT, 0, 128, 0, [[1, nb], [0, 32]])),
                waits=([toks[-1]] if toks else ([prev_tok] if prev_tok else [])),
                dbg=dbg + f"q{cch}")
            v_done[cch] = v_q
            toks.append(GD.dma(d_out.ap()[cch * 128:(cch + 1) * 128, :], WQB[:, 0:wid],
                               waits=[(SV, v_q)], dbg=dbg + f"w{cch}"))
        return toks

    tok_up = lpbq(wup, d_wupq, 24, "lu")
    tok_gt = lpbq(wgate, d_wgtq, 24, "lg", prev_tok=tok_up[-1])
    tok_dn = lpbq(wdown, d_wdnq, 8, "ld", prev_tok=tok_gt[-1])

    def pretrans(src_d, dst_d, R, C, wtok, dbg, prev=None):
        ks = prev
        for j in range(C // 128):
            k_t = TP.dma(WTT[:, 0:R], src_d.ap()[:, j * 128:(j + 1) * 128],
                         waits=[wtok] + ([ks] if ks else []), transpose=True,
                         dbg=dbg + f"t{j}")
            ks = LD.dma(dst_d.ap()[j * 128:(j + 1) * 128, :], WTT[:, 0:R],
                        waits=[k_t], dbg=dbg + f"s{j}")
        return ks

    tok_upT = pretrans(d_wupq, d_wupqT, FF, D, tok_up[-1], "ptu")
    tok_gtT = pretrans(d_wgtq, d_wgtqT, FF, D, tok_gt[-1], "ptg", prev=tok_upT)
    tok_dnT = pretrans(d_wdnq, d_wdnqT, D, FF, tok_dn[-1], "ptd", prev=tok_gtT)

    def ffn_proj(dT, spill, mx_col, wtok, dbg, prev_sp=None):
        sp_toks = []
        v_last_mx = SV.v
        for cch in range(24):
            wr = WUT if cch % 2 == 0 else WUT2
            kws = None
            for dk in range(8):
                kws = LD.dma(wr[:, dk * 128:(dk + 1) * 128],
                             dT.ap()[dk * 128:dk * 128 + 128, cch * 128:(cch + 1) * 128],
                             waits=([wtok, (ST, ST.v)] if dk == 0 else []),
                             dbg=dbg + f"w{cch}", group=(dk > 0))
            for tb in range(4):
                bank = (cch * 4 + tb) % 4
                for dk in range(8):
                    kb.mm(pb[bank][0:128, 0:512],
                          wr[:, dk * 128:(dk + 1) * 128],
                          XN2[:, dk * TOK + tb * 512:dk * TOK + (tb + 1) * 512],
                          dk == 0, dk == 7,
                          waits=(([kws, (SV, v_xq)] + bw(bank)) if dk == 0 else []),
                          dbg=dbg + f"m{cch}{tb}")
                v_mmu = ST.v
                v_ev = kb.op("vector", lambda e, b=bank, tb=tb: e.tensor_copy(
                    UBB[:, tb * 512:(tb + 1) * 512], pb[b][0:128, 0:512]),
                    waits=[(ST, v_mmu)] +
                          ([sp_toks[cch - 1], (SV, v_last_mx)] if (tb == 0 and cch > 0) else []) +
                          ([prev_sp, (SV, v_last_mx)] if (tb == 0 and cch == 0 and prev_sp) else []),
                    dbg=dbg + f"e{cch}{tb}")
                brel(bank, SV, v_ev)
            sp_toks.append(GD.dma(spill.ap()[cch * 128:(cch + 1) * 128, :], UBB[:, :],
                                  waits=[(SV, v_ev)], dbg=dbg + f"sp{cch}"))
            kb.op("vector", lambda e: e.tensor_reduce(
                t_mxs[:, 0:1], UBB[:, :], AX.X, ALU.max, apply_absolute_value=True),
                dbg=dbg + f"mr{cch}")
            v_last_mx = kb.op("vector", lambda e, c=mx_col: e.tensor_max(
                t_mx[:, c:c + 1], t_mx[:, c:c + 1], t_mxs[:, 0:1]), dbg=dbg + f"mx{cch}")
        return sp_toks

    tok_u = ffn_proj(d_wupqT, d_u, 0, tok_upT, "U")
    tok_g = ffn_proj(d_wgtqT, d_g, 1, tok_gtT, "G", prev_sp=tok_u[-1])
    v_maxug = SV.v

    v_p1 = preduce(t_mx[:, 0:1], t_arstage[0:1, 0:1], 6, [(SV, v_maxug)], ALU.max, "p_u")
    v_p2 = preduce(t_mx[:, 1:2], t_arstage[0:1, 1:2], 6, [(SV, v_p1)], ALU.max, "p_g")
    k_a2 = GD.dma(c_ar_i[1].ap(), t_arstage[:, :], waits=[(SV, v_p2)], dbg="ar2w")
    v_cc2 = kb.op("gpsimd", lambda e: e.collective_compute(
        "AllReduce", ALU.max, replica_groups=ALL8,
        ins=[c_ar_i[1].ap().opt()], outs=[c_ar_o[1].ap().opt()]),
        waits=[k_a2], inc=SC, dbg="AR2")
    k_a2r = GD.dma(t_ar[:, :], c_ar_o[1].ap(), waits=[(SC, v_cc2)], dbg="ar2r")

    def mk_scale(col, tile, wtok, dbg):
        v_b = bcast_rows(t_ar[0:1, col:col + 1], tile[:, 0:1], 1, 6, [wtok], dbg + "b")
        kb.op("vector", lambda e, t=tile: e.tensor_scalar(
            t[:, 0:1], t[:, 0:1], 1.0 / QMAX, 1e-12, ALU.mult, ALU.max),
            waits=[(SV, v_b)], dbg=dbg + "s")
        kb.op("vector", lambda e, t=tile: e.reciprocal(t[:, 1:2], t[:, 0:1]), dbg=dbg + "r")
        kb.op("vector", lambda e, t=tile: e.tensor_copy(t[:, 2:3], t[:, 0:1]), dbg=dbg + "c")
        return kb.op("vector", lambda e, t=tile: e.memset(t[:, 3:4], 0.0), dbg=dbg + "z")

    v_s2 = mk_scale(0, t_s[1], k_a2r, "s2")
    v_s3 = mk_scale(1, t_s[2], k_a2r, "s3")

    v_c34 = kb.op("vector", lambda e: e.tensor_scalar_mul(
        t_c1[:, 0:1], t_s[2][:, 0:1], 1.0 / 65536.0), waits=[(SV, v_s3)], dbg="c34")
    v_t4last = 0
    t4_toks = []
    for cch in range(24):
        k_g = LD.dma(UBB[:, :], d_g.ap()[cch * 128:(cch + 1) * 128, :],
                     waits=[tok_g[cch], (SV, v_t4last)] +
                           ([tok_g[-1], (SV, SV.v)] if cch == 0 else []), dbg=f"Cg{cch}")
        kb.op("vector", lambda e: e.tensor_scalar_mul(SQA[:, 0:TOK], UBB[:, :], t_s[2][:, 1:2]),
              waits=[k_g, (SV, v_s3)], dbg=f"Ca{cch}")
        kb.op("vector", lambda e: e.tensor_copy(I32B[:, 0:TOK], SQA[:, 0:TOK]), dbg=f"Cb{cch}")
        v_r3 = kb.op("vector", lambda e: e.tensor_copy(SQA[:, 0:TOK], I32B[:, 0:TOK]), dbg=f"Cc{cch}")
        v_sg = kb.op("scalar", lambda e: e.activation(
            SQB[:, 0:TOK], SQA[:, 0:TOK], AF.Sigmoid, scale=t_s[2][:, 2:3], bias=t_s[2][:, 3:4]),
            waits=[(SV, v_r3)], dbg=f"Cs{cch}")
        kb.op("vector", lambda e: e.tensor_scalar_mul(SQB[:, 0:TOK], SQB[:, 0:TOK], 65536.0),
              waits=[(SS, v_sg)], dbg=f"Cd{cch}")
        kb.op("vector", lambda e: e.tensor_copy(I32B[:, 0:TOK], SQB[:, 0:TOK]), dbg=f"Ce{cch}")
        kb.op("vector", lambda e: e.tensor_copy(SQB[:, 0:TOK], I32B[:, 0:TOK]), dbg=f"Cf{cch}")
        kb.op("vector", lambda e: e.tensor_scalar_min(SQB[:, 0:TOK], SQB[:, 0:TOK], 65535.0),
              dbg=f"Cg2{cch}")
        v_t4 = kb.op("vector", lambda e: e.scalar_tensor_tensor(
            TBB[:, :], SQA[:, 0:TOK], t_c1[:, 0:1], SQB[:, 0:TOK], ALU.mult, ALU.mult),
            waits=[(SV, v_c34)] + ([t4_toks[cch - 1]] if cch > 0 else []), dbg=f"Ct{cch}")
        t4_toks.append(GD.dma(d_t4.ap()[cch * 128:(cch + 1) * 128, :], TBB[:, :],
                              waits=[(SV, v_t4)], dbg=f"Cw{cch}"))
        kb.op("vector", lambda e: e.tensor_reduce(
            t_mxs[:, 0:1], TBB[:, :], AX.X, ALU.max, apply_absolute_value=True), dbg=f"Cm{cch}")
        v_t4last = kb.op("vector", lambda e: e.tensor_max(
            t_mx[:, 2:3], t_mx[:, 2:3], t_mxs[:, 0:1]), dbg=f"Cx{cch}")

    v_p3 = preduce(t_mx[:, 2:3], t_arstage[0:1, 0:1], 6, [(SV, v_t4last)], ALU.max, "p_t4")
    k_a3 = GD.dma(c_ar_i[2].ap(), t_arstage[:, :], waits=[(SV, v_p3)], dbg="ar3w")
    v_cc3 = kb.op("gpsimd", lambda e: e.collective_compute(
        "AllReduce", ALU.max, replica_groups=ALL8,
        ins=[c_ar_i[2].ap().opt()], outs=[c_ar_o[2].ap().opt()]),
        waits=[k_a3], inc=SC, dbg="AR3")
    k_a3r = GD.dma(t_ar[:, :], c_ar_o[2].ap(), waits=[(SC, v_cc3)], dbg="ar3r")
    v_s4 = mk_scale(0, t_s[3], k_a3r, "s4")

    v_c24 = kb.op("vector", lambda e: e.tensor_mul(
        t_c1[:, 0:1], t_s[1][:, 0:1], t_s[3][:, 0:1]),
        waits=[(SV, max(v_s2, v_s4))], dbg="c24")
    v_t5last = 0
    t5_toks = []
    for cch in range(24):
        k_u = LD.dma(UBB[:, :], d_u.ap()[cch * 128:(cch + 1) * 128, :],
                     waits=[tok_u[cch], (SV, v_t5last)] +
                           ([(SV, SV.v)] if cch == 0 else []), dbg=f"Du{cch}")
        k_t = LD.dma(TBB[:, :], d_t4.ap()[cch * 128:(cch + 1) * 128, :],
                     waits=[t4_toks[cch], (SV, v_t5last)] +
                           ([t5_toks[cch - 1]] if cch > 0 else [(SV, SV.v), t4_toks[-1]]),
                     dbg=f"Dt{cch}")
        kb.op("vector", lambda e: e.tensor_scalar_mul(SQA[:, 0:TOK], UBB[:, :], t_s[1][:, 1:2]),
              waits=[k_u, (SV, v_c24)], dbg=f"Da{cch}")
        kb.op("vector", lambda e: e.tensor_copy(I32B[:, 0:TOK], SQA[:, 0:TOK]), dbg=f"Db{cch}")
        kb.op("vector", lambda e: e.tensor_copy(SQA[:, 0:TOK], I32B[:, 0:TOK]), dbg=f"Dc{cch}")
        kb.op("vector", lambda e: e.tensor_scalar_mul(SQB[:, 0:TOK], TBB[:, :], t_s[3][:, 1:2]),
              waits=[k_t], dbg=f"Dd{cch}")
        kb.op("vector", lambda e: e.tensor_copy(I32B[:, 0:TOK], SQB[:, 0:TOK]), dbg=f"De{cch}")
        kb.op("vector", lambda e: e.tensor_copy(SQB[:, 0:TOK], I32B[:, 0:TOK]), dbg=f"Df{cch}")
        v_t5 = kb.op("vector", lambda e: e.scalar_tensor_tensor(
            TBB[:, :], SQA[:, 0:TOK], t_c1[:, 0:1], SQB[:, 0:TOK], ALU.mult, ALU.mult), dbg=f"Dg{cch}")
        t5_toks.append(GD.dma(d_t5.ap()[cch * 128:(cch + 1) * 128, :], TBB[:, :],
                              waits=[(SV, v_t5)], dbg=f"Dw{cch}"))
        kb.op("vector", lambda e: e.tensor_reduce(
            t_mxs[:, 0:1], TBB[:, :], AX.X, ALU.max, apply_absolute_value=True), dbg=f"Dm{cch}")
        v_t5last = kb.op("vector", lambda e: e.tensor_max(
            t_mx[:, 3:4], t_mx[:, 3:4], t_mxs[:, 0:1]), dbg=f"Dx{cch}")

    v_p4 = preduce(t_mx[:, 3:4], t_arstage[0:1, 0:1], 6, [(SV, v_t5last)], ALU.max, "p_t5")
    k_a4 = GD.dma(c_ar_i[3].ap(), t_arstage[:, :], waits=[(SV, v_p4)], dbg="ar4w")
    v_cc4 = kb.op("gpsimd", lambda e: e.collective_compute(
        "AllReduce", ALU.max, replica_groups=ALL8,
        ins=[c_ar_i[3].ap().opt()], outs=[c_ar_o[3].ap().opt()]),
        waits=[k_a4], inc=SC, dbg="AR4")
    k_a4r = GD.dma(t_ar[:, :], c_ar_o[3].ap(), waits=[(SC, v_cc4)], dbg="ar4r")
    v_s5 = mk_scale(0, t_s[4], k_a4r, "s5")

    for tb in range(4):
        v_sil_prev = SS.v
        for cch in range(24):
            wr = WUT if cch % 2 == 0 else WUT2
            k_w = LD.dma(wr[:, :], d_wdnqT.ap()[cch * 128:(cch + 1) * 128, :],
                         waits=[tok_dnT, (ST, ST.v)], dbg=f"Ew{tb}_{cch}")
            k_5 = LD.dma(TB5[:, :],
                         d_t5.ap()[cch * 128:(cch + 1) * 128, tb * 512:(tb + 1) * 512],
                         waits=[t5_toks[cch], (SV, SV.v)], dbg=f"E5{tb}_{cch}")
            kb.op("vector", lambda e: e.tensor_scalar_mul(
                SQA[:, 0:512], TB5[:, :], t_s[4][:, 1:2]),
                waits=[k_5, (SV, v_s5), (ST, ST.v)], dbg=f"Ea{tb}_{cch}")
            kb.op("vector", lambda e: e.tensor_copy(I32B[:, 0:512], SQA[:, 0:512]),
                  dbg=f"Eb{tb}_{cch}")
            kb.op("vector", lambda e: e.tensor_copy(SQA[:, 0:512], I32B[:, 0:512]),
                  dbg=f"Ec{tb}_{cch}")
            v_o = kb.op("vector", lambda e: e.tensor_scalar_mul(
                OB[:, :], SQA[:, 0:512], t_s[4][:, 0:1]), dbg=f"Ed{tb}_{cch}")
            for dk in range(8):
                w_ = []
                if dk == 0:
                    w_ = [k_w, (SV, v_o), (SS, v_sil_prev)]
                if cch == 0:
                    w_ = w_ + bw(dk)
                kb.mm(pb[dk][0:128, 0:512],
                      wr[:, dk * 128:(dk + 1) * 128], OB[:, :],
                      cch == 0, cch == 23, waits=w_, dbg=f"Em{tb}_{cch}_{dk}")
            v_mme = ST.v
        for dk in range(8):
            v_sil = kb.op("scalar", lambda e, dk=dk, tb=tb: e.activation(
                GFF[:, dk * TOK + tb * 512:dk * TOK + (tb + 1) * 512],
                pb[dk][0:128, 0:512], AF.Silu),
                waits=[(ST, v_mme)], dbg=f"Es{tb}_{dk}")
            brel(dk, SS, v_sil)
    v_gff = SS.v
    tap("gff", GFF[:, 0:8 * TOK], [128, 8 * TOK], [(SS, v_gff)], BF16)
    if "uu" in taps:
        dt_u = nc.dram_tensor("tap_uu", [FF, TOK], BF16, kind="ExternalOutput")
        LD.dma(dt_u.ap(), d_u.ap(), waits=list(tok_u), dbg="tap_uu")
        kb.tapinfo["uu"] = ([FF, TOK], BF16)
    if "t4t" in taps:
        dt_t4 = nc.dram_tensor("tap_t4t", [FF, TOK], BF16, kind="ExternalOutput")
        LD.dma(dt_t4.ap(), d_t4.ap(), waits=list(t4_toks), dbg="tap_t4t")
        kb.tapinfo["t4t"] = ([FF, TOK], BF16)
    if "wupq" in taps:
        dt_wq = nc.dram_tensor("tap_wupq", [FF, D], BF16, kind="ExternalOutput")
        LD.dma(dt_wq.ap(), d_wupq.ap(), waits=[tok_up[-1]], dbg="tap_wupq")
        kb.tapinfo["wupq"] = ([FF, D], BF16)
    if "sc" in taps:
        dt_sc = nc.dram_tensor("tap_sc", [128, 20], F32, kind="ExternalOutput")
        v_scw = SV.v
        for i in range(5):
            LD.dma(dt_sc.ap()[:, 4 * i:4 * (i + 1)], t_s[i][:, 0:4],
                   waits=[(SV, v_scw)], dbg=f"tap_sc{i}", group=(i > 0))
        kb.tapinfo["sc"] = ([128, 20], F32)

    if nseg < 7:
        return _finish(kb, nc)

    AR.reset(MK_S5B)
    kb.barrier("rst_s7")
    WOT = AR.alloc("WOT", [128, 4 * 1024], BF16)
    XR0 = AR.alloc("XR0", [128, TOK], F32)
    XR1 = AR.alloc("XR1", [128, TOK], F32)
    X1B = AR.alloc("X1B", [128, 512], F32)
    k_wo = None
    for cc in range(4):
        k_wo = LD.dma(WOT[:, cc * 1024:(cc + 1) * 1024],
                      d_wo.ap()[cc * 128:(cc + 1) * 128, :],
                      waits=([k_cast] if cc == 0 else []), dbg="ldwo", group=(cc > 0))
    v_x1_done = {}
    x1_toks = []
    for dk in range(8):
        xr = XR0 if dk % 2 == 0 else XR1
        k_x = LD.dma(xr[:, :], xT.ap()[dk * 128:(dk + 1) * 128, :],
                     waits=[(SV, v_x1_done.get(dk - 2, 0))], dbg=f"x2{dk}")
        for tb in range(4):
            bank = (dk * 4 + tb) % 4
            for cc in range(4):
                kb.mm(pb[bank][0:128, 0:512],
                      WOT[:, cc * 1024 + dk * 128:cc * 1024 + (dk + 1) * 128],
                      YGT[:, cc * TOK + tb * 512:cc * TOK + (tb + 1) * 512],
                      cc == 0, cc == 3,
                      waits=(([k_wo, k_tp, (SV, v_scan)] + bw(bank)) if cc == 0 else []),
                      dbg=f"wo{dk}{tb}")
            v_mmo = ST.v
            kb.op("vector", lambda e, b=bank, dk=dk, tb=tb: e.scalar_tensor_tensor(
                X1B[:, 0:512], pb[b][0:128, 0:512], 1.0,
                GFF[:, dk * TOK + tb * 512:dk * TOK + (tb + 1) * 512],
                ALU.mult, ALU.mult),
                waits=[(ST, v_mmo), (SS, v_gff), k_x] +
                      ([x1_toks[-1]] if x1_toks else []), dbg=f"cmb{dk}{tb}a")
            v_x1 = kb.op("vector", lambda e, xr=xr, tb=tb: e.tensor_add(
                X1B[:, 0:512], X1B[:, 0:512], xr[:, tb * 512:(tb + 1) * 512]),
                dbg=f"cmb{dk}{tb}b")
            brel(bank, SV, v_x1)
            x1_toks.append(GD.dma(
                d_x1.ap()[dk * 128:(dk + 1) * 128, tb * 512:(tb + 1) * 512],
                X1B[:, 0:512], waits=[(SV, v_x1)], dbg=f"x1w{dk}{tb}"))
        v_x1_done[dk] = SV.v
    tok_x1 = x1_toks[-1]
    if "x1" in taps:
        dt_x1 = nc.dram_tensor("tap_x1", [D, TOK], F32, kind="ExternalOutput")
        LD.dma(dt_x1.ap(), d_x1.ap(), waits=list(x1_toks), dbg="tap_x1")
        kb.tapinfo["x1"] = ([D, TOK], F32)

    if nseg < 8:
        return _finish(kb, nc)

    AR.reset(MK_S5)
    kb.barrier("rst_s8")
    MK_S8 = AR.mark()
    XC2 = [AR.alloc(f"XC2{i}", [128, TOK], F32) for i in range(2)]
    SQ2b = AR.alloc("SQ2b", [128, TOK], F32)
    m2_row = AR.alloc("m2_row", [1, TOK], F32)
    s2_row = AR.alloc("s2_row", [1, TOK], F32)
    mn2_row = AR.alloc("mn2_row", [1, TOK], F32)
    var2_row = AR.alloc("var2_row", [1, TOK], F32)
    sd2_row = AR.alloc("sd2_row", [1, TOK], F32)
    rstd2_row = AR.alloc("rstd2_row", [1, TOK], F32)
    m2_rep = AR.alloc("m2_rep", [128, TOK], F32)
    r2_rep = AR.alloc("r2_rep", [128, TOK], F32)
    SGN = AR.alloc("SGN", [128, 8 * 1024], BF16)
    WS0 = AR.alloc("WS0", [128, 1024], F32)
    WS1 = AR.alloc("WS1", [128, 1024], F32)
    G1B = AR.alloc("G1B", [128, 512], BF16)

    v_t2 = 0
    v_t2A = {}
    for cc in range(8):
        xc = XC2[cc % 2]
        k_ld = LD.dma(xc[:, :], d_x1.ap()[cc * 128:(cc + 1) * 128, :],
                      waits=[(ST, v_t2A.get(cc - 2, 0))] + list(x1_toks), dbg=f"l2x{cc}")
        v_sq = kb.op("vector", lambda e, xc=xc: e.tensor_mul(SQ2b[:, :], xc[:, :], xc[:, :]),
                     waits=([k_ld, (ST, v_t2)] + (bw(0, 1, 2, 3, 4, 5, 6, 7) if cc == 0 else [])),
                     inc=SV, dbg=f"l2sq{cc}")
        for tb in range(4):
            kb.mm(pb[tb][0:1, 0:512], ones128[:, :], xc[:, tb * 512:(tb + 1) * 512],
                  cc == 0, cc == 7, waits=([(SV, v_sq)] if tb == 0 else []), dbg=f"l2m{cc}{tb}")
            kb.mm(pb[4 + tb][0:1, 0:512], ones128[:, :], SQ2b[:, tb * 512:(tb + 1) * 512],
                  cc == 0, cc == 7, dbg=f"l2s{cc}{tb}")
        v_t2 = ST.v
        v_t2A[cc] = v_t2
    for tb in range(4):
        kb.op("vector", lambda e, tb=tb: e.tensor_copy(
            m2_row[:, tb * 512:(tb + 1) * 512], pb[tb][0:1, 0:512]),
            waits=[(ST, v_t2)], dbg=f"l2em{tb}")
        kb.op("vector", lambda e, tb=tb: e.tensor_copy(
            s2_row[:, tb * 512:(tb + 1) * 512], pb[4 + tb][0:1, 0:512]), dbg=f"l2es{tb}")
    v_ln2ev = SV.v
    for i in range(8):
        brel(i, SV, v_ln2ev)
    kb.op("vector", lambda e: e.tensor_scalar_mul(mn2_row[:, :], m2_row[:, :], 1.0 / D), dbg="l2r1")
    kb.op("vector", lambda e: e.tensor_scalar_mul(s2_row[:, :], s2_row[:, :], 1.0 / D), dbg="l2r2")
    kb.op("vector", lambda e: e.tensor_mul(var2_row[:, :], mn2_row[:, :], mn2_row[:, :]), dbg="l2r3")
    v_v_var2 = kb.op("vector", lambda e: e.tensor_sub(var2_row[:, :], s2_row[:, :], var2_row[:, :]),
                     inc=SV, dbg="l2r4")
    v_s_sd2 = kb.op("scalar", lambda e: e.activation(
        sd2_row[:, :], var2_row[:, :], AF.Sqrt, scale=p_eps[0:1, 0:1], bias=p_eps[0:1, 1:2]),
        waits=[(SV, v_v_var2)], inc=SS, dbg="l2sqrt")
    v_v_rstd2 = kb.op("vector", lambda e: e.reciprocal(rstd2_row[:, :], sd2_row[:, :]),
                      waits=[(SS, v_s_sd2)], inc=SV, dbg="l2recip")
    for tb in range(4):
        b0, b1 = tb % 2, 2 + tb % 2
        kb.mm(pb[b0][0:128, 0:512], ones1r[:, :], mn2_row[:, tb * 512:(tb + 1) * 512],
              True, True, waits=[(SV, v_v_rstd2)] + bw(b0), dbg=f"l2rm{tb}")
        v_mm_rep2 = kb.mm(pb[b1][0:128, 0:512], ones1r[:, :], rstd2_row[:, tb * 512:(tb + 1) * 512],
                          True, True, waits=bw(b1), inc=ST, dbg=f"l2rr{tb}")
        kb.op("vector", lambda e, tb=tb, b0=b0: e.tensor_copy(
            m2_rep[:, tb * 512:(tb + 1) * 512], pb[b0][0:128, 0:512]),
            waits=[(ST, v_mm_rep2)], dbg=f"l2re{tb}")
        v_v_rep2 = kb.op("vector", lambda e, tb=tb, b1=b1: e.tensor_copy(
            r2_rep[:, tb * 512:(tb + 1) * 512], pb[b1][0:128, 0:512]), inc=SV, dbg=f"l2rf{tb}")
        brel(b0, SV, v_v_rep2); brel(b1, SV, v_v_rep2)
    v_vB2 = {}
    for cc in range(8):
        xc = XC2[cc % 2]
        k_ld = LD.dma(xc[:, :], d_x1.ap()[cc * 128:(cc + 1) * 128, :],
                      waits=[(SV, v_vB2.get(cc - 2, 0)), (ST, v_t2A[7] if cc < 2 else 0)],
                      dbg=f"l2b{cc}")
        kb.op("vector", lambda e, xc=xc: e.tensor_sub(SQ2b[:, :], xc[:, :], m2_rep[:, :]),
              waits=[k_ld], dbg=f"x2a{cc}")
        v_vB2[cc] = kb.op("vector", lambda e, cc=cc: e.tensor_mul(
            XN2[:, cc * TOK:(cc + 1) * TOK], SQ2b[:, :], r2_rep[:, :]), inc=SV, dbg=f"x2b{cc}")
    v_xn1 = SV.v
    tap("xn1", XN2[:, 0:8 * TOK], [128, 8 * TOK], [(SV, v_xn1)], BF16)

    k_agw = None
    for cc in range(8):
        k_agw = GD.dma(c_ag_i.ap()[cc * 128:(cc + 1) * 128, :], XN2[:, cc * TOK:(cc + 1) * TOK],
                       waits=([(SV, v_xn1)] if cc == 0 else []), dbg=f"agw{cc}", group=(cc > 0))
    v_ccag = kb.op("gpsimd", lambda e: e.collective_compute(
        "AllGather", ALU.bypass, replica_groups=PAIRS,
        ins=[c_ag_i.ap().opt()], outs=[c_ag_o.ap().opt()]),
        waits=[k_agw], inc=SC, dbg="AG")

    v_sgn_done = {}
    for dk in range(8):
        ws = WS0 if dk % 2 == 0 else WS1
        k_wf = LD.dma(ws[:, :], l1_wfT.ap()[dk * 128:(dk + 1) * 128, :],
                      waits=[(SV, v_sgn_done.get(dk - 2, 0))], dbg=f"sgl{dk}")
        kb.op("vector", lambda e, ws=ws: e.tensor_single_scalar(
            SQ2b[:, 0:1024], ws[:, :], 0.0, ALU.is_ge), waits=[k_wf], dbg=f"sga{dk}")
        v_sgn_done[dk] = kb.op("vector", lambda e, dk=dk: e.tensor_scalar(
            SGN[:, dk * 1024:(dk + 1) * 1024], SQ2b[:, 0:1024], 2.0, -1.0,
            ALU.mult, ALU.add), inc=SV, dbg=f"sgb{dk}")
    v_sgn = SV.v
    g1_toks = []
    for cho in range(8):
        for tb in range(4):
            bank = (cho * 4 + tb) % 4
            for dk in range(8):
                kb.mm(pb[bank][0:128, 0:512],
                      SGN[:, dk * 1024 + cho * 128:dk * 1024 + (cho + 1) * 128],
                      XN2[:, dk * TOK + tb * 512:dk * TOK + (tb + 1) * 512],
                      dk == 0, dk == 7,
                      waits=(([(SV, max(v_sgn, v_xn1))] + bw(bank)) if dk == 0 else []),
                      dbg=f"f1{cho}{tb}")
            v_mmf = ST.v
            kb.op("vector", lambda e, b=bank, cho=cho: e.tensor_scalar(
                SQ2b[:, 0:512], pb[b][0:128, 0:512],
                t_l1sb[:, 2 * cho:2 * cho + 1], t_l1sb[:, 2 * cho + 1:2 * cho + 2],
                ALU.mult, ALU.add),
                waits=[(ST, v_mmf), (SS, SS.v)], dbg=f"f1e{cho}{tb}a")
            v_cl = kb.op("vector", lambda e: e.tensor_scalar(
                SQ2b[:, 0:512], SQ2b[:, 0:512], 100.0, -100.0, ALU.min, ALU.max),
                inc=SV, dbg=f"f1e{cho}{tb}b")
            brel(bank, SV, v_cl)
            v_si = kb.op("scalar", lambda e: e.activation(G1B[:, :], SQ2b[:, 0:512], AF.Silu),
                         waits=[(SV, v_cl)] + ([g1_toks[-1]] if g1_toks else []),
                         inc=SS, dbg=f"f1e{cho}{tb}c")
            g1_toks.append(GD.dma(
                d_g1.ap()[cho * 128:(cho + 1) * 128, tb * 512:(tb + 1) * 512],
                G1B[:, :], waits=[(SS, v_si)], dbg=f"g1w{cho}{tb}"))
    tok_g1 = g1_toks[-1]

    if nseg < 9:
        return _finish(kb, nc)

    AR.reset(MK_S8)
    kb.barrier("rst_s9")
    MK_S9 = AR.mark()
    QF1 = AR.alloc("QF1", [128, 4 * 4096], BF16)
    VA1 = AR.alloc("VA1", [128, 32 * 520], BF16)
    MK_W1R = AR.mark()
    XNF = AR.alloc("XNF", [128, 16 * TOK], BF16)
    W1R = AR.alloc("W1R", [128, 8 * 512], BF16)
    KF1 = XN2

    k_agl = None
    for hf in range(2):
        for dk in range(8):
            k_agl = LD.dma(XNF[:, hf * 8 * TOK + dk * TOK:hf * 8 * TOK + (dk + 1) * TOK],
                           c_ag_o.ap()[hf * D + dk * 128:hf * D + (dk + 1) * 128, :],
                           waits=([(SC, v_ccag)] if (hf == 0 and dk == 0) else []),
                           dbg=f"agl{hf}{dk}", group=not (hf == 0 and dk == 0))
    v_vone1 = kb.op("vector", lambda e: e.memset(
        ap3(VA1, 0, 128, 64, [[65, 8 * 32]]), 1.0), inc=SV, dbg="vone1")

    k_cast1 = None
    for dd, ss_ in [(d_l1wq, l1_wqT), (d_l1wk, l1_wkT), (d_l1wv, l1_wvT), (d_l1wo, l1_woT)]:
        k_cast1 = GD.dma(dd.ap(), ss_.ap(), dbg="cast_w1", group=(dd is not d_l1wq))

    def l1_proj_fm(d_w, dst, nm, scale=None):
        k_w = None
        for dk in range(8):
            k_w = LD.dma(W1R[:, dk * 512:(dk + 1) * 512],
                         d_w.ap()[dk * 128:(dk + 1) * 128, :],
                         waits=([k_cast1, (ST, ST.v)] if dk == 0 else []),
                         dbg=nm + f"w{dk}", group=(dk > 0))
        for cc in range(4):
            for hf in range(2):
                for tb in range(4):
                    bank = (cc * 8 + hf * 4 + tb) % 4
                    for dk in range(8):
                        kb.mm(pb[bank][0:128, 0:512],
                              W1R[:, dk * 512 + cc * 128:dk * 512 + (cc + 1) * 128],
                              XNF[:, hf * 8 * TOK + dk * TOK + tb * 512:
                                  hf * 8 * TOK + dk * TOK + (tb + 1) * 512],
                              dk == 0, dk == 7,
                              waits=(([k_w, k_agl] + bw(bank)) if dk == 0 else []),
                              dbg=nm + f"{cc}{hf}{tb}")
                    v_mm1 = ST.v
                    col = cc * 4096 + hf * TOK + tb * 512
                    if scale is None:
                        v_e1 = kb.op("vector", lambda e, b=bank, col=col: e.tensor_copy(
                            dst[:, col:col + 512], pb[b][0:128, 0:512]),
                            waits=[(ST, v_mm1)], inc=SV, dbg=nm + f"e{cc}{hf}{tb}")
                    else:
                        v_e1 = kb.op("vector", lambda e, b=bank, col=col: e.tensor_scalar_mul(
                            dst[:, col:col + 512], pb[b][0:128, 0:512], scale),
                            waits=[(ST, v_mm1)], inc=SV, dbg=nm + f"e{cc}{hf}{tb}")
                    brel(bank, SV, v_e1)

    l1_proj_fm(d_l1wq, QF1, "q1", scale=0.125)
    l1_proj_fm(d_l1wk, KF1, "k1")
    k_wv1 = None
    for dk in range(8):
        k_wv1 = LD.dma(W1R[:, dk * 512:(dk + 1) * 512],
                       d_l1wv.ap()[dk * 128:(dk + 1) * 128, :],
                       waits=([k_cast1, (ST, ST.v)] if dk == 0 else []),
                       dbg=f"v1w{dk}", group=(dk > 0))
    for tcg in range(32):
        hf, tc = tcg // 16, tcg % 16
        bank = tcg % 4
        for dk in range(8):
            kb.mm(pb[bank][0:128, 0:512],
                  XNF[:, hf * 8 * TOK + dk * TOK + tc * 128:hf * 8 * TOK + dk * TOK + (tc + 1) * 128],
                  W1R[:, dk * 512:(dk + 1) * 512],
                  dk == 0, dk == 7,
                  waits=(([k_wv1, k_agl] + bw(bank)) if dk == 0 else []), dbg=f"v1{tcg}")
        v_mm1 = ST.v
        v_e1 = kb.op("vector", lambda e, b=bank, tcg=tcg: e.tensor_copy(
            ap3(VA1, 0, 128, tcg * 520, [[65, 8], [1, 64]]), pb[b][0:128, 0:512]),
            waits=[(ST, v_mm1), (SV, v_vone1)], inc=SV, dbg=f"v1e{tcg}")
        brel(bank, SV, v_e1)
    v_proj1 = SV.v
    tap("qf1", QF1[:, 0:4 * 4096], [128, 4 * 4096], [(SV, v_proj1)], BF16)
    tap("va1", VA1[:, 0:32 * 520], [128, 32 * 520], [(SV, v_proj1)], BF16)

    AR.reset(MK_W1R)
    kb.barrier("rst_nrm")
    PC = AR.alloc("PC", [128, 16], F32)
    MK_NRM = AR.mark()
    SQW = AR.alloc("SQW", [128, 4096], F32)
    nrm_row = AR.alloc("nrm_row", [1, 512], F32)
    mq8 = AR.alloc("mq8", [1, 16], F32)
    crow = AR.alloc("crow", [1, 8], F32)
    kb.op("vector", lambda e: e.memset(mq8[:, :], 0.0), dbg="nq0")
    for src_t, base_col in [(QF1, 0), (KF1, 8)]:
        for cc in range(4):
            v_sqw = kb.op("vector", lambda e, s=src_t, cc=cc: e.tensor_mul(
                SQW[:, :], s[:, cc * 4096:(cc + 1) * 4096], s[:, cc * 4096:(cc + 1) * 4096]),
                waits=[(SV, v_proj1), (ST, ST.v)], inc=SV, dbg=f"nsq{base_col}{cc}")
            for par in range(2):
                h = cc * 2 + par
                pp = par * 64
                for qb in range(8):
                    kb.mm(pb[6][0:1, 0:512], PONE2[0:128, par:par + 1],
                          SQW[0:128, qb * 512:(qb + 1) * 512],
                          True, True, waits=[(SV, v_sqw)] + bw(6), dbg=f"nm{base_col}{h}{qb}")
                    v_nm = ST.v
                    v_ncp = kb.op("vector", lambda e: e.tensor_copy(nrm_row[:, :], pb[6][0:1, 0:512]),
                          waits=[(ST, v_nm)], dbg=f"ne{base_col}{h}{qb}")
                    brel(6, SV, v_ncp)
                    kb.op("vector", lambda e: e.tensor_reduce(
                        nrm_row[0:1, 0:1], nrm_row[0:1, :], AX.X, ALU.max), dbg=f"nr{base_col}{h}{qb}")
                    kb.op("vector", lambda e, h=h, b=base_col: e.tensor_max(
                        mq8[0:1, b + h:b + h + 1], mq8[0:1, b + h:b + h + 1],
                        nrm_row[0:1, 0:1]), dbg=f"nx{base_col}{h}{qb}")
    v_nrm = SV.v
    v_sq8 = kb.op("scalar", lambda e: e.activation(mq8[0:1, :], mq8[0:1, :], AF.Sqrt),
                  waits=[(SV, v_nrm)], inc=SS, dbg="nsqrt")
    kb.op("vector", lambda e: e.tensor_mul(crow[0:1, :], mq8[0:1, 0:8], mq8[0:1, 8:16]),
          waits=[(SS, v_sq8)], dbg="ncr")
    v_crow = kb.op("vector", lambda e: e.tensor_scalar_mul(crow[0:1, :], crow[0:1, :], -1.0),
                   inc=SV, dbg="ncr2")
    kb.op("vector", lambda e: e.memset(ap3(PC, 0, 128, 0, [[2, 8]]), 1.0), dbg="pc1")
    v_pcz = SV.v
    v_pc = v_pcz
    for h in range(8):
        v_pc = bcast_rows(crow[0:1, h:h + 1], PC[:, 2 * h + 1:2 * h + 2], 1, 6,
                          [(SV, v_crow), (SV, v_pc)], f"pc{h}")
    tap("crow", PC[:, 0:16], [128, 16], [(SV, v_pc)], F32)

    if nseg < 10:
        return _finish(kb, nc)

    AR.reset(MK_NRM)
    kb.barrier("rst_s10")
    Y1 = AR.alloc("Y1", [128, 4 * 4096], BF16)
    PBUF = [AR.alloc(f"PBUF{i}", [128, 512], BF16) for i in range(3)]
    CMI = AR.alloc("CMI", [128, 512], F32)
    CM = AR.alloc("CM", [128, 4 * 512], BF16)
    RLrow = AR.alloc("RLrow", [1, 512], F32)
    RLrep = AR.alloc("RLrep", [64, 512], BF16)
    YS = AR.alloc("YS", [64, 512], BF16)
    QZ1 = AR.alloc("QZ1", [128, 512], BF16)
    PBUFB = [AR.alloc(f"PBUFB{i}", [128, 512], BF16) for i in range(3)]
    RLrowB = AR.alloc("RLrowB", [1, 512], F32)
    RLrepB = AR.alloc("RLrepB", [64, 512], BF16)
    QZ1B = AR.alloc("QZ1B", [128, 512], BF16)

    v_cm = 0
    for mloc in range(4):
        kb.op("gpsimd", g_iota(CMI[:, :], [[1, 512]], -128 * mloc, -1),
              waits=[(SV, SV.v)], dbg=f"cmi{mloc}")
        v_cm = kb.op("vector", lambda e, m=mloc: e.tensor_single_scalar(
            CM[:, m * 512:(m + 1) * 512], CMI[:, :], 0.0, ALU.is_ge),
            waits=[(SG, SG.v)], inc=SV, dbg=f"cm{mloc}")

    sb_toks = []
    chA = dict(stb0=6, pvb=5, rlb=4, pbuf=PBUF, qz=QZ1, rlrow=RLrow,
               rlrep=RLrep, pv_prev=0)
    chB = dict(stb0=2, pvb=1, rlb=0, pbuf=PBUFB, qz=QZ1B, rlrow=RLrowB,
               rlrep=RLrepB, pv_prev=0)
    for hp in range(4):
        for qb in range(8):
            nkc = 4 * qb + 4
            chains = [(chA, 2 * hp), (chB, 2 * hp + 1)]
            for C, h in chains:
                cc, par = h // 2, h % 2
                C["v_qz"] = kb.op("vector", lambda e, C=C, cc=cc, qb=qb, par=par:
                    e.tensor_scalar_mul(
                        C["qz"][:, 0:512],
                        QF1[:, cc * 4096 + qb * 512:cc * 4096 + (qb + 1) * 512],
                        PONE2[:, par:par + 1]),
                    waits=[(ST, C["pv_prev"])] + ([(SV, v_proj1), (SV, v_pc)]
                                                  if (hp == 0 and qb == 0) else []),
                    inc=SV, dbg=f"qz1_{h}{qb}")
            for kc in range(nkc):
                for C, h in chains:
                    stb = C["stb0"] + kc % 2
                    kb.op("tensor", lambda e, b=stb, kc=kc, cc=h // 2, C=C: e.matmul(
                        pb[b][0:128, 0:512],
                        KF1[0:128, cc * 4096 + kc * 128:cc * 4096 + (kc + 1) * 128],
                        C["qz"][:, 0:512],
                        start=True, stop=True, skip_group_check=True),
                        waits=bw(stb) + ([(SV, C["v_qz"])] if kc == 0 else []),
                        inc=ST, dbg=f"st{h}{qb}{kc}")
                    C["v_st"] = ST.v
                for C, h in chains:
                    stb = C["stb0"] + kc % 2
                    pbuf = C["pbuf"][kc % 3]
                    v_exp = kb.op("scalar", lambda e, b=stb, p=pbuf, h=h: e.activation(
                        p[:, :], pb[b][0:128, 0:512], AF.Exp,
                        scale=PC[:, 2 * h:2 * h + 1], bias=PC[:, 2 * h + 1:2 * h + 2]),
                        waits=[(ST, C["v_st"]), (SV, SV.v)], inc=SS, dbg=f"ex{h}{qb}{kc}")
                    brel(stb, SS, v_exp)
                    C["w_pv"] = [(SS, v_exp)]
                    if kc >= 4 * qb:
                        v_mk = kb.op("vector", lambda e, p=pbuf, m=kc - 4 * qb: e.tensor_mul(
                            p[:, :], p[:, :], CM[:, m * 512:(m + 1) * 512]),
                            waits=[(SS, v_exp), (SV, v_cm)], inc=SV, dbg=f"msk{h}{qb}{kc}")
                        C["w_pv"] = [(SV, v_mk)]
                for C, h in chains:
                    pbuf = C["pbuf"][kc % 3]
                    kb.op("tensor", lambda e, p=pbuf, h=h, kc=kc, b=C["pvb"]: e.matmul(
                        pb[b][0:65, 0:512],
                        VA1[:, kc * 520 + h * 65:kc * 520 + (h + 1) * 65],
                        p[:, :], start=(kc == 0), stop=(kc == nkc - 1),
                        skip_group_check=True),
                        waits=C["w_pv"] + (bw(C["pvb"]) if kc == 0 else []),
                        inc=ST, dbg=f"pv{h}{qb}{kc}")
                    C["v_pv"] = ST.v
            for C, h in chains:
                cc, par = h // 2, h % 2
                C["pv_prev"] = C["v_pv"]
                v_rl = kb.op("vector", lambda e, C=C: e.reciprocal(
                    C["rlrow"][0:1, :], pb[C["pvb"]][64:65, 0:512]),
                    waits=[(ST, C["v_pv"])], inc=SV, dbg=f"rl{h}{qb}")
                v_rep = kb.mm(pb[C["rlb"]][0:64, 0:512], ones1r[0:1, 0:64],
                              C["rlrow"][0:1, :], True, True,
                              waits=[(SV, v_rl)] + bw(C["rlb"]), inc=ST, dbg=f"rlr{h}{qb}")
                v_repc = kb.op("vector", lambda e, C=C: e.tensor_copy(
                    C["rlrep"][:, :], pb[C["rlb"]][0:64, 0:512]),
                    waits=[(ST, v_rep)], inc=SV, dbg=f"rlc{h}{qb}")
                brel(C["rlb"], SV, v_repc)
                ycol = cc * 4096 + qb * 512
                if par == 0:
                    v_y1 = kb.op("vector", lambda e, ycol=ycol, C=C: e.tensor_mul(
                        Y1[0:64, ycol:ycol + 512], pb[C["pvb"]][0:64, 0:512],
                        C["rlrep"][:, :]),
                        waits=[(SV, v_repc)], inc=SV, dbg=f"y{h}{qb}")
                    brel(C["pvb"], SV, v_y1)
                else:
                    v_ys = kb.op("vector", lambda e, C=C: e.tensor_mul(
                        YS[:, :], pb[C["pvb"]][0:64, 0:512], C["rlrep"][:, :]),
                        waits=[(SV, v_repc)] + ([sb_toks[-1]] if sb_toks else []),
                        inc=SV, dbg=f"ys{h}{qb}")
                    brel(C["pvb"], SV, v_ys)
                    sb_toks.append(GD.dma(
                        bass.AP(Y1, 64 * (4 * 4096) + ycol, [[4 * 4096, 64], [1, 512]]),
                        YS[0:64, :], waits=[(SV, v_ys)], dbg=f"ysc{h}{qb}"))
    v_flash = SV.v
    tap("y1", Y1[:, 0:4 * 4096], [128, 4 * 4096], [(SV, v_flash)] + sb_toks[-1:], BF16)

    if nseg < 11:
        return _finish(kb, nc)

    WO1 = AR.alloc("WO1", [128, 4 * 1024], BF16)
    RSB = AR.alloc("RSB", [128, 1024], BF16)
    XF0 = AR.alloc("XF0", [128, 512], F32)
    GF0 = AR.alloc("GF0", [128, 512], BF16)
    SF0 = AR.alloc("SF0", [128, 512], BF16)
    OUTB = AR.alloc("OUTB", [128, 512], F32)

    k_wo1 = None
    for cc in range(4):
        k_wo1 = LD.dma(WO1[:, cc * 1024:(cc + 1) * 1024],
                       d_l1wo.ap()[cc * 128:(cc + 1) * 128, :],
                       waits=([k_cast1] if cc == 0 else []), dbg="ldwo1", group=(cc > 0))
    rs_w_toks = [[] for _ in range(4)]
    v_rsb_free = 0
    last_tk = None
    for tcg in range(32):
        kbuf = (tcg % 16) // 4
        arow = (tcg % 4) * 128 + (512 if tcg >= 16 else 0)
        for db in range(2):
            bank = (tcg * 2 + db) % 4
            for cc in range(4):
                kb.mm(pb[bank][0:128, 0:512],
                      Y1[:, cc * 4096 + tcg * 128:cc * 4096 + (tcg + 1) * 128],
                      WO1[:, cc * 1024 + db * 512:cc * 1024 + (db + 1) * 512],
                      cc == 0, cc == 3,
                      waits=(([k_wo1, (SV, v_flash)] + sb_toks[-1:] + bw(bank))
                             if cc == 0 else []),
                      dbg=f"wp{tcg}{db}")
            v_mmw = ST.v
            v_ev = kb.op("vector", lambda e, b=bank, db=db: e.tensor_copy(
                RSB[:, db * 512:(db + 1) * 512], pb[b][0:128, 0:512]),
                waits=[(ST, v_mmw)] + ([last_tk] if (db == 0 and last_tk) else []),
                dbg=f"wpe{tcg}{db}")
            brel(bank, SV, v_ev)
        v_rsb = SV.v
        tk = GD.dma(c_rs_i[kbuf].ap()[arow:arow + 128, :], RSB[:, :],
                    waits=[(SV, v_rsb)], dbg=f"rsw{tcg}")
        rs_w_toks[kbuf].append(tk)
        last_tk = tk
    v_ccrs = {}
    for kbuf in range(4):
        v_ccrs[kbuf] = kb.op("gpsimd", lambda e, k=kbuf: e.collective_compute(
            "ReduceScatter", ALU.add, replica_groups=PAIRS,
            ins=[c_rs_i[k].ap().opt()], outs=[c_rs_o[k].ap().opt()]),
            waits=list(rs_w_toks[kbuf]), inc=SC, dbg=f"RS{kbuf}")

    out_toks = []
    v_fc = 0
    for dk in range(8):
        for kbuf in range(4):
            k_x1 = LD.dma(XF0[:, :], d_x1.ap()[dk * 128:(dk + 1) * 128,
                                               kbuf * 512:(kbuf + 1) * 512],
                          waits=[(SV, v_fc)] + list(x1_toks[-1:]), dbg=f"fx{dk}{kbuf}")
            k_g1 = LD.dma(GF0[:, :], d_g1.ap()[dk * 128:(dk + 1) * 128,
                                               kbuf * 512:(kbuf + 1) * 512],
                          waits=[tok_g1, (SV, v_fc)], dbg=f"fg{dk}{kbuf}")
            k_s1 = TP.dma(SF0[:, :], c_rs_o[kbuf].ap()[:, dk * 128:(dk + 1) * 128],
                          waits=[(SC, v_ccrs[kbuf]), (SV, v_fc)] + sb_toks[-1:], transpose=True,
                          dbg=f"fs{dk}{kbuf}")
            kb.op("vector", lambda e: e.tensor_mul(OUTB[:, :], SF0[:, :], GF0[:, :]),
                  waits=[k_x1, k_g1, k_s1] + out_toks[-1:], dbg=f"fc{dk}{kbuf}a")
            v_fc = kb.op("vector", lambda e: e.tensor_add(OUTB[:, :], OUTB[:, :], XF0[:, :]),
                         inc=SV, dbg=f"fc{dk}{kbuf}b")
            out_toks.append(LD.dma(out_T.ap()[dk * 128:(dk + 1) * 128,
                                              kbuf * 512:(kbuf + 1) * 512],
                                   OUTB[:, :], waits=[(SV, v_fc)], dbg=f"fo{dk}{kbuf}"))

    return _finish(kb, nc)


def _finish(kb, nc):
    waits = [(s, s.v) for s in kb.sems.values() if s.v > 0]
    kb.wait_only("sync", waits, dbg="final_wait")
    kb.check()
    with nc.Block() as block:
        for eng_name in KB.ENGS:
            def make(en):
                def emit(eng):
                    for e2, cl in kb.prog:
                        if e2 == en:
                            cl(eng)
                return emit
            getattr(block, eng_name)(make(eng_name))
    return nc, kb



def _host_inputs(inputs, core):
    x = np.asarray(inputs["L0_attn_x"])
    xf = np.ascontiguousarray(x.reshape(B * S, D))
    rows = slice(core * TOK, (core + 1) * TOK)
    hh = core % 2
    d = {}
    d["xT"] = np.ascontiguousarray(xf[rows].T)
    dl = np.asarray(inputs["L0_attn_decay_log"]).reshape(8)
    d["dlog_c"] = np.ascontiguousarray(dl.reshape(8, 1))
    d["dlog_r"] = np.ascontiguousarray(dl.reshape(1, 8))
    d["wqT"] = np.ascontiguousarray(np.asarray(inputs["L0_attn_W_Q"]).T)
    d["wkT"] = np.ascontiguousarray(np.asarray(inputs["L0_attn_W_K"]).T)
    d["wvT"] = np.ascontiguousarray(np.asarray(inputs["L0_attn_W_V"]).T)
    d["wgT"] = np.ascontiguousarray(np.asarray(inputs["L0_attn_W_g_w"]).T)
    d["wgb_row"] = np.ascontiguousarray(np.asarray(inputs["L0_attn_W_g_b"]).reshape(1, CH0))
    d["woT"] = np.ascontiguousarray(np.asarray(inputs["L0_attn_W_O"]).T)
    d["wup"] = np.ascontiguousarray(np.asarray(inputs["L0_ffn_w_up"]))
    d["wgate"] = np.ascontiguousarray(np.asarray(inputs["L0_ffn_w_gate"]))
    d["wdown"] = np.ascontiguousarray(np.asarray(inputs["L0_ffn_w_down"]))
    sl = slice(hh * CH1, (hh + 1) * CH1)
    d["l1_wqT"] = np.ascontiguousarray(np.asarray(inputs["L1_attn_q_heads"])[sl].T)
    d["l1_wkT"] = np.ascontiguousarray(np.asarray(inputs["L1_attn_k_heads"])[sl].T)
    d["l1_wvT"] = np.ascontiguousarray(np.asarray(inputs["L1_attn_v_heads"])[sl].T)
    d["l1_woT"] = np.ascontiguousarray(np.asarray(inputs["L1_attn_output_proj"]).T[sl])
    d["l1_wfT"] = np.ascontiguousarray(np.asarray(inputs["L1_ffn_weights"]).T)
    sc = np.asarray(inputs["L1_ffn_scale"]).reshape(8, 128)
    bi = np.asarray(inputs["L1_ffn_bias"]).reshape(8, 128)
    sb = np.empty((128, 16), np.float32)
    sb[:, 0::2] = sc.T
    sb[:, 1::2] = bi.T
    d["l1_sb"] = sb
    fl = np.zeros((1, 8), np.float32)
    fl[0, 0] = 1.0 if core % 2 == 0 else 0.0
    fl[0, 1] = 0.0 if core % 2 == 0 else 1.0
    d["flags"] = fl
    hs = np.zeros((8, CH0), np.float32)
    for ch in range(CH0):
        hs[ch // DK0, ch] = 1.0
    d["hsel_ch"] = hs
    h65 = np.zeros((8, 4 * 65), np.float32)
    ps = np.zeros((8, 128), np.float32)
    for h in range(H0):
        h65[h, (h // 2) * 65:(h // 2) * 65 + 65] = 1.0
        ps[h, (h % 2) * 64:(h % 2) * 64 + 64] = 1.0
    d["hsel65"] = h65
    d["psel"] = ps
    for k in d:
        if d[k].dtype == np.float64:
            d[k] = d[k].astype(np.float32)
    return d


def kernel(**inputs):
    nc, kb = build()
    in_maps = [_host_inputs(inputs, c) for c in range(NCORES)]
    res = run_bass_kernel_spmd(nc, in_maps, core_ids=list(range(NCORES)))
    outs = []
    for c in range(NCORES):
        outs.append(np.asarray(res.results[c]["out_T"]).T)
    return np.ascontiguousarray(
        np.concatenate(outs, axis=0).reshape(B, S, D)).astype(np.float32)


# version 04 (final)
# speedup vs baseline: 1.1167x; 1.0223x over previous
import sys, os
if "/opt/trn_rl_repo" not in sys.path:
    sys.path.insert(0, "/opt/trn_rl_repo")

import numpy as np
import concourse.bass as bass
import concourse.mybir as mybir
from concourse.bass_utils import run_bass_kernel_spmd
from concourse.bass_isa import ReduceOp
import concourse.library_config as libcfg

F32 = mybir.dt.float32
BF16 = mybir.dt.bfloat16
I32 = mybir.dt.int32
U8 = mybir.dt.uint8
ALU = mybir.AluOpType
AF = mybir.ActivationFunctionType
AX = mybir.AxisListType

B, S, D = 4, 4096, 1024
NCORES = 8
TOK = 2048
H0, DK0, CH0 = 8, 64, 512
FF = 3072
CHUNK = 128
NCH = TOK // CHUNK
H1, DK1, CH1 = 8, 64, 512
QMAX = 32767.0
EPS_LN = 1e-5
EPS_ATT = 1e-6
PAIRS = [[0, 1], [2, 3], [4, 5], [6, 7]]
ALL8 = [list(range(8))]
NSEG_DEFAULT = int(os.environ.get("KSEG", "99"))
SUB3 = int(os.environ.get("KSUB3", "9"))
KPHI = int(os.environ.get("KPHI", "1"))
KGRP = int(os.environ.get("KGRP", "99"))
KBANK0 = int(os.environ.get("KBANK0", "0"))
KNONOP = int(os.environ.get("KNONOP", "0"))
KSC4 = int(os.environ.get("KSC4", "9"))
KODD = int(os.environ.get("KODD", "1"))


def rowsz(t):
    r = 1
    for s in t.shape[1:]:
        r *= s
    return r


def ap3(t, p0, np_, base, dims):
    return bass.AP(t, p0 * rowsz(t) + base, [[rowsz(t), np_]] + [list(d) for d in dims])


class Sem:
    __slots__ = ("name", "h", "v", "maxwait")

    def __init__(self, name, h):
        self.name, self.h, self.v, self.maxwait = name, h, 0, 0


class Arena:

    def __init__(self, nc, nbytes):
        self.nc = nc
        self.res = nc.alloc_sbuf_tensor("ARENA", [128, nbytes], U8)
        self.base = nc.lookup_mloc(self.res).addr
        self.off = 0
        self.size = nbytes
        self.n = 0

    def alloc(self, name, shape, dtype):
        per = 1
        for s in shape[1:]:
            per *= s
        per *= mybir.dt.size(dtype)
        self.off = (self.off + 63) & ~63
        assert self.off + per <= self.size, f"arena overflow at {name}: {self.off}+{per}"
        self.n += 1
        h = self.nc.virtual_sbuf_tensor(f"{name}_{self.n}", list(shape), dtype)
        self.nc._tensor(h.name, h.shape, h.dtype, type="SB")
        mloc = self.nc.lookup_mloc(h)
        mloc.allocated = True
        mloc.addr = self.base + self.off
        self.off += per
        return h

    def mark(self):
        return self.off

    def reset(self, m):
        self.off = m


class KB:
    ENGS = ("tensor", "vector", "scalar", "sync", "gpsimd")

    def __init__(self, nc):
        self.nc = nc
        self.prog = []
        self.meta = []
        self.sems = {}
        self.tapinfo = {}
        self.auto = {}

    def sem(self, name):
        s = Sem(name, self.nc.alloc_semaphore(name=name))
        self.sems[name] = s
        return s

    def op(self, eng, fn, waits=(), inc=None, inc_by=1, dbg=""):
        if inc is None:
            inc = self.auto.get(eng)
        waits = list(waits)
        if eng in ("vector", "scalar", "gpsimd") and eng in self.auto:
            s_ = self.auto[eng]
            waits.append((s_, s_.v))
        waits = tuple((s, v) for (s, v) in waits if s is not None and v > 0)
        for s, v in waits:
            s.maxwait = max(s.maxwait, v)

        def cl(e, fn=fn, waits=waits, inc=inc, inc_by=inc_by):
            for s, v in waits:
                e.wait_ge(s.h, v)
            ins = fn(e)
            if inc is not None:
                assert ins is not None, "inc on an op that emitted no instruction"
                ins.then_inc(inc.h, inc_by)
        self.prog.append((eng, cl))
        self.meta.append((eng, waits, inc, inc_by, dbg))
        if inc is not None:
            inc.v += inc_by
            return inc.v
        return None

    def dma_raw(self, q, dst, src, sem, waits=(), transpose=False, dbg="dma"):
        def fn(e, dst=dst, src=src, tp=transpose):
            if tp:
                return e.dma_start(out=dst, in_=src, transpose=True)
            return e.dma_start(out=dst, in_=src)
        return self.op(q, fn, waits=waits, inc=sem, inc_by=16, dbg=dbg)


    def wait_only(self, eng, waits, dbg="w"):
        waits = [(s, v) for (s, v) in waits if s is not None and v > 0]
        for s, v in waits:
            s.maxwait = max(s.maxwait, v)

        def cl(e, waits=tuple(waits)):
            for s, v in waits:
                e.wait_ge(s.h, v)
        self.prog.append((eng, cl))
        self.meta.append((eng, tuple(waits), None, 0, dbg))

    def barrier(self, dbg="barrier"):
        waits = [(s, s.v) for s in self.sems.values() if s.v > 0]
        for e in self.ENGS:
            self.wait_only(e, waits, dbg=dbg + "_" + e)

    def mm(self, out, lhsT, rhs, start, stop, waits=(), inc=None, dbg="mm", tp=None):
        return self.op(
            "tensor",
            lambda e, o=out, l=lhsT, r=rhs, st=start, sp=stop, tp=tp:
                e.matmul(o, l, r, start=st, stop=sp, skip_group_check=True,
                         tile_position=tp),
            waits=waits, inc=inc, dbg=dbg)

    def check(self):
        for s in self.sems.values():
            assert s.maxwait <= s.v, f"sem {s.name}: wait {s.maxwait} > final {s.v}"
        streams = {e: [] for e in self.ENGS}
        for (e, w, inc, ib, dbg) in self.meta:
            streams[e].append((w, inc, ib, dbg))
        cnt = {name: 0 for name in self.sems}
        ptr = {e: 0 for e in self.ENGS}
        progress = True
        while progress:
            progress = False
            for e in self.ENGS:
                while ptr[e] < len(streams[e]):
                    w, inc, ib, dbg = streams[e][ptr[e]]
                    if all(cnt[s.name] >= v for s, v in w):
                        if inc is not None:
                            cnt[inc.name] += ib
                        ptr[e] += 1
                        progress = True
                    else:
                        break
        stuck = {e: ptr[e] for e in self.ENGS if ptr[e] < len(streams[e])}
        if stuck:
            msg = []
            for e, p in stuck.items():
                w, inc, ib, dbg = streams[e][p]
                need = [(s.name, v, cnt[s.name]) for s, v in w if cnt[s.name] < v]
                msg.append(f"{e}@{p} '{dbg}' needs {need}")
            raise AssertionError("DEADLOCK in static check: " + "; ".join(msg))


class Stream:

    def __init__(self, kb, q, name, R=4):
        self.kb, self.q, self.R = kb, q, R
        self.sems = [kb.sem(f"{name}{i}") for i in range(R)]
        self.g = 0
        self._open = False

    def dma(self, dst, src, waits=(), transpose=False, dbg="dma", group=False):
        s = self.sems[self.g % self.R]
        if not group or not self._open:
            if not group:
                self.g += 1
                s = self.sems[self.g % self.R]
            gate = [(s, s.v)]
            self._open = True
        else:
            gate = []
        v = self.kb.dma_raw(self.q, dst, src, s, waits=list(waits) + gate,
                            transpose=transpose, dbg=dbg)
        return (s, v)

    def new_group(self):
        self._open = False


def build(nseg=NSEG_DEFAULT, taps=()):
    taps = set(taps)
    nc = bass.Bass("TRN2", target_bir_lowering=False)
    kb = KB(nc)

    def inp(name, shape, dtype=F32):
        return nc.dram_tensor(name, list(shape), dtype, kind="ExternalInput")

    xT = inp("xT", [D, TOK])
    dlog_c = inp("dlog_c", [8, 1]); dlog_r = inp("dlog_r", [1, 8])
    wqT = inp("wqT", [D, CH0]); wkT = inp("wkT", [D, CH0])
    wvT = inp("wvT", [D, CH0]); wgT = inp("wgT", [D, CH0])
    wgb_row = inp("wgb_row", [1, CH0])
    woT = inp("woT", [CH0, D])
    wup = inp("wup", [FF, D]); wgate = inp("wgate", [FF, D])
    wdown = inp("wdown", [D, FF])
    l1_wqT = inp("l1_wqT", [D, CH1]); l1_wkT = inp("l1_wkT", [D, CH1])
    l1_wvT = inp("l1_wvT", [D, CH1]); l1_woT = inp("l1_woT", [CH1, D])
    l1_wfT = inp("l1_wfT", [D, D])
    l1_sb = inp("l1_sb", [128, 16])
    flags = inp("flags", [1, 8])
    hsel_ch = inp("hsel_ch", [8, CH0])
    hsel65 = inp("hsel65", [8, 4 * 65])
    psel = inp("psel", [8, 128])

    out_T = nc.dram_tensor("out_T", [D, TOK], F32, kind="ExternalOutput")

    dram = lambda n, s, dt=BF16: nc.dram_tensor(n, list(s), dt)
    d_wq = dram("d_wq", [D, CH0]); d_wk = dram("d_wk", [D, CH0])
    d_wv = dram("d_wv", [D, CH0]); d_wg = dram("d_wg", [D, CH0])
    d_wo = dram("d_wo", [CH0, D])
    d_wupq = dram("d_wupq", [FF, D]); d_wgtq = dram("d_wgtq", [FF, D])
    d_wdnq = dram("d_wdnq", [D, FF])
    d_wupqT = dram("d_wupqT", [D, FF]); d_wgtqT = dram("d_wgtqT", [D, FF])
    d_wdnqT = dram("d_wdnqT", [FF, D])
    d_t5 = dram("d_t5", [FF, TOK])
    d_u = dram("d_u", [FF, TOK]); d_g = dram("d_g", [FF, TOK]); d_t4 = dram("d_t4", [FF, TOK])
    d_x1 = dram("d_x1", [D, TOK], F32)
    d_g1 = dram("d_g1", [D, TOK])
    d_l1wq = dram("d_l1wq", [D, CH1]); d_l1wk = dram("d_l1wk", [D, CH1])
    d_l1wv = dram("d_l1wv", [D, CH1]); d_l1wo = dram("d_l1wo", [CH1, D])
    c_ar_i = [dram(f"c_ar_i{i}", [1, 16], F32) for i in range(4)]
    c_ar_o = [dram(f"c_ar_o{i}", [1, 16], F32) for i in range(4)]
    c_st_i = dram("c_st_i", [128, 4 * 65], F32)
    c_st_o = dram("c_st_o", [128, 4 * 65], F32)
    c_ag_i = dram("c_ag_i", [D, TOK])
    c_ag_o = dram("c_ag_o", [2 * D, TOK])
    c_rs_i = [dram(f"c_rs_i{i}", [1024, D]) for i in range(4)]
    c_rs_o = [dram(f"c_rs_o{i}", [512, D]) for i in range(4)]

    SV = kb.sem("SV"); SS = kb.sem("SS"); ST = kb.sem("ST")
    SG = kb.sem("SG")
    SC = kb.sem("SC")
    kb.auto = {"vector": SV, "scalar": SS, "tensor": ST, "gpsimd": SG}
    LD = Stream(kb, "sync", "LD", 4)
    TP = Stream(kb, "sync", "TP", 4)
    GD = Stream(kb, "gpsimd", "GD", 4)

    pb = [nc.alloc_psum_tensor(f"pb{i}", [128, 512], F32) for i in range(8)]
    bank_free = {i: (None, 0) for i in range(8)}

    def bw(*idx):
        out = []
        for i in idx:
            s, v = bank_free[i]
            if s is not None:
                out.append((s, v))
        return out

    def brel(i, sem, val):
        bank_free[i] = (sem, val)

    AR = Arena(nc, ((nc.sbuf_bytes_remaining - 2048) // 64) * 64)

    ones128 = AR.alloc("ones128", [128, 1], F32)
    ones1r = AR.alloc("ones1r", [1, 128], F32)
    ones8x64 = AR.alloc("ones8x64", [8, 64], F32)
    mask_all = AR.alloc("mask_all", [128, H0 * 128], F32)
    gammaQ = AR.alloc("gammaQ", [128, 4 * 128], F32)
    gammaK = AR.alloc("gammaK", [128, CH0], F32)
    gammaC = AR.alloc("gammaC", [128, 4 * 65], F32)
    t_brep = AR.alloc("t_brep", [128, CH0], F32)
    t_flag = AR.alloc("t_flag", [1, 8], F32)
    fl_contrib = AR.alloc("fl_contrib", [128, 1], F32)
    fl_use = AR.alloc("fl_use", [128, 1], F32)
    t_l1sb = AR.alloc("t_l1sb", [128, 16], F32)
    p_eps = AR.alloc("p_eps", [1, 2], F32)
    t_s = [AR.alloc(f"t_s{i}", [128, 4], F32) for i in range(1, 6)]
    t_locmax = AR.alloc("t_locmax", [128, 2], F32)
    t_ar = AR.alloc("t_ar", [1, 16], F32)
    t_arstage = AR.alloc("t_arstage", [1, 16], F32)
    t_ident = AR.alloc("t_ident", [128, 128], F32)
    t_row128 = AR.alloc("t_row128", [1, 128], F32)
    PONE2 = AR.alloc("PONE2", [128, 2], F32)
    PH1 = AR.alloc("PH1", [128, 512], F32)
    PH2 = AR.alloc("PH2", [128, 512], F32)

    XN2 = AR.alloc("XN2", [128, 8 * TOK], BF16)
    YGT = AR.alloc("YGT", [128, 4 * TOK], BF16)

    MK_SETUP = AR.mark()
    t_dlc = AR.alloc("t_dlc", [8, 1], F32)
    t_dlr = AR.alloc("t_dlr", [1, 8], F32)
    t_g8 = AR.alloc("t_g8", [8, 1], F32)
    t_lg8 = AR.alloc("t_lg8", [8, 1], F32)
    t_lgr = AR.alloc("t_lgr", [1, 8], F32)
    t_lgr2 = AR.alloc("t_lgr2", [1, 8], F32)
    t_lgbc = AR.alloc("t_lgbc", [128, 8], F32)
    t_gc8 = AR.alloc("t_gc8", [8, 1], F32)
    t_hsel = AR.alloc("t_hsel", [8, CH0], F32)
    t_hsel65 = AR.alloc("t_hsel65", [8, 4 * 65], F32)
    t_psel = AR.alloc("t_psel", [8, 128], F32)
    t_iotaD = AR.alloc("t_iotaD", [128, 128], F32)
    t_reluD = AR.alloc("t_reluD", [128, 128], F32)
    t_causal = AR.alloc("t_causal", [128, 128], F32)
    t_jcol = AR.alloc("t_jcol", [128, 1], F32)
    t_rev = AR.alloc("t_rev", [128, 1], F32)
    t_jp1 = AR.alloc("t_jp1", [128, 1], F32)
    t_ramp = AR.alloc("t_ramp", [128, 128], F32)
    t_lnch = AR.alloc("t_lnch", [1, CH0], F32)
    t_lnchb = AR.alloc("t_lnchb", [128, CH0], F32)
    t_lnG = AR.alloc("t_lnG", [128, 4], F32)
    t_wgb = AR.alloc("t_wgb", [1, CH0], F32)
    p_128 = AR.alloc("p_128", [8, 2], F32)
    t_scr8 = AR.alloc("t_scr8", [8, 4 * 65], F32)

    def tap(name, src_ap, shape, waits, dtype=F32):
        if name not in taps:
            return
        dt_ = nc.dram_tensor("tap_" + name, list(shape), dtype, kind="ExternalOutput")
        LD.dma(dt_.ap(), src_ap, waits=waits, dbg=f"tap_{name}")
        kb.tapinfo[name] = (list(shape), dtype)

    for i, (dst, src) in enumerate([
            (t_dlc[:, :], dlog_c.ap()), (t_dlr[:, :], dlog_r.ap()),
            (t_hsel[:, :], hsel_ch.ap()), (t_hsel65[:, :], hsel65.ap()), (t_psel[:, :], psel.ap()),
            (t_flag[:, :], flags.ap()), (t_l1sb[:, :], l1_sb.ap()),
            (t_wgb[:, :], wgb_row.ap())]):
        k_ld0 = LD.dma(dst, src, dbg="ld_small", group=(i > 0))
    for i, (dd, ss_) in enumerate([(d_wq, wqT), (d_wk, wkT), (d_wv, wvT),
                                   (d_wg, wgT), (d_wo, woT)]):
        k_cast = GD.dma(dd.ap(), ss_.ap(), dbg="cast_w0", group=(i > 0))

    def g_iota(t_ap, pattern, base, chm):
        return lambda e: e.iota(t_ap, pattern, base=base, channel_multiplier=chm,
                                allow_small_or_imprecise_dtypes=True)

    kb.op("gpsimd", g_iota(t_iotaD[:, :], [[1, 128]], 0, -1), dbg="iotaD")
    v_g_iota = kb.op("gpsimd", g_iota(t_jcol[:, 0:1], [[0, 1]], 0, 1), dbg="iotaJ") or SG.v

    for t_, val in [(ones128, 1.0), (ones1r, 1.0), (ones8x64, 1.0)]:
        kb.op("vector", lambda e, t=t_, v=val: e.memset(t[:, :], v), dbg="memset")
    kb.op("vector", lambda e: e.memset(p_eps[0:1, 0:1], 1.0), dbg="ms")
    kb.op("vector", lambda e: e.memset(p_eps[0:1, 1:2], EPS_LN), dbg="ms")
    kb.op("vector", lambda e: e.memset(p_128[:, 0:1], float(CHUNK)), dbg="ms")
    kb.op("vector", lambda e: e.memset(p_128[:, 1:2], 0.0), dbg="ms")
    kb.op("vector", lambda e: e.memset(t_arstage[:, :], 0.0), dbg="ms")
    for _ts in t_s:
        kb.op("vector", lambda e, t=_ts: e.memset(t[:, :], 0.0), dbg="ms_ts")
    v_vconst = SV.v
    v_ident = kb.op("vector", lambda e: e.tensor_single_scalar(
        t_ident[:, :], t_iotaD[:, :], 0.0, ALU.is_equal),
        waits=[(SG, v_g_iota)], inc=SV, dbg="ident")

    def bcast_rows(src_row_ap, dst_ap, ncols, bank, waits, dbg):
        v_mm = kb.mm(pb[bank][0:128, 0:ncols], ones1r[:, :], src_row_ap, True, True,
                     waits=list(waits) + bw(bank) + [(SV, v_vconst)], inc=ST, dbg=dbg + "m")
        v_cp = kb.op("vector", lambda e, d=dst_ap, b=bank, n=ncols: e.tensor_copy(
            d, pb[b][0:128, 0:n]), waits=[(ST, v_mm)], inc=SV, dbg=dbg + "c")
        brel(bank, SV, v_cp)
        return v_cp

    def preduce(col_ap, out11_ap, bank, waits, op, dbg):
        v_mm = kb.op("tensor", lambda e, b=bank, c=col_ap: e.transpose(
            pb[b][0:1, 0:128], c, t_ident[:, :]),
            waits=list(waits) + bw(bank) + [(SV, v_ident)], inc=ST, dbg=dbg + "t")
        kb.op("vector", lambda e, b=bank: e.tensor_copy(t_row128[:, :], pb[b][0:1, 0:128]),
              waits=[(ST, v_mm)], dbg=dbg + "c")
        v_rd = kb.op("vector", lambda e, o=out11_ap, op=op: e.tensor_reduce(
            o, t_row128[0:1, :], AX.X, op), inc=SV, dbg=dbg + "r")
        brel(bank, SV, v_rd)
        return v_rd

    v_tmp = kb.op("scalar", lambda e: e.activation(t_g8[:, :], t_dlc[:, :], AF.Sigmoid),
                  waits=[k_ld0], inc=SS, dbg="gam")
    v_s_lg8 = kb.op("scalar", lambda e: e.activation(t_lg8[:, :], t_g8[:, :], AF.Ln),
                    waits=[(SS, v_tmp)], inc=SS, dbg="lngam")
    v_tmp = kb.op("scalar", lambda e: e.activation(t_lgr[:, :], t_dlr[:, :], AF.Sigmoid),
                  waits=[k_ld0], inc=SS, dbg="gamr")
    v_s_lgr = kb.op("scalar", lambda e: e.activation(t_lgr2[:, :], t_lgr[:, :], AF.Ln),
                    waits=[(SS, v_tmp)], inc=SS, dbg="lngamr")
    v_s_gc8 = kb.op("scalar", lambda e: e.activation(
        t_gc8[:, :], t_lg8[:, :], AF.Exp, scale=p_128[:, 0:1], bias=p_128[:, 1:2]),
        waits=[(SV, v_vconst), (SS, v_s_lg8)], inc=SS, dbg="gc8")
    v_g_lgbc = bcast_rows(t_lgr2[0:1, :], t_lgbc[:, 0:8], 8, 6, [(SS, v_s_lgr)], "lgbc")

    v_v_caus = kb.op("vector", lambda e: e.tensor_single_scalar(
        t_causal[:, :], t_iotaD[:, :], 0.0, ALU.is_ge),
        waits=[(SG, v_g_iota)], inc=SV, dbg="caus")
    v_s_relu = kb.op("scalar", lambda e: e.activation(t_reluD[:, :], t_iotaD[:, :], AF.Relu),
                     waits=[(SG, v_g_iota)], inc=SS, dbg="reluD")
    kb.op("vector", lambda e: e.tensor_scalar(
        t_rev[:, 0:1], t_jcol[:, 0:1], -1.0, 127.0, ALU.mult, ALU.add),
        waits=[(SG, v_g_iota)], dbg="rev")
    kb.op("vector", lambda e: e.tensor_scalar_add(t_jp1[:, 0:1], t_jcol[:, 0:1], 1.0), dbg="jp1")
    kb.op("vector", lambda e: e.tensor_single_scalar(
        PONE2[:, 0:1], t_jcol[:, 0:1], 64.0, ALU.is_lt), dbg="pmb0")
    kb.op("vector", lambda e: e.tensor_single_scalar(
        PONE2[:, 1:2], t_jcol[:, 0:1], 64.0, ALU.is_ge), dbg="pmb1")
    v_v_rev = kb.op("vector", lambda e: e.tensor_scalar(
        t_ramp[:, :], t_iotaD[:, :], t_jp1[:, 0:1], None, ALU.add), inc=SV, dbg="ramp")

    v_ss = v_s_relu
    for h in range(H0):
        v_vv = kb.op("vector", lambda e, h=h: e.tensor_scalar_mul(
            PH1[:, 0:128], t_reluD[:, :], t_lgbc[:, h:h + 1]),
            waits=[(SS, v_ss), (SV, v_g_lgbc)], inc=SV, dbg=f"mk{h}a")
        v_ss = kb.op("scalar", lambda e: e.activation(PH2[:, 0:128], PH1[:, 0:128], AF.Exp),
                     waits=[(SV, v_vv)], inc=SS, dbg=f"mk{h}b")
        kb.op("vector", lambda e, h=h: e.tensor_mul(
            mask_all[:, h * 128:(h + 1) * 128], PH2[:, 0:128], t_causal[:, :]),
            waits=[(SS, v_ss), (SV, v_v_caus)], dbg=f"mk{h}c")
    v_masks = SV.v

    v_mm_lnch = kb.mm(pb[0][0:1, 0:512], t_lg8[:, :], t_hsel[:, :], True, True,
                      waits=[(SS, v_s_lg8), k_ld0] + bw(0), inc=ST, dbg="lnch")
    v_v_lnch = kb.op("vector", lambda e: e.tensor_copy(t_lnch[:, :], pb[0][0:1, 0:512]),
                     waits=[(ST, v_mm_lnch)], inc=SV, dbg="lnch_ev")
    brel(0, SV, v_v_lnch)
    v_g_lnchb = bcast_rows(t_lnch[0:1, :], t_lnchb[:, :], 512, 7, [(SV, v_v_lnch)], "lnchb")
    v_v_gk1 = kb.op("vector", lambda e: e.tensor_scalar_mul(
        gammaK[:, :], t_lnchb[:, :], t_rev[:, 0:1]),
        waits=[(SV, max(v_g_lnchb, v_v_rev))], inc=SV, dbg="gK1")
    v_s_gK = kb.op("scalar", lambda e: e.activation(gammaK[:, :], gammaK[:, :], AF.Exp),
                   waits=[(SV, v_v_gk1)], inc=SS, dbg="gK2")

    v_mmg = None
    for cc in range(4):
        v_mmg = kb.mm(pb[1][0:128, cc:cc + 1], t_hsel[:, cc * 128:(cc + 1) * 128], t_lg8[:, :],
                      cc == 0, cc == 3,
                      waits=(([(SS, v_s_lg8)] + bw(1)) if cc == 0 else []),
                      inc=(ST if cc == 3 else None), dbg=f"lnG{cc}")
    v_v_lnG = kb.op("vector", lambda e: e.tensor_copy(t_lnG[:, :], pb[1][0:128, 0:4]),
                    waits=[(ST, v_mmg)], inc=SV, dbg="lnG_ev")
    brel(1, SV, v_v_lnG)
    for cc in range(4):
        kb.op("vector", lambda e, cc=cc: e.tensor_scalar_mul(
            gammaQ[:, cc * 128:(cc + 1) * 128], t_ramp[:, :], t_lnG[:, cc:cc + 1]),
            waits=[(SV, max(v_v_lnG, v_v_rev))], dbg=f"gQ{cc}")
    v_v_gq = SV.v
    v_s_gQ = kb.op("scalar", lambda e: e.activation(gammaQ[:, :], gammaQ[:, :], AF.Exp),
                   waits=[(SV, v_v_gq)], inc=SS, dbg="gQ2")

    v_v_gca = kb.op("vector", lambda e: e.tensor_scalar_mul(
        t_scr8[:, :], t_hsel65[:, :], t_gc8[:, 0:1]),
        waits=[(SS, v_s_gc8), k_ld0], inc=SV, dbg="gCa")
    v_mm_gc = kb.mm(pb[2][0:128, 0:260], t_psel[:, :], t_scr8[:, 0:260], True, True,
                    waits=[(SV, v_v_gca), k_ld0] + bw(2), inc=ST, dbg="gCmm0")
    v_v_gC = kb.op("vector", lambda e: e.tensor_copy(gammaC[:, 0:260], pb[2][0:128, 0:260]),
                   waits=[(ST, v_mm_gc)], inc=SV, dbg="gCe0")
    brel(2, SV, v_v_gC)

    v_g_brep = bcast_rows(t_wgb[0:1, :], t_brep[:, :], 512, 6, [k_ld0], "brep")
    bcast_rows(t_flag[0:1, 0:1], fl_contrib[:, 0:1], 1, 7, [k_ld0], "flc")
    v_g_fl = bcast_rows(t_flag[0:1, 1:2], fl_use[:, 0:1], 1, 7, [k_ld0], "flu")

    if nseg < 1:
        return _finish(kb, nc)

    AR.reset(MK_SETUP)
    kb.barrier("rst_setup")
    MK_LN = AR.mark()
    XN1 = AR.alloc("XN1", [128, 8 * TOK], BF16)
    MK_B = AR.mark()
    XC = [AR.alloc(f"XC{i}", [128, TOK], F32) for i in range(2)]
    SQ = AR.alloc("SQ", [128, TOK], F32)
    MK_ROWS = AR.mark()
    m_row = AR.alloc("m_row", [1, TOK], F32)
    s_row = AR.alloc("s_row", [1, TOK], F32)
    mn_row = AR.alloc("mn_row", [1, TOK], F32)
    var_row = AR.alloc("var_row", [1, TOK], F32)
    sd_row = AR.alloc("sd_row", [1, TOK], F32)
    rstd_row = AR.alloc("rstd_row", [1, TOK], F32)
    m_rep = AR.alloc("m_rep", [128, TOK], F32)
    r_rep = AR.alloc("r_rep", [128, TOK], F32)

    v_t = 0
    v_tA = {}
    w_first = bw(0, 1, 2, 3, 4, 5, 6, 7)
    for cc in range(8):
        xc = XC[cc % 2]
        k_ld = LD.dma(xc[:, :], xT.ap()[cc * 128:(cc + 1) * 128, :],
                      waits=[(ST, v_tA.get(cc - 2, 0))], dbg=f"ld_x{cc}")
        v_sq = kb.op("vector", lambda e, xc=xc: e.tensor_mul(SQ[:, :], xc[:, :], xc[:, :]),
                     waits=([k_ld, (ST, v_t)] + (w_first if cc == 0 else [])),
                     inc=SV, dbg=f"lnsq{cc}")
        for tb in range(4):
            kb.mm(pb[tb][0:1, 0:512], ones128[:, :],
                  xc[:, tb * 512:(tb + 1) * 512],
                  cc == 0, cc == 7,
                  waits=([(SV, v_sq)] if tb == 0 else []), dbg=f"lnm{cc}{tb}")
            kb.mm(pb[4 + tb][0:1, 0:512], ones128[:, :], SQ[:, tb * 512:(tb + 1) * 512],
                  cc == 0, cc == 7, dbg=f"lns{cc}{tb}")
        v_t = ST.v
        v_tA[cc] = v_t
    for tb in range(4):
        kb.op("vector", lambda e, tb=tb: e.tensor_copy(
            m_row[:, tb * 512:(tb + 1) * 512], pb[tb][0:1, 0:512]),
            waits=[(ST, v_t)], dbg=f"lnem{tb}")
        kb.op("vector", lambda e, tb=tb: e.tensor_copy(
            s_row[:, tb * 512:(tb + 1) * 512], pb[4 + tb][0:1, 0:512]), dbg=f"lnes{tb}")
    v_ln_ev = SV.v
    for i in range(8):
        brel(i, SV, v_ln_ev)
    kb.op("vector", lambda e: e.tensor_scalar_mul(mn_row[:, :], m_row[:, :], 1.0 / D), dbg="lnr1")
    kb.op("vector", lambda e: e.tensor_scalar_mul(s_row[:, :], s_row[:, :], 1.0 / D), dbg="lnr2")
    kb.op("vector", lambda e: e.tensor_mul(var_row[:, :], mn_row[:, :], mn_row[:, :]), dbg="lnr3")
    v_v_var = kb.op("vector", lambda e: e.tensor_sub(var_row[:, :], s_row[:, :], var_row[:, :]),
                    inc=SV, dbg="lnr4")
    v_s_sd = kb.op("scalar", lambda e: e.activation(
        sd_row[:, :], var_row[:, :], AF.Sqrt, scale=p_eps[0:1, 0:1], bias=p_eps[0:1, 1:2]),
        waits=[(SV, v_v_var)], inc=SS, dbg="lnsqrt")
    v_v_rstd = kb.op("vector", lambda e: e.reciprocal(rstd_row[:, :], sd_row[:, :]),
                     waits=[(SS, v_s_sd)], inc=SV, dbg="lnrecip")
    for tb in range(4):
        b0, b1 = tb % 2, 2 + tb % 2
        kb.mm(pb[b0][0:128, 0:512], ones1r[:, :], mn_row[:, tb * 512:(tb + 1) * 512],
              True, True, waits=[(SV, v_v_rstd)] + bw(b0), dbg=f"lnrm{tb}")
        v_mm_rep = kb.mm(pb[b1][0:128, 0:512], ones1r[:, :], rstd_row[:, tb * 512:(tb + 1) * 512],
                         True, True, waits=bw(b1), inc=ST, dbg=f"lnrr{tb}")
        kb.op("vector", lambda e, tb=tb, b0=b0: e.tensor_copy(
            m_rep[:, tb * 512:(tb + 1) * 512], pb[b0][0:128, 0:512]),
            waits=[(ST, v_mm_rep)], dbg=f"lnre{tb}")
        v_v_rep = kb.op("vector", lambda e, tb=tb, b1=b1: e.tensor_copy(
            r_rep[:, tb * 512:(tb + 1) * 512], pb[b1][0:128, 0:512]), inc=SV, dbg=f"lnrf{tb}")
        brel(b0, SV, v_v_rep); brel(b1, SV, v_v_rep)
    v_vB = {}
    for cc in range(8):
        xc = XC[cc % 2]
        k_ld = LD.dma(xc[:, :], xT.ap()[cc * 128:(cc + 1) * 128, :],
                      waits=[(SV, v_vB.get(cc - 2, 0)),
                             (ST, v_tA[7] if cc < 2 else 0)], dbg=f"ld2_x{cc}")
        kb.op("vector", lambda e, xc=xc: e.tensor_sub(SQ[:, :], xc[:, :], m_rep[:, :]),
              waits=[k_ld], dbg=f"xna{cc}")
        v_vB[cc] = kb.op("vector", lambda e, cc=cc: e.tensor_mul(
            XN1[:, cc * TOK:(cc + 1) * TOK], SQ[:, :], r_rep[:, :]), inc=SV, dbg=f"xnb{cc}")
    v_xn = SV.v
    tap("xn", XN1[:, 0:8 * TOK], [128, 8 * TOK], [(SV, v_xn)], BF16)

    if nseg < 2:
        return _finish(kb, nc)

    AR.reset(MK_ROWS)
    kb.barrier("rst_rows")
    I32S = AR.alloc("I32S", [128, TOK], I32)
    v_vm = kb.op("vector", lambda e: e.tensor_reduce(
        t_locmax[:, 0:1], XN1[:, 0:8 * TOK], AX.X, ALU.max, apply_absolute_value=True),
        waits=[(SV, v_xn)], inc=SV, dbg="xnmax")
    v_gm = preduce(t_locmax[:, 0:1], t_arstage[0:1, 0:1], 6, [(SV, v_vm)], ALU.max, "xnmax_p")
    k_d1 = GD.dma(c_ar_i[0].ap(), t_arstage[:, :], waits=[(SV, v_gm)], dbg="ar1w")
    v_cc1 = kb.op("gpsimd", lambda e: e.collective_compute(
        "AllReduce", ALU.max, replica_groups=ALL8,
        ins=[c_ar_i[0].ap().opt()], outs=[c_ar_o[0].ap().opt()]),
        waits=[k_d1], inc=SC, dbg="AR1")
    k_d1o = GD.dma(t_ar[:, :], c_ar_o[0].ap(), waits=[(SC, v_cc1)], dbg="ar1r")
    v_g_s1 = bcast_rows(t_ar[0:1, 0:1], t_s[0][:, 0:1], 1, 6, [k_d1o], "s1bc")
    kb.op("vector", lambda e: e.tensor_scalar(
        t_s[0][:, 0:1], t_s[0][:, 0:1], 1.0 / QMAX, 1e-12, ALU.mult, ALU.max),
        waits=[(SV, v_g_s1)], dbg="s1a")
    v_v_s1 = kb.op("vector", lambda e: e.reciprocal(t_s[0][:, 1:2], t_s[0][:, 0:1]),
                   inc=SV, dbg="s1b")
    for cc in range(8):
        kb.op("vector", lambda e, cc=cc: e.tensor_scalar_mul(
            SQ[:, :], XN1[:, cc * TOK:(cc + 1) * TOK], t_s[0][:, 1:2]),
            waits=([(SV, v_v_s1)] if cc == 0 else []), dbg=f"xqa{cc}")
        kb.op("vector", lambda e: e.tensor_copy(I32S[:, :], SQ[:, :]), dbg=f"xqb{cc}")
        kb.op("vector", lambda e: e.tensor_copy(SQ[:, :], I32S[:, :]), dbg=f"xqc{cc}")
        kb.op("vector", lambda e, cc=cc: e.tensor_scalar_mul(
            XN2[:, cc * TOK:(cc + 1) * TOK], SQ[:, :], t_s[0][:, 0:1]), dbg=f"xqd{cc}")
    v_xq = SV.v
    tap("xq", XN2[:, 0:8 * TOK], [128, 8 * TOK], [(SV, v_xq)], BF16)
    tap("s1", t_s[0][:, 0:2], [128, 2], [(SV, v_xq)], F32)

    if nseg < 3:
        return _finish(kb, nc)

    AR.reset(MK_B)
    kb.barrier("rst_ln")
    QF = AR.alloc("QF", [128, 4 * TOK], BF16)
    KF = AR.alloc("KF", [128, 4 * TOK], BF16)
    KT = AR.alloc("KT", [128, NCH * CH0], BF16)
    VA = AR.alloc("VA", [128, NCH * H0 * 65], BF16)
    GT = AR.alloc("GT", [128, NCH * CH0], BF16)
    MK_WR = AR.mark()
    WRA = AR.alloc("WRA", [128, 8 * CH0], BF16)
    WRB = AR.alloc("WRB", [128, 8 * CH0], BF16)
    WRC = AR.alloc("WRC", [128, 8 * CH0], BF16)

    v_vone = kb.op("vector", lambda e: e.memset(
        ap3(VA, 0, 128, 64, [[H0 * 65, NCH], [65, H0]]), 1.0), inc=SV, dbg="vone")

    for cc in range(8):
        k_wq = LD.dma(WRA[:, cc * CH0:(cc + 1) * CH0],
                      d_wq.ap()[cc * 128:(cc + 1) * 128, :],
                      waits=([k_cast] if cc == 0 else []), dbg="ldwq", group=(cc > 0))
    for cc in range(8):
        k_wk = LD.dma(WRB[:, cc * CH0:(cc + 1) * CH0],
                      d_wk.ap()[cc * 128:(cc + 1) * 128, :], dbg="ldwk", group=(cc > 0))

    def phi_evac(bank, dst_ap, w_extra, dbg):
        if not KPHI:
            v3 = kb.op("vector", lambda e, b=bank, d=dst_ap: e.tensor_copy(
                d, pb[b][0:128, 0:512]), waits=w_extra, inc=SV, dbg=dbg + "cp")
            return v3
        v1 = kb.op("vector", lambda e, b=bank: e.tensor_scalar_min(
            PH1[:, 0:512], pb[b][0:128, 0:512], 0.0), waits=w_extra, inc=SV, dbg=dbg + "a")
        v2 = kb.op("scalar", lambda e: e.activation(PH2[:, 0:512], PH1[:, 0:512], AF.Exp),
                   waits=[(SV, v1)], inc=SS, dbg=dbg + "b")
        v3 = kb.op("vector", lambda e, b=bank, d=dst_ap: e.scalar_tensor_tensor(
            d, pb[b][0:128, 0:512], 0.0, PH2[:, 0:512], ALU.max, ALU.add),
            waits=[(SS, v2)], inc=SV, dbg=dbg + "c")
        return v3

    bq = 0
    for (wring, k_w, dst, nm) in ([(WRA, k_wq, QF, "q"), (WRB, k_wk, KF, "k")][:SUB3]):
        for cc in range(4):
            for tb in range(4):
                if cc * 4 + tb >= KGRP:
                    continue
                bank = 0 if KBANK0 else bq % 4
                bq += 1
                for dk in range(8):
                    kb.mm(pb[bank][0:128, 0:512],
                          wring[:, dk * CH0 + cc * 128:dk * CH0 + (cc + 1) * 128],
                          XN1[:, dk * TOK + tb * 512:dk * TOK + (tb + 1) * 512],
                          dk == 0, dk == 7,
                          waits=(([k_w, (SV, v_xn)] + bw(bank)) if dk == 0 else []),
                          inc=(ST if (KNONOP and dk == 7) else None),
                          dbg=f"{nm}{cc}{tb}")
                v_mmx = ST.v
                v3 = phi_evac(bank, dst[:, cc * TOK + tb * 512:cc * TOK + (tb + 1) * 512],
                              [(ST, v_mmx)], f"{nm}e{cc}{tb}")
                brel(bank, SV, v3)
    v_guard = SV.v
    for cc in range(8):
        k_wv = LD.dma(WRA[:, cc * CH0:(cc + 1) * CH0],
                      d_wv.ap()[cc * 128:(cc + 1) * 128, :],
                      waits=([(SV, v_guard)] if cc == 0 else []), dbg="ldwv", group=(cc > 0))
    for cc in range(8):
        k_wg = LD.dma(WRC[:, cc * CH0:(cc + 1) * CH0],
                      d_wg.ap()[cc * 128:(cc + 1) * 128, :], dbg="ldwg", group=(cc > 0))

    for tc in (range(NCH) if SUB3 >= 3 else []):
        bank = bq % 4; bq += 1
        for dk in range(8):
            kb.mm(pb[bank][0:128, 0:512],
                  XN1[:, dk * TOK + tc * 128:dk * TOK + (tc + 1) * 128],
                  WRB[:, dk * CH0:(dk + 1) * CH0],
                  dk == 0, dk == 7, waits=(bw(bank) if dk == 0 else []), dbg=f"kt{tc}")
        v_mmx = ST.v
        v3 = phi_evac(bank, KT[:, tc * CH0:(tc + 1) * CH0], [(ST, v_mmx)], f"kte{tc}")
        brel(bank, SV, v3)

        bank = bq % 4; bq += 1
        for dk in range(8):
            kb.mm(pb[bank][0:128, 0:512],
                  XN1[:, dk * TOK + tc * 128:dk * TOK + (tc + 1) * 128],
                  WRA[:, dk * CH0:(dk + 1) * CH0],
                  dk == 0, dk == 7,
                  waits=(([k_wv] + bw(bank)) if dk == 0 else []), dbg=f"v{tc}")
        v_mmx = ST.v
        v3 = kb.op("vector", lambda e, b=bank, tc=tc: e.tensor_copy(
            ap3(VA, 0, 128, tc * H0 * 65, [[65, H0], [1, 64]]), pb[b][0:128, 0:512]),
            waits=[(ST, v_mmx), (SV, v_vone)], inc=SV, dbg=f"ve{tc}")
        brel(bank, SV, v3)

        bank = bq % 4; bq += 1
        for dk in range(8):
            kb.mm(pb[bank][0:128, 0:512],
                  XN1[:, dk * TOK + tc * 128:dk * TOK + (tc + 1) * 128],
                  WRC[:, dk * CH0:(dk + 1) * CH0],
                  dk == 0, dk == 7,
                  waits=(([k_wg] + bw(bank)) if dk == 0 else []), dbg=f"g{tc}")
        v_mmx = ST.v
        v1 = kb.op("vector", lambda e, b=bank: e.tensor_add(
            PH1[:, 0:512], pb[b][0:128, 0:512], t_brep[:, :]),
            waits=[(ST, v_mmx), (SV, v_g_brep)], inc=SV, dbg=f"ge{tc}a")
        brel(bank, SV, v1)
        v2 = kb.op("scalar", lambda e, tc=tc: e.activation(
            GT[:, tc * CH0:(tc + 1) * CH0], PH1[:, 0:512], AF.Sigmoid),
            waits=[(SV, v1)], inc=SS, dbg=f"ge{tc}b")
        kb.wait_only("vector", [(SS, v2)], dbg=f"ge{tc}c")
    v_proj = SV.v
    v_proj_s = SS.v
    tap("qf", QF[:, 0:4 * TOK], [128, 4 * TOK], [(SV, v_proj)], BF16)
    tap("kf", KF[:, 0:4 * TOK], [128, 4 * TOK], [(SV, v_proj)], BF16)
    tap("kt", KT[:, 0:NCH * CH0], [128, NCH * CH0], [(SV, v_proj)], BF16)
    tap("va", VA[:, 0:NCH * H0 * 65], [128, NCH * H0 * 65], [(SV, v_proj)], BF16)
    tap("gt", GT[:, 0:NCH * CH0], [128, NCH * CH0], [(SS, v_proj_s)], BF16)

    if nseg < 4:
        return _finish(kb, nc)

    AR.reset(MK_WR)
    kb.barrier("rst_wr")
    MK_SCAN = AR.mark()
    QG = AR.alloc("QG", [128, 512], BF16)
    KG = AR.alloc("KG", [128, 512], BF16)
    QZE = AR.alloc("QZE", [128, 512], BF16)
    QZO = AR.alloc("QZO", [128, 512], BF16)
    QGE = AR.alloc("QGE", [128, 512], BF16)
    QGO = AR.alloc("QGO", [128, 512], BF16)
    MATB = AR.alloc("MATB", [128, H0 * 128], BF16)
    SSUM = AR.alloc("SSUM", [128, 4 * 65], F32)
    SEA = AR.alloc("SEA", [128, 4 * 65], F32)
    SEB = AR.alloc("SEB", [128, 4 * 65], F32)
    SRV = AR.alloc("SRV", [128, 4 * 65], F32)
    DENB = AR.alloc("DENB", [128, 8], F32)
    DENR = AR.alloc("DENR", [128, 8], F32)
    YG = AR.alloc("YG", [128, CH0], BF16)
    TMPY = AR.alloc("TMPY", [128, CH0], F32)
    SE16 = AR.alloc("SE16", [128, 4 * 65], BF16)

    last_state_mm = [0]

    def state_mms(tc, w_extra, dbg):
        v_kg = kb.op("vector", lambda e, tc=tc: e.tensor_mul(
            KG[:, 0:512], KT[:, tc * CH0:(tc + 1) * CH0], gammaK[:, :]),
            waits=list(w_extra) + [(SS, v_s_gK), (ST, last_state_mm[0]), (SV, v_proj)],
            inc=SV, dbg=dbg + "kg")
        for h in range(H0):
            pp = (h % 2) * 64
            bank = 4 + h % 2
            cg = (h // 2) * 65
            kb.mm(pb[bank][pp:pp + 64, cg:cg + 65],
                  KG[:, h * 64:(h + 1) * 64],
                  VA[:, tc * H0 * 65 + h * 65:tc * H0 * 65 + (h + 1) * 65],
                  h < 2, h >= 6,
                  waits=(([(SV, v_kg)] + bw(4, 5)) if h == 0 else []),
                  dbg=dbg + f"s{h}")
        v_mm = ST.v
        last_state_mm[0] = v_mm
        kb.op("vector", lambda e: e.tensor_copy(SSUM[0:64, 0:260], pb[4][0:64, 0:260]),
              waits=[(ST, v_mm)], dbg=dbg + "e0")
        v_ev = kb.op("vector", lambda e: e.tensor_copy(SSUM[64:128, 0:260], pb[5][64:128, 0:260]),
                     inc=SV, dbg=dbg + "e1")
        brel(4, SV, v_ev); brel(5, SV, v_ev)
        return v_ev

    v_z = kb.op("vector", lambda e: e.memset(SEA[:, 0:260], 0.0), inc=SV, dbg="seaz")
    cur, nxt = SEA, SEB
    v_ch = v_z
    for tc in range(NCH):
        v_ev = state_mms(tc, [], f"c1_{tc}")
        kb.op("vector", lambda e, c=cur, n=nxt: e.tensor_mul(
            n[:, 0:260], c[:, 0:260], gammaC[:, 0:260]),
            waits=[(SV, max(v_ev, v_ch)), (SV, v_v_gC)], dbg=f"ch{tc}a")
        v_ch = kb.op("vector", lambda e, n=nxt: e.tensor_add(
            n[:, 0:260], n[:, 0:260], SSUM[:, 0:260]), inc=SV, dbg=f"ch{tc}b")
        cur, nxt = nxt, cur
    v_ctr = kb.op("vector", lambda e, c=cur: e.tensor_scalar_mul(
        SRV[:, 0:260], c[:, 0:260], fl_contrib[:, 0:1]),
        waits=[(SV, max(v_ch, v_g_fl))], inc=SV, dbg="ctr")
    k_dst = GD.dma(c_st_i.ap(), SRV[0:128, 0:260], waits=[(SV, v_ctr)], dbg="stw")
    v_ccs = kb.op("gpsimd", lambda e: e.collective_compute(
        "AllReduce", ALU.add, replica_groups=PAIRS,
        ins=[c_st_i.ap().opt()], outs=[c_st_o.ap().opt()]),
        waits=[k_dst], inc=SC, dbg="ARst")
    k_dsr = GD.dma(SRV[0:128, 0:260], c_st_o.ap(), waits=[(SC, v_ccs)], dbg="str")
    v_se0 = kb.op("vector", lambda e: e.tensor_scalar_mul(
        SEA[:, 0:260], SRV[:, 0:260], fl_use[:, 0:1]),
        waits=[k_dsr, (SV, v_g_fl)], inc=SV, dbg="se0")

    cur, nxt = SEA, SEB
    v_chain = v_se0
    v_num_prev = 0
    k_tp = (None, 0)
    for tc in range(NCH):
        kb.op("vector", lambda e, tc=tc: e.tensor_scalar_mul(
            QZE[:, 0:512], ap3(QF, 0, 128, tc * 128, [[TOK, 4], [1, 128]]), PONE2[:, 0:1]),
            waits=[(SV, v_proj)], dbg=f"qze{tc}")
        v_qz = kb.op("vector", lambda e, tc=tc: e.tensor_scalar_mul(
            QZO[:, 0:512], ap3(QF, 0, 128, tc * 128, [[TOK, 4], [1, 128]]), PONE2[:, 1:2]),
            inc=SV, dbg=f"qzo{tc}")
        for h in range(H0):
            bank = h // 4
            cc = h // 2
            QZ = QZE if h % 2 == 0 else QZO
            kb.mm(pb[bank][0:128, (h % 4) * 128:(h % 4) * 128 + 128],
                  KF[0:128, cc * TOK + tc * 128:cc * TOK + (tc + 1) * 128],
                  QZ[0:128, cc * 128:(cc + 1) * 128],
                  True, True,
                  waits=(([(SV, v_qz)] + bw(bank)) if h % 4 == 0 else []),
                  dbg=f"at{tc}_{h}")
        v_at = ST.v
        kb.op("vector", lambda e: e.tensor_mul(
            MATB[:, 0:512], pb[0][0:128, 0:512], mask_all[:, 0:512]),
            waits=[(ST, v_at), (SV, v_masks)], dbg=f"ms{tc}0")
        v_msk = kb.op("vector", lambda e: e.tensor_mul(
            MATB[:, 512:1024], pb[1][0:128, 0:512], mask_all[:, 512:1024]),
            inc=SV, dbg=f"ms{tc}1")
        brel(0, SV, v_msk); brel(1, SV, v_msk)
        if KSC4 < 3:
            continue
        kb.op("vector", lambda e, tc=tc: e.tensor_mul(
            QG[:, 0:512], ap3(QF, 0, 128, tc * 128, [[TOK, 4], [1, 128]]), gammaQ[:, :]),
            waits=[(SS, v_s_gQ), (ST, v_num_prev)], dbg=f"qg{tc}")
        kb.op("vector", lambda e: e.tensor_scalar_mul(
            QGE[:, 0:512], QG[:, 0:512], PONE2[:, 0:1]), dbg=f"qge{tc}")
        v_qg = kb.op("vector", lambda e: e.tensor_scalar_mul(
            QGO[:, 0:512], QG[:, 0:512], PONE2[:, 1:2]), inc=SV, dbg=f"qgo{tc}")
        v_se16 = kb.op("vector", lambda e, c=cur: e.tensor_copy(SE16[:, 0:260], c[:, 0:260]),
                       waits=[(SV, v_chain), (ST, v_num_prev)], inc=SV, dbg=f"se16_{tc}")
        for h in range(H0):
            bank = 2 + h // 4
            kb.mm(pb[bank][0:128, (h % 4) * 65:(h % 4) * 65 + 65],
                  MATB[:, h * 128:(h + 1) * 128],
                  VA[:, tc * H0 * 65 + h * 65:tc * H0 * 65 + (h + 1) * 65],
                  h % 4 == 0, False,
                  waits=(([(SV, v_msk)] + bw(bank)) if h % 4 == 0 else []), dbg=f"in{tc}_{h}")
        for h in range(H0):
            bank = 2 + h // 4
            QGZ = QGE if h % 2 == 0 else QGO
            kb.mm(pb[bank][0:128, (h % 4) * 65:(h % 4) * 65 + 65],
                  QGZ[0:128, (h // 2) * 128:(h // 2 + 1) * 128],
                  SE16[0:128, (h // 2) * 65:(h // 2) * 65 + 65],
                  False, h % 4 == 3,
                  waits=([(SV, max(v_qg, v_se16))] if h == 0 else []), dbg=f"ir{tc}_{h}")
        v_num = ST.v
        v_num_prev = v_num
        if KSC4 < 4:
            brel(2, ST, v_num); brel(3, ST, v_num)
            continue
        for bi in range(2):
            kb.op("vector", lambda e, b=2 + bi, bi=bi: e.tensor_scalar_add(
                DENB[:, bi * 4:bi * 4 + 4],
                bass.AP(pb[b], 64, [[512, 128], [65, 4]]), EPS_ATT),
                waits=[(ST, v_num)], dbg=f"den{tc}{bi}")
        v_den = kb.op("vector", lambda e: e.reciprocal(DENR[:, 0:8], DENB[:, 0:8]),
                      inc=SV, dbg=f"denr{tc}")
        for bi in range(2):
            kb.op("vector", lambda e, b=2 + bi, bi=bi: e.tensor_mul(
                TMPY[:, bi * 256:bi * 256 + 256],
                bass.AP(pb[b], 0, [[512, 128], [65, 4], [1, 64]]),
                bass.AP(DENR, bi * 4, [[8, 128], [1, 4], [0, 64]])),
                waits=[(SV, v_den)], dbg=f"y{tc}{bi}")
        v_y = SV.v
        brel(2, SV, v_y); brel(3, SV, v_y)
        v_yg = kb.op("vector", lambda e, tc=tc: e.tensor_mul(
            YG[:, 0:512], TMPY[:, 0:512], GT[:, tc * CH0:(tc + 1) * CH0]),
            waits=[(SV, v_y), (SS, v_proj_s), k_tp], inc=SV, dbg=f"yg{tc}")
        if KSC4 < 5:
            continue
        for cc in range(4):
            k_tp = TP.dma(YGT[:, cc * TOK + tc * 128:cc * TOK + (tc + 1) * 128],
                          YG[:, cc * 128:(cc + 1) * 128],
                          waits=([(SV, v_yg)] if cc == 0 else []), transpose=True,
                          dbg=f"tp{tc}{cc}", group=(cc > 0))
        v_ev = state_mms(tc, [], f"c2_{tc}")
        kb.op("vector", lambda e, c=cur, n=nxt: e.tensor_mul(
            n[:, 0:260], c[:, 0:260], gammaC[:, 0:260]), waits=[(SV, v_ev)], dbg=f"cb{tc}a")
        v_chain = kb.op("vector", lambda e, n=nxt: e.tensor_add(
            n[:, 0:260], n[:, 0:260], SSUM[:, 0:260]), inc=SV, dbg=f"cb{tc}b")
        cur, nxt = nxt, cur
    v_scan = SV.v
    tap("ygt", YGT[:, 0:4 * TOK], [128, 4 * TOK], [k_tp, (SV, v_scan)], BF16)

    if nseg < 5:
        return _finish(kb, nc)

    AR.reset(MK_LN)
    kb.barrier("rst_s5")
    MK_S5 = AR.mark()
    GFF = AR.alloc("GFF", [128, 8 * TOK], BF16)
    MK_S5B = AR.mark()
    WQ0 = AR.alloc("WQ0", [128, FF], F32)
    WQ1 = AR.alloc("WQ1", [128, FF], F32)
    SWT = AR.alloc("SWT", [128, 96], F32)
    SWR = AR.alloc("SWR", [128, 96], F32)
    WQB = AR.alloc("WQB", [128, FF], BF16)
    RND = AR.alloc("RND", [128, FF], F32)
    SQA = AR.alloc("SQA", [128, FF], F32)
    SQB = AR.alloc("SQB", [128, FF], F32)
    I32B = AR.alloc("I32B", [128, FF], I32)
    UBB = AR.alloc("UBB", [128, TOK], BF16)
    TBB = AR.alloc("TBB", [128, TOK], BF16)
    TB5 = AR.alloc("TB5", [128, 512], BF16)
    OB = AR.alloc("OB", [128, 512], BF16)
    WUT = AR.alloc("WUT", [128, 8 * 128], BF16)
    WUT2 = AR.alloc("WUT2", [128, 8 * 128], BF16)
    WTT = AR.alloc("WTT", [128, FF], BF16)
    WTT2 = AR.alloc("WTT2", [128, FF], BF16)
    t_mx = AR.alloc("t_mx", [128, 4], F32)
    t_mxs = AR.alloc("t_mxs", [128, 1], F32)
    t_c1 = AR.alloc("t_c1", [128, 1], F32)

    kb.op("vector", lambda e: e.memset(t_mx[:, :], 0.0), dbg="mx0")

    def lpbq(w_in, d_out, nch, dbg, prev_tok=None):
        wid = w_in.shape[1]
        nblk = wid // 32
        toks = []
        v_done = {}
        for cch in range(nch):
            wq_t = WQ0 if cch % 2 == 0 else WQ1
            k_l = LD.dma(wq_t[:, 0:wid], w_in.ap()[cch * 128:(cch + 1) * 128, :],
                         waits=[(SV, v_done.get(cch - 2, SV.v if cch < 2 else 0))],
                         dbg=dbg + f"l{cch}")
            kb.op("vector", lambda e, w=wq_t, nb=nblk, wid=wid: e.tensor_reduce(
                SWT[:, 0:nb], ap3(w, 0, 128, 0, [[32, nb], [1, 32]]), AX.X, ALU.max,
                apply_absolute_value=True), waits=[k_l], dbg=dbg + f"a{cch}")
            kb.op("vector", lambda e, nb=nblk: e.tensor_scalar(
                SWT[:, 0:nb], SWT[:, 0:nb], 1.0 / 7.0, 1e-12, ALU.mult, ALU.max),
                dbg=dbg + f"s{cch}")
            kb.op("vector", lambda e, nb=nblk: e.reciprocal(SWR[:, 0:nb], SWT[:, 0:nb]),
                  dbg=dbg + f"r{cch}")
            kb.op("vector", lambda e, w=wq_t, nb=nblk, wid=wid: e.tensor_mul(
                SQA[:, 0:wid], w[:, 0:wid], ap3(SWR, 0, 128, 0, [[1, nb], [0, 32]])),
                dbg=dbg + f"d{cch}")
            kb.op("vector", lambda e, wid=wid: e.tensor_copy(I32B[:, 0:wid], SQA[:, 0:wid]),
                  dbg=dbg + f"i{cch}")
            kb.op("vector", lambda e, wid=wid: e.tensor_copy(SQB[:, 0:wid], I32B[:, 0:wid]),
                  dbg=dbg + f"j{cch}")
            kb.op("vector", lambda e, wid=wid: e.tensor_sub(
                RND[:, 0:wid], SQA[:, 0:wid], SQB[:, 0:wid]), dbg=dbg + f"j2{cch}")
            kb.op("vector", lambda e, wid=wid: e.scalar_tensor_tensor(
                SQA[:, 0:wid], RND[:, 0:wid], 0.5, SQB[:, 0:wid], ALU.is_gt, ALU.add),
                dbg=dbg + f"j3{cch}")
            kb.op("vector", lambda e, wid=wid: e.tensor_scalar(
                RND[:, 0:wid], RND[:, 0:wid], -0.5, -1.0, ALU.is_lt, ALU.mult),
                dbg=dbg + f"j4{cch}")
            kb.op("vector", lambda e, wid=wid: e.tensor_add(
                SQA[:, 0:wid], SQA[:, 0:wid], RND[:, 0:wid]), dbg=dbg + f"j5{cch}")
            v_q = kb.op("vector", lambda e, nb=nblk, wid=wid: e.tensor_mul(
                WQB[:, 0:wid], SQA[:, 0:wid], ap3(SWT, 0, 128, 0, [[1, nb], [0, 32]])),
                waits=([toks[-1]] if toks else ([prev_tok] if prev_tok else [])),
                dbg=dbg + f"q{cch}")
            v_done[cch] = v_q
            toks.append(GD.dma(d_out.ap()[cch * 128:(cch + 1) * 128, :], WQB[:, 0:wid],
                               waits=[(SV, v_q)], dbg=dbg + f"w{cch}"))
        return toks

    tok_up = lpbq(wup, d_wupq, 24, "lu")
    tok_gt = lpbq(wgate, d_wgtq, 24, "lg", prev_tok=tok_up[-1])
    tok_dn = lpbq(wdown, d_wdnq, 8, "ld", prev_tok=tok_gt[-1])

    pt_last = [None, None]

    def pretrans(src_d, dst_d, R, C, wtok, dbg):
        nj = C // 128
        ks = None
        for j in range(nj):
            buf = [WTT, WTT2][j % 2]
            k_t = TP.dma(buf[:, 0:R], src_d.ap()[:, j * 128:(j + 1) * 128],
                         waits=[wtok] + ([pt_last[j % 2]] if pt_last[j % 2] else []),
                         transpose=True, dbg=dbg + f"t{j}")
            ks = LD.dma(dst_d.ap()[j * 128:(j + 1) * 128, :], buf[:, 0:R],
                        waits=[k_t] + ([ks] if (j == nj - 1 and ks) else []),
                        dbg=dbg + f"s{j}")
            pt_last[j % 2] = ks
        return ks

    tok_upT = pretrans(d_wupq, d_wupqT, FF, D, tok_up[-1], "ptu")
    tok_gtT = pretrans(d_wgtq, d_wgtqT, FF, D, tok_gt[-1], "ptg")
    tok_dnT = pretrans(d_wdnq, d_wdnqT, D, FF, tok_dn[-1], "ptd")

    def ffn_proj(dT, spill, mx_col, wtok, dbg, prev_sp=None):
        sp_toks = []
        v_last_mx = SV.v
        for cch in range(24):
            wr = WUT if cch % 2 == 0 else WUT2
            kws = None
            for dk in range(8):
                kws = LD.dma(wr[:, dk * 128:(dk + 1) * 128],
                             dT.ap()[dk * 128:dk * 128 + 128, cch * 128:(cch + 1) * 128],
                             waits=([wtok, (ST, ST.v)] if dk == 0 else []),
                             dbg=dbg + f"w{cch}", group=(dk > 0))
            for tb in range(4):
                bank = (cch * 4 + tb) % 4
                for dk in range(8):
                    kb.mm(pb[bank][0:128, 0:512],
                          wr[:, dk * 128:(dk + 1) * 128],
                          XN2[:, dk * TOK + tb * 512:dk * TOK + (tb + 1) * 512],
                          dk == 0, dk == 7,
                          waits=(([kws, (SV, v_xq)] + bw(bank)) if dk == 0 else []),
                          dbg=dbg + f"m{cch}{tb}")
                v_mmu = ST.v
                v_ev = kb.op("vector", lambda e, b=bank, tb=tb: e.tensor_copy(
                    UBB[:, tb * 512:(tb + 1) * 512], pb[b][0:128, 0:512]),
                    waits=[(ST, v_mmu)] +
                          ([sp_toks[cch - 1], (SV, v_last_mx)] if (tb == 0 and cch > 0) else []) +
                          ([prev_sp, (SV, v_last_mx)] if (tb == 0 and cch == 0 and prev_sp) else []),
                    dbg=dbg + f"e{cch}{tb}")
                brel(bank, SV, v_ev)
            sp_toks.append(GD.dma(spill.ap()[cch * 128:(cch + 1) * 128, :], UBB[:, :],
                                  waits=[(SV, v_ev)], dbg=dbg + f"sp{cch}"))
            kb.op("vector", lambda e: e.tensor_reduce(
                t_mxs[:, 0:1], UBB[:, :], AX.X, ALU.max, apply_absolute_value=True),
                dbg=dbg + f"mr{cch}")
            v_last_mx = kb.op("vector", lambda e, c=mx_col: e.tensor_max(
                t_mx[:, c:c + 1], t_mx[:, c:c + 1], t_mxs[:, 0:1]), dbg=dbg + f"mx{cch}")
        return sp_toks

    tok_u = ffn_proj(d_wupqT, d_u, 0, tok_upT, "U")
    tok_g = ffn_proj(d_wgtqT, d_g, 1, tok_gtT, "G", prev_sp=tok_u[-1])
    v_maxug = SV.v

    v_p1 = preduce(t_mx[:, 0:1], t_arstage[0:1, 0:1], 6, [(SV, v_maxug)], ALU.max, "p_u")
    v_p2 = preduce(t_mx[:, 1:2], t_arstage[0:1, 1:2], 6, [(SV, v_p1)], ALU.max, "p_g")
    k_a2 = GD.dma(c_ar_i[1].ap(), t_arstage[:, :], waits=[(SV, v_p2)], dbg="ar2w")
    v_cc2 = kb.op("gpsimd", lambda e: e.collective_compute(
        "AllReduce", ALU.max, replica_groups=ALL8,
        ins=[c_ar_i[1].ap().opt()], outs=[c_ar_o[1].ap().opt()]),
        waits=[k_a2], inc=SC, dbg="AR2")
    k_a2r = GD.dma(t_ar[:, :], c_ar_o[1].ap(), waits=[(SC, v_cc2)], dbg="ar2r")

    def mk_scale(col, tile, wtok, dbg):
        v_b = bcast_rows(t_ar[0:1, col:col + 1], tile[:, 0:1], 1, 6, [wtok], dbg + "b")
        kb.op("vector", lambda e, t=tile: e.tensor_scalar(
            t[:, 0:1], t[:, 0:1], 1.0 / QMAX, 1e-12, ALU.mult, ALU.max),
            waits=[(SV, v_b)], dbg=dbg + "s")
        kb.op("vector", lambda e, t=tile: e.reciprocal(t[:, 1:2], t[:, 0:1]), dbg=dbg + "r")
        kb.op("vector", lambda e, t=tile: e.tensor_copy(t[:, 2:3], t[:, 0:1]), dbg=dbg + "c")
        return kb.op("vector", lambda e, t=tile: e.memset(t[:, 3:4], 0.0), dbg=dbg + "z")

    v_s2 = mk_scale(0, t_s[1], k_a2r, "s2")
    v_s3 = mk_scale(1, t_s[2], k_a2r, "s3")

    v_c34 = kb.op("vector", lambda e: e.tensor_scalar_mul(
        t_c1[:, 0:1], t_s[2][:, 0:1], 1.0 / 65536.0), waits=[(SV, v_s3)], dbg="c34")
    v_t4last = 0
    t4_toks = []
    for cch in range(24):
        k_g = LD.dma(UBB[:, :], d_g.ap()[cch * 128:(cch + 1) * 128, :],
                     waits=[tok_g[cch], (SV, v_t4last)] +
                           ([tok_g[-1], (SV, SV.v)] if cch == 0 else []), dbg=f"Cg{cch}")
        kb.op("vector", lambda e: e.tensor_scalar_mul(SQA[:, 0:TOK], UBB[:, :], t_s[2][:, 1:2]),
              waits=[k_g, (SV, v_s3)], dbg=f"Ca{cch}")
        kb.op("vector", lambda e: e.tensor_copy(I32B[:, 0:TOK], SQA[:, 0:TOK]), dbg=f"Cb{cch}")
        v_r3 = kb.op("vector", lambda e: e.tensor_copy(SQA[:, 0:TOK], I32B[:, 0:TOK]), dbg=f"Cc{cch}")
        v_sg = kb.op("scalar", lambda e: e.activation(
            SQB[:, 0:TOK], SQA[:, 0:TOK], AF.Sigmoid, scale=t_s[2][:, 2:3], bias=t_s[2][:, 3:4]),
            waits=[(SV, v_r3)], dbg=f"Cs{cch}")
        kb.op("vector", lambda e: e.tensor_scalar_mul(SQB[:, 0:TOK], SQB[:, 0:TOK], 65536.0),
              waits=[(SS, v_sg)], dbg=f"Cd{cch}")
        kb.op("vector", lambda e: e.tensor_copy(I32B[:, 0:TOK], SQB[:, 0:TOK]), dbg=f"Ce{cch}")
        kb.op("vector", lambda e: e.tensor_copy(SQB[:, 0:TOK], I32B[:, 0:TOK]), dbg=f"Cf{cch}")
        kb.op("vector", lambda e: e.tensor_scalar_min(SQB[:, 0:TOK], SQB[:, 0:TOK], 65535.0),
              dbg=f"Cg2{cch}")
        v_t4 = kb.op("vector", lambda e: e.scalar_tensor_tensor(
            TBB[:, :], SQA[:, 0:TOK], t_c1[:, 0:1], SQB[:, 0:TOK], ALU.mult, ALU.mult),
            waits=[(SV, v_c34)] + ([t4_toks[cch - 1]] if cch > 0 else []), dbg=f"Ct{cch}")
        t4_toks.append(GD.dma(d_t4.ap()[cch * 128:(cch + 1) * 128, :], TBB[:, :],
                              waits=[(SV, v_t4)], dbg=f"Cw{cch}"))
        kb.op("vector", lambda e: e.tensor_reduce(
            t_mxs[:, 0:1], TBB[:, :], AX.X, ALU.max, apply_absolute_value=True), dbg=f"Cm{cch}")
        v_t4last = kb.op("vector", lambda e: e.tensor_max(
            t_mx[:, 2:3], t_mx[:, 2:3], t_mxs[:, 0:1]), dbg=f"Cx{cch}")

    v_p3 = preduce(t_mx[:, 2:3], t_arstage[0:1, 0:1], 6, [(SV, v_t4last)], ALU.max, "p_t4")
    k_a3 = GD.dma(c_ar_i[2].ap(), t_arstage[:, :], waits=[(SV, v_p3)], dbg="ar3w")
    v_cc3 = kb.op("gpsimd", lambda e: e.collective_compute(
        "AllReduce", ALU.max, replica_groups=ALL8,
        ins=[c_ar_i[2].ap().opt()], outs=[c_ar_o[2].ap().opt()]),
        waits=[k_a3], inc=SC, dbg="AR3")
    k_a3r = GD.dma(t_ar[:, :], c_ar_o[2].ap(), waits=[(SC, v_cc3)], dbg="ar3r")
    v_s4 = mk_scale(0, t_s[3], k_a3r, "s4")

    v_c24 = kb.op("vector", lambda e: e.tensor_mul(
        t_c1[:, 0:1], t_s[1][:, 0:1], t_s[3][:, 0:1]),
        waits=[(SV, max(v_s2, v_s4))], dbg="c24")
    v_t5last = 0
    t5_toks = []
    for cch in range(24):
        k_u = LD.dma(UBB[:, :], d_u.ap()[cch * 128:(cch + 1) * 128, :],
                     waits=[tok_u[cch], (SV, v_t5last)] +
                           ([(SV, SV.v)] if cch == 0 else []), dbg=f"Du{cch}")
        k_t = LD.dma(TBB[:, :], d_t4.ap()[cch * 128:(cch + 1) * 128, :],
                     waits=[t4_toks[cch], (SV, v_t5last)] +
                           ([t5_toks[cch - 1]] if cch > 0 else [(SV, SV.v), t4_toks[-1]]),
                     dbg=f"Dt{cch}")
        kb.op("vector", lambda e: e.tensor_scalar_mul(SQA[:, 0:TOK], UBB[:, :], t_s[1][:, 1:2]),
              waits=[k_u, (SV, v_c24)], dbg=f"Da{cch}")
        kb.op("vector", lambda e: e.tensor_copy(I32B[:, 0:TOK], SQA[:, 0:TOK]), dbg=f"Db{cch}")
        kb.op("vector", lambda e: e.tensor_copy(SQA[:, 0:TOK], I32B[:, 0:TOK]), dbg=f"Dc{cch}")
        kb.op("vector", lambda e: e.tensor_scalar_mul(SQB[:, 0:TOK], TBB[:, :], t_s[3][:, 1:2]),
              waits=[k_t], dbg=f"Dd{cch}")
        kb.op("vector", lambda e: e.tensor_copy(I32B[:, 0:TOK], SQB[:, 0:TOK]), dbg=f"De{cch}")
        kb.op("vector", lambda e: e.tensor_copy(SQB[:, 0:TOK], I32B[:, 0:TOK]), dbg=f"Df{cch}")
        v_t5 = kb.op("vector", lambda e: e.scalar_tensor_tensor(
            TBB[:, :], SQA[:, 0:TOK], t_c1[:, 0:1], SQB[:, 0:TOK], ALU.mult, ALU.mult), dbg=f"Dg{cch}")
        t5_toks.append(GD.dma(d_t5.ap()[cch * 128:(cch + 1) * 128, :], TBB[:, :],
                              waits=[(SV, v_t5)], dbg=f"Dw{cch}"))
        kb.op("vector", lambda e: e.tensor_reduce(
            t_mxs[:, 0:1], TBB[:, :], AX.X, ALU.max, apply_absolute_value=True), dbg=f"Dm{cch}")
        v_t5last = kb.op("vector", lambda e: e.tensor_max(
            t_mx[:, 3:4], t_mx[:, 3:4], t_mxs[:, 0:1]), dbg=f"Dx{cch}")

    v_p4 = preduce(t_mx[:, 3:4], t_arstage[0:1, 0:1], 6, [(SV, v_t5last)], ALU.max, "p_t5")
    k_a4 = GD.dma(c_ar_i[3].ap(), t_arstage[:, :], waits=[(SV, v_p4)], dbg="ar4w")
    v_cc4 = kb.op("gpsimd", lambda e: e.collective_compute(
        "AllReduce", ALU.max, replica_groups=ALL8,
        ins=[c_ar_i[3].ap().opt()], outs=[c_ar_o[3].ap().opt()]),
        waits=[k_a4], inc=SC, dbg="AR4")
    k_a4r = GD.dma(t_ar[:, :], c_ar_o[3].ap(), waits=[(SC, v_cc4)], dbg="ar4r")
    v_s5 = mk_scale(0, t_s[4], k_a4r, "s5")

    for tb in range(4):
        v_sil_prev = SS.v
        for cch in range(24):
            wr = WUT if cch % 2 == 0 else WUT2
            k_w = LD.dma(wr[:, :], d_wdnqT.ap()[cch * 128:(cch + 1) * 128, :],
                         waits=[tok_dnT, (ST, ST.v)], dbg=f"Ew{tb}_{cch}")
            k_5 = LD.dma(TB5[:, :],
                         d_t5.ap()[cch * 128:(cch + 1) * 128, tb * 512:(tb + 1) * 512],
                         waits=[t5_toks[cch], (SV, SV.v)], dbg=f"E5{tb}_{cch}")
            kb.op("vector", lambda e: e.tensor_scalar_mul(
                SQA[:, 0:512], TB5[:, :], t_s[4][:, 1:2]),
                waits=[k_5, (SV, v_s5), (ST, ST.v)], dbg=f"Ea{tb}_{cch}")
            kb.op("vector", lambda e: e.tensor_copy(I32B[:, 0:512], SQA[:, 0:512]),
                  dbg=f"Eb{tb}_{cch}")
            kb.op("vector", lambda e: e.tensor_copy(SQA[:, 0:512], I32B[:, 0:512]),
                  dbg=f"Ec{tb}_{cch}")
            v_o = kb.op("vector", lambda e: e.tensor_scalar_mul(
                OB[:, :], SQA[:, 0:512], t_s[4][:, 0:1]), dbg=f"Ed{tb}_{cch}")
            for dk in range(8):
                w_ = []
                if dk == 0:
                    w_ = [k_w, (SV, v_o), (SS, v_sil_prev)]
                if cch == 0:
                    w_ = w_ + bw(dk)
                kb.mm(pb[dk][0:128, 0:512],
                      wr[:, dk * 128:(dk + 1) * 128], OB[:, :],
                      cch == 0, cch == 23, waits=w_, dbg=f"Em{tb}_{cch}_{dk}")
            v_mme = ST.v
        for dk in range(8):
            v_sil = kb.op("scalar", lambda e, dk=dk, tb=tb: e.activation(
                GFF[:, dk * TOK + tb * 512:dk * TOK + (tb + 1) * 512],
                pb[dk][0:128, 0:512], AF.Silu),
                waits=[(ST, v_mme)], dbg=f"Es{tb}_{dk}")
            brel(dk, SS, v_sil)
    v_gff = SS.v
    tap("gff", GFF[:, 0:8 * TOK], [128, 8 * TOK], [(SS, v_gff)], BF16)
    if "uu" in taps:
        dt_u = nc.dram_tensor("tap_uu", [FF, TOK], BF16, kind="ExternalOutput")
        LD.dma(dt_u.ap(), d_u.ap(), waits=list(tok_u), dbg="tap_uu")
        kb.tapinfo["uu"] = ([FF, TOK], BF16)
    if "t4t" in taps:
        dt_t4 = nc.dram_tensor("tap_t4t", [FF, TOK], BF16, kind="ExternalOutput")
        LD.dma(dt_t4.ap(), d_t4.ap(), waits=list(t4_toks), dbg="tap_t4t")
        kb.tapinfo["t4t"] = ([FF, TOK], BF16)
    if "wupq" in taps:
        dt_wq = nc.dram_tensor("tap_wupq", [FF, D], BF16, kind="ExternalOutput")
        LD.dma(dt_wq.ap(), d_wupq.ap(), waits=[tok_up[-1]], dbg="tap_wupq")
        kb.tapinfo["wupq"] = ([FF, D], BF16)
    if "sc" in taps:
        dt_sc = nc.dram_tensor("tap_sc", [128, 20], F32, kind="ExternalOutput")
        v_scw = SV.v
        for i in range(5):
            LD.dma(dt_sc.ap()[:, 4 * i:4 * (i + 1)], t_s[i][:, 0:4],
                   waits=[(SV, v_scw)], dbg=f"tap_sc{i}", group=(i > 0))
        kb.tapinfo["sc"] = ([128, 20], F32)

    if nseg < 7:
        return _finish(kb, nc)

    AR.reset(MK_S5B)
    kb.barrier("rst_s7")
    WOT = AR.alloc("WOT", [128, 4 * 1024], BF16)
    XR0 = AR.alloc("XR0", [128, TOK], F32)
    XR1 = AR.alloc("XR1", [128, TOK], F32)
    X1B = AR.alloc("X1B", [128, 512], F32)
    k_wo = None
    for cc in range(4):
        k_wo = LD.dma(WOT[:, cc * 1024:(cc + 1) * 1024],
                      d_wo.ap()[cc * 128:(cc + 1) * 128, :],
                      waits=([k_cast] if cc == 0 else []), dbg="ldwo", group=(cc > 0))
    v_x1_done = {}
    x1_toks = []
    for dk in range(8):
        xr = XR0 if dk % 2 == 0 else XR1
        k_x = LD.dma(xr[:, :], xT.ap()[dk * 128:(dk + 1) * 128, :],
                     waits=[(SV, v_x1_done.get(dk - 2, 0))], dbg=f"x2{dk}")
        for tb in range(4):
            bank = (dk * 4 + tb) % 4
            for cc in range(4):
                kb.mm(pb[bank][0:128, 0:512],
                      WOT[:, cc * 1024 + dk * 128:cc * 1024 + (dk + 1) * 128],
                      YGT[:, cc * TOK + tb * 512:cc * TOK + (tb + 1) * 512],
                      cc == 0, cc == 3,
                      waits=(([k_wo, k_tp, (SV, v_scan)] + bw(bank)) if cc == 0 else []),
                      dbg=f"wo{dk}{tb}")
            v_mmo = ST.v
            kb.op("vector", lambda e, b=bank, dk=dk, tb=tb: e.scalar_tensor_tensor(
                X1B[:, 0:512], pb[b][0:128, 0:512], 1.0,
                GFF[:, dk * TOK + tb * 512:dk * TOK + (tb + 1) * 512],
                ALU.mult, ALU.mult),
                waits=[(ST, v_mmo), (SS, v_gff), k_x] +
                      ([x1_toks[-1]] if x1_toks else []), dbg=f"cmb{dk}{tb}a")
            v_x1 = kb.op("vector", lambda e, xr=xr, tb=tb: e.tensor_add(
                X1B[:, 0:512], X1B[:, 0:512], xr[:, tb * 512:(tb + 1) * 512]),
                dbg=f"cmb{dk}{tb}b")
            brel(bank, SV, v_x1)
            x1_toks.append(GD.dma(
                d_x1.ap()[dk * 128:(dk + 1) * 128, tb * 512:(tb + 1) * 512],
                X1B[:, 0:512], waits=[(SV, v_x1)], dbg=f"x1w{dk}{tb}"))
        v_x1_done[dk] = SV.v
    tok_x1 = x1_toks[-1]
    if "x1" in taps:
        dt_x1 = nc.dram_tensor("tap_x1", [D, TOK], F32, kind="ExternalOutput")
        LD.dma(dt_x1.ap(), d_x1.ap(), waits=list(x1_toks), dbg="tap_x1")
        kb.tapinfo["x1"] = ([D, TOK], F32)

    if nseg < 8:
        return _finish(kb, nc)

    AR.reset(MK_S5)
    kb.barrier("rst_s8")
    MK_S8 = AR.mark()
    XC2 = [AR.alloc(f"XC2{i}", [128, TOK], F32) for i in range(2)]
    SQ2b = AR.alloc("SQ2b", [128, TOK], F32)
    m2_row = AR.alloc("m2_row", [1, TOK], F32)
    s2_row = AR.alloc("s2_row", [1, TOK], F32)
    mn2_row = AR.alloc("mn2_row", [1, TOK], F32)
    var2_row = AR.alloc("var2_row", [1, TOK], F32)
    sd2_row = AR.alloc("sd2_row", [1, TOK], F32)
    rstd2_row = AR.alloc("rstd2_row", [1, TOK], F32)
    m2_rep = AR.alloc("m2_rep", [128, TOK], F32)
    r2_rep = AR.alloc("r2_rep", [128, TOK], F32)
    SGN = AR.alloc("SGN", [128, 8 * 1024], BF16)
    WS0 = AR.alloc("WS0", [128, 1024], F32)
    WS1 = AR.alloc("WS1", [128, 1024], F32)
    G1B = AR.alloc("G1B", [128, 512], BF16)

    v_t2 = 0
    v_t2A = {}
    for cc in range(8):
        xc = XC2[cc % 2]
        k_ld = LD.dma(xc[:, :], d_x1.ap()[cc * 128:(cc + 1) * 128, :],
                      waits=[(ST, v_t2A.get(cc - 2, 0))] + list(x1_toks), dbg=f"l2x{cc}")
        v_sq = kb.op("vector", lambda e, xc=xc: e.tensor_mul(SQ2b[:, :], xc[:, :], xc[:, :]),
                     waits=([k_ld, (ST, v_t2)] + (bw(0, 1, 2, 3, 4, 5, 6, 7) if cc == 0 else [])),
                     inc=SV, dbg=f"l2sq{cc}")
        for tb in range(4):
            kb.mm(pb[tb][0:1, 0:512], ones128[:, :], xc[:, tb * 512:(tb + 1) * 512],
                  cc == 0, cc == 7, waits=([(SV, v_sq)] if tb == 0 else []), dbg=f"l2m{cc}{tb}")
            kb.mm(pb[4 + tb][0:1, 0:512], ones128[:, :], SQ2b[:, tb * 512:(tb + 1) * 512],
                  cc == 0, cc == 7, dbg=f"l2s{cc}{tb}")
        v_t2 = ST.v
        v_t2A[cc] = v_t2
    for tb in range(4):
        kb.op("vector", lambda e, tb=tb: e.tensor_copy(
            m2_row[:, tb * 512:(tb + 1) * 512], pb[tb][0:1, 0:512]),
            waits=[(ST, v_t2)], dbg=f"l2em{tb}")
        kb.op("vector", lambda e, tb=tb: e.tensor_copy(
            s2_row[:, tb * 512:(tb + 1) * 512], pb[4 + tb][0:1, 0:512]), dbg=f"l2es{tb}")
    v_ln2ev = SV.v
    for i in range(8):
        brel(i, SV, v_ln2ev)
    kb.op("vector", lambda e: e.tensor_scalar_mul(mn2_row[:, :], m2_row[:, :], 1.0 / D), dbg="l2r1")
    kb.op("vector", lambda e: e.tensor_scalar_mul(s2_row[:, :], s2_row[:, :], 1.0 / D), dbg="l2r2")
    kb.op("vector", lambda e: e.tensor_mul(var2_row[:, :], mn2_row[:, :], mn2_row[:, :]), dbg="l2r3")
    v_v_var2 = kb.op("vector", lambda e: e.tensor_sub(var2_row[:, :], s2_row[:, :], var2_row[:, :]),
                     inc=SV, dbg="l2r4")
    v_s_sd2 = kb.op("scalar", lambda e: e.activation(
        sd2_row[:, :], var2_row[:, :], AF.Sqrt, scale=p_eps[0:1, 0:1], bias=p_eps[0:1, 1:2]),
        waits=[(SV, v_v_var2)], inc=SS, dbg="l2sqrt")
    v_v_rstd2 = kb.op("vector", lambda e: e.reciprocal(rstd2_row[:, :], sd2_row[:, :]),
                      waits=[(SS, v_s_sd2)], inc=SV, dbg="l2recip")
    for tb in range(4):
        b0, b1 = tb % 2, 2 + tb % 2
        kb.mm(pb[b0][0:128, 0:512], ones1r[:, :], mn2_row[:, tb * 512:(tb + 1) * 512],
              True, True, waits=[(SV, v_v_rstd2)] + bw(b0), dbg=f"l2rm{tb}")
        v_mm_rep2 = kb.mm(pb[b1][0:128, 0:512], ones1r[:, :], rstd2_row[:, tb * 512:(tb + 1) * 512],
                          True, True, waits=bw(b1), inc=ST, dbg=f"l2rr{tb}")
        kb.op("vector", lambda e, tb=tb, b0=b0: e.tensor_copy(
            m2_rep[:, tb * 512:(tb + 1) * 512], pb[b0][0:128, 0:512]),
            waits=[(ST, v_mm_rep2)], dbg=f"l2re{tb}")
        v_v_rep2 = kb.op("vector", lambda e, tb=tb, b1=b1: e.tensor_copy(
            r2_rep[:, tb * 512:(tb + 1) * 512], pb[b1][0:128, 0:512]), inc=SV, dbg=f"l2rf{tb}")
        brel(b0, SV, v_v_rep2); brel(b1, SV, v_v_rep2)
    v_vB2 = {}
    for cc in range(8):
        xc = XC2[cc % 2]
        k_ld = LD.dma(xc[:, :], d_x1.ap()[cc * 128:(cc + 1) * 128, :],
                      waits=[(SV, v_vB2.get(cc - 2, 0)), (ST, v_t2A[7] if cc < 2 else 0)],
                      dbg=f"l2b{cc}")
        kb.op("vector", lambda e, xc=xc: e.tensor_sub(SQ2b[:, :], xc[:, :], m2_rep[:, :]),
              waits=[k_ld], dbg=f"x2a{cc}")
        v_vB2[cc] = kb.op("vector", lambda e, cc=cc: e.tensor_mul(
            XN2[:, cc * TOK:(cc + 1) * TOK], SQ2b[:, :], r2_rep[:, :]), inc=SV, dbg=f"x2b{cc}")
    v_xn1 = SV.v
    tap("xn1", XN2[:, 0:8 * TOK], [128, 8 * TOK], [(SV, v_xn1)], BF16)

    k_agw = None
    for cc in range(8):
        k_agw = GD.dma(c_ag_i.ap()[cc * 128:(cc + 1) * 128, :], XN2[:, cc * TOK:(cc + 1) * TOK],
                       waits=([(SV, v_xn1)] if cc == 0 else []), dbg=f"agw{cc}", group=(cc > 0))
    v_ccag = kb.op("gpsimd", lambda e: e.collective_compute(
        "AllGather", ALU.bypass, replica_groups=PAIRS,
        ins=[c_ag_i.ap().opt()], outs=[c_ag_o.ap().opt()]),
        waits=[k_agw], inc=SC, dbg="AG")

    v_sgn_done = {}
    for dk in range(8):
        ws = WS0 if dk % 2 == 0 else WS1
        k_wf = LD.dma(ws[:, :], l1_wfT.ap()[dk * 128:(dk + 1) * 128, :],
                      waits=[(SV, v_sgn_done.get(dk - 2, 0))], dbg=f"sgl{dk}")
        kb.op("vector", lambda e, ws=ws: e.tensor_single_scalar(
            SQ2b[:, 0:1024], ws[:, :], 0.0, ALU.is_ge), waits=[k_wf], dbg=f"sga{dk}")
        v_sgn_done[dk] = kb.op("vector", lambda e, dk=dk: e.tensor_scalar(
            SGN[:, dk * 1024:(dk + 1) * 1024], SQ2b[:, 0:1024], 2.0, -1.0,
            ALU.mult, ALU.add), inc=SV, dbg=f"sgb{dk}")
    v_sgn = SV.v
    g1_toks = []
    for cho in range(8):
        for tb in range(4):
            bank = (cho * 4 + tb) % 4
            for dk in range(8):
                kb.mm(pb[bank][0:128, 0:512],
                      SGN[:, dk * 1024 + cho * 128:dk * 1024 + (cho + 1) * 128],
                      XN2[:, dk * TOK + tb * 512:dk * TOK + (tb + 1) * 512],
                      dk == 0, dk == 7,
                      waits=(([(SV, max(v_sgn, v_xn1))] + bw(bank)) if dk == 0 else []),
                      dbg=f"f1{cho}{tb}")
            v_mmf = ST.v
            kb.op("vector", lambda e, b=bank, cho=cho: e.tensor_scalar(
                SQ2b[:, 0:512], pb[b][0:128, 0:512],
                t_l1sb[:, 2 * cho:2 * cho + 1], t_l1sb[:, 2 * cho + 1:2 * cho + 2],
                ALU.mult, ALU.add),
                waits=[(ST, v_mmf), (SS, SS.v)], dbg=f"f1e{cho}{tb}a")
            v_cl = kb.op("vector", lambda e: e.tensor_scalar(
                SQ2b[:, 0:512], SQ2b[:, 0:512], 100.0, -100.0, ALU.min, ALU.max),
                inc=SV, dbg=f"f1e{cho}{tb}b")
            brel(bank, SV, v_cl)
            v_si = kb.op("scalar", lambda e: e.activation(G1B[:, :], SQ2b[:, 0:512], AF.Silu),
                         waits=[(SV, v_cl)] + ([g1_toks[-1]] if g1_toks else []),
                         inc=SS, dbg=f"f1e{cho}{tb}c")
            g1_toks.append(GD.dma(
                d_g1.ap()[cho * 128:(cho + 1) * 128, tb * 512:(tb + 1) * 512],
                G1B[:, :], waits=[(SS, v_si)], dbg=f"g1w{cho}{tb}"))
    tok_g1 = g1_toks[-1]

    if nseg < 9:
        return _finish(kb, nc)

    AR.reset(MK_S8)
    kb.barrier("rst_s9")
    MK_S9 = AR.mark()
    QF1 = AR.alloc("QF1", [128, 4 * 4096], BF16)
    VA1 = AR.alloc("VA1", [128, 32 * 520], BF16)
    MK_W1R = AR.mark()
    XNF = AR.alloc("XNF", [128, 16 * TOK], BF16)
    W1R = AR.alloc("W1R", [128, 8 * 512], BF16)
    KF1 = XN2

    k_agl = None
    for hf in range(2):
        for dk in range(8):
            k_agl = LD.dma(XNF[:, hf * 8 * TOK + dk * TOK:hf * 8 * TOK + (dk + 1) * TOK],
                           c_ag_o.ap()[hf * D + dk * 128:hf * D + (dk + 1) * 128, :],
                           waits=([(SC, v_ccag)] if (hf == 0 and dk == 0) else []),
                           dbg=f"agl{hf}{dk}", group=not (hf == 0 and dk == 0))
    v_vone1 = kb.op("vector", lambda e: e.memset(
        ap3(VA1, 0, 128, 64, [[65, 8 * 32]]), 1.0), inc=SV, dbg="vone1")

    k_cast1 = None
    for dd, ss_ in [(d_l1wq, l1_wqT), (d_l1wk, l1_wkT), (d_l1wv, l1_wvT), (d_l1wo, l1_woT)]:
        k_cast1 = GD.dma(dd.ap(), ss_.ap(), dbg="cast_w1", group=(dd is not d_l1wq))

    def l1_proj_fm(d_w, dst, nm, scale=None):
        k_w = None
        for dk in range(8):
            k_w = LD.dma(W1R[:, dk * 512:(dk + 1) * 512],
                         d_w.ap()[dk * 128:(dk + 1) * 128, :],
                         waits=([k_cast1, (ST, ST.v)] if dk == 0 else []),
                         dbg=nm + f"w{dk}", group=(dk > 0))
        for cc in range(4):
            for hf in range(2):
                for tb in range(4):
                    bank = (cc * 8 + hf * 4 + tb) % 4
                    for dk in range(8):
                        kb.mm(pb[bank][0:128, 0:512],
                              W1R[:, dk * 512 + cc * 128:dk * 512 + (cc + 1) * 128],
                              XNF[:, hf * 8 * TOK + dk * TOK + tb * 512:
                                  hf * 8 * TOK + dk * TOK + (tb + 1) * 512],
                              dk == 0, dk == 7,
                              waits=(([k_w, k_agl] + bw(bank)) if dk == 0 else []),
                              dbg=nm + f"{cc}{hf}{tb}")
                    v_mm1 = ST.v
                    col = cc * 4096 + hf * TOK + tb * 512
                    if scale is None:
                        v_e1 = kb.op("vector", lambda e, b=bank, col=col: e.tensor_copy(
                            dst[:, col:col + 512], pb[b][0:128, 0:512]),
                            waits=[(ST, v_mm1)], inc=SV, dbg=nm + f"e{cc}{hf}{tb}")
                    else:
                        v_e1 = kb.op("vector", lambda e, b=bank, col=col: e.tensor_scalar_mul(
                            dst[:, col:col + 512], pb[b][0:128, 0:512], scale),
                            waits=[(ST, v_mm1)], inc=SV, dbg=nm + f"e{cc}{hf}{tb}")
                    brel(bank, SV, v_e1)

    l1_proj_fm(d_l1wq, QF1, "q1", scale=0.125)
    l1_proj_fm(d_l1wk, KF1, "k1")
    k_wv1 = None
    for dk in range(8):
        k_wv1 = LD.dma(W1R[:, dk * 512:(dk + 1) * 512],
                       d_l1wv.ap()[dk * 128:(dk + 1) * 128, :],
                       waits=([k_cast1, (ST, ST.v)] if dk == 0 else []),
                       dbg=f"v1w{dk}", group=(dk > 0))
    for tcg in range(32):
        hf, tc = tcg // 16, tcg % 16
        bank = tcg % 4
        for dk in range(8):
            kb.mm(pb[bank][0:128, 0:512],
                  XNF[:, hf * 8 * TOK + dk * TOK + tc * 128:hf * 8 * TOK + dk * TOK + (tc + 1) * 128],
                  W1R[:, dk * 512:(dk + 1) * 512],
                  dk == 0, dk == 7,
                  waits=(([k_wv1, k_agl] + bw(bank)) if dk == 0 else []), dbg=f"v1{tcg}")
        v_mm1 = ST.v
        v_e1 = kb.op("vector", lambda e, b=bank, tcg=tcg: e.tensor_copy(
            ap3(VA1, 0, 128, tcg * 520, [[65, 8], [1, 64]]), pb[b][0:128, 0:512]),
            waits=[(ST, v_mm1), (SV, v_vone1)], inc=SV, dbg=f"v1e{tcg}")
        brel(bank, SV, v_e1)
    v_proj1 = SV.v
    tap("qf1", QF1[:, 0:4 * 4096], [128, 4 * 4096], [(SV, v_proj1)], BF16)
    tap("va1", VA1[:, 0:32 * 520], [128, 32 * 520], [(SV, v_proj1)], BF16)

    AR.reset(MK_W1R)
    kb.barrier("rst_nrm")
    PC = AR.alloc("PC", [128, 16], F32)
    MK_NRM = AR.mark()
    SQW = AR.alloc("SQW", [128, 4096], F32)
    nrm_row = AR.alloc("nrm_row", [1, 512], F32)
    mq8 = AR.alloc("mq8", [1, 16], F32)
    crow = AR.alloc("crow", [1, 8], F32)
    kb.op("vector", lambda e: e.memset(mq8[:, :], 0.0), dbg="nq0")
    for src_t, base_col in [(QF1, 0), (KF1, 8)]:
        for cc in range(4):
            v_sqw = kb.op("vector", lambda e, s=src_t, cc=cc: e.tensor_mul(
                SQW[:, :], s[:, cc * 4096:(cc + 1) * 4096], s[:, cc * 4096:(cc + 1) * 4096]),
                waits=[(SV, v_proj1), (ST, ST.v)], inc=SV, dbg=f"nsq{base_col}{cc}")
            for par in range(2):
                h = cc * 2 + par
                pp = par * 64
                for qb in range(8):
                    kb.mm(pb[6][0:1, 0:512], PONE2[0:128, par:par + 1],
                          SQW[0:128, qb * 512:(qb + 1) * 512],
                          True, True, waits=[(SV, v_sqw)] + bw(6), dbg=f"nm{base_col}{h}{qb}")
                    v_nm = ST.v
                    v_ncp = kb.op("vector", lambda e: e.tensor_copy(nrm_row[:, :], pb[6][0:1, 0:512]),
                          waits=[(ST, v_nm)], dbg=f"ne{base_col}{h}{qb}")
                    brel(6, SV, v_ncp)
                    kb.op("vector", lambda e: e.tensor_reduce(
                        nrm_row[0:1, 0:1], nrm_row[0:1, :], AX.X, ALU.max), dbg=f"nr{base_col}{h}{qb}")
                    kb.op("vector", lambda e, h=h, b=base_col: e.tensor_max(
                        mq8[0:1, b + h:b + h + 1], mq8[0:1, b + h:b + h + 1],
                        nrm_row[0:1, 0:1]), dbg=f"nx{base_col}{h}{qb}")
    v_nrm = SV.v
    v_sq8 = kb.op("scalar", lambda e: e.activation(mq8[0:1, :], mq8[0:1, :], AF.Sqrt),
                  waits=[(SV, v_nrm)], inc=SS, dbg="nsqrt")
    kb.op("vector", lambda e: e.tensor_mul(crow[0:1, :], mq8[0:1, 0:8], mq8[0:1, 8:16]),
          waits=[(SS, v_sq8)], dbg="ncr")
    v_crow = kb.op("vector", lambda e: e.tensor_scalar_mul(crow[0:1, :], crow[0:1, :], -1.0),
                   inc=SV, dbg="ncr2")
    kb.op("vector", lambda e: e.memset(ap3(PC, 0, 128, 0, [[2, 8]]), 1.0), dbg="pc1")
    v_pcz = SV.v
    v_pc = v_pcz
    for h in range(8):
        v_pc = bcast_rows(crow[0:1, h:h + 1], PC[:, 2 * h + 1:2 * h + 2], 1, 6,
                          [(SV, v_crow), (SV, v_pc)], f"pc{h}")
    tap("crow", PC[:, 0:16], [128, 16], [(SV, v_pc)], F32)

    if nseg < 10:
        return _finish(kb, nc)

    AR.reset(MK_NRM)
    kb.barrier("rst_s10")
    Y1 = AR.alloc("Y1", [128, 4 * 4096], BF16)
    PBUF = [AR.alloc(f"PBUF{i}", [128, 512], BF16) for i in range(3)]
    CMI = AR.alloc("CMI", [128, 512], F32)
    CM = AR.alloc("CM", [128, 4 * 512], BF16)
    RLrow = AR.alloc("RLrow", [1, 512], F32)
    RLrep = AR.alloc("RLrep", [64, 512], BF16)
    YS = AR.alloc("YS", [64, 512], BF16)
    QZ1 = AR.alloc("QZ1", [128, 512], BF16)
    PBUFB = [AR.alloc(f"PBUFB{i}", [128, 512], BF16) for i in range(3)]
    RLrowB = AR.alloc("RLrowB", [1, 512], F32)
    RLrepB = AR.alloc("RLrepB", [64, 512], BF16)
    QZ1B = AR.alloc("QZ1B", [128, 512], BF16)

    v_cm = 0
    for mloc in range(4):
        kb.op("gpsimd", g_iota(CMI[:, :], [[1, 512]], -128 * mloc, -1),
              waits=[(SV, SV.v)], dbg=f"cmi{mloc}")
        v_cm = kb.op("vector", lambda e, m=mloc: e.tensor_single_scalar(
            CM[:, m * 512:(m + 1) * 512], CMI[:, :], 0.0, ALU.is_ge),
            waits=[(SG, SG.v)], inc=SV, dbg=f"cm{mloc}")

    sb_toks = []
    chA = dict(stb0=6, pvb=5, rlb=4, pbuf=PBUF, qz=QZ1, rlrow=RLrow,
               rlrep=RLrep, pv_prev=0, pbuf_sv=[0, 0, 0])
    chB = dict(stb0=2, pvb=1, rlb=0, pbuf=PBUFB, qz=QZ1B, rlrow=RLrowB,
               rlrep=RLrepB, pv_prev=0, pbuf_sv=[0, 0, 0])
    for hp in range(4):
        for qb in range(8):
            nkc = 4 * qb + 4
            chains = [(chA, 2 * hp), (chB, 2 * hp + 1)]
            for C, h in chains:
                cc, par = h // 2, h % 2
                C["v_qz"] = kb.op("vector", lambda e, C=C, cc=cc, qb=qb, par=par:
                    e.tensor_scalar_mul(
                        C["qz"][:, 0:512],
                        QF1[:, cc * 4096 + qb * 512:cc * 4096 + (qb + 1) * 512],
                        PONE2[:, par:par + 1]),
                    waits=[(ST, C["pv_prev"])] + ([(SV, v_proj1), (SV, v_pc)]
                                                  if (hp == 0 and qb == 0) else []),
                    inc=SV, dbg=f"qz1_{h}{qb}")
            for kc in range(nkc):
                for C, h in chains:
                    stb = C["stb0"] + kc % 2
                    kb.op("tensor", lambda e, b=stb, kc=kc, cc=h // 2, C=C: e.matmul(
                        pb[b][0:128, 0:512],
                        KF1[0:128, cc * 4096 + kc * 128:cc * 4096 + (kc + 1) * 128],
                        C["qz"][:, 0:512],
                        start=True, stop=True, skip_group_check=True),
                        waits=bw(stb) + ([(SV, C["v_qz"])] if kc == 0 else []),
                        inc=ST, dbg=f"st{h}{qb}{kc}")
                    C["v_st"] = ST.v
                for C, h in chains:
                    stb = C["stb0"] + kc % 2
                    pbuf = C["pbuf"][kc % 3]
                    v_exp = kb.op("scalar", lambda e, b=stb, p=pbuf, h=h: e.activation(
                        p[:, :], pb[b][0:128, 0:512], AF.Exp,
                        scale=PC[:, 2 * h:2 * h + 1], bias=PC[:, 2 * h + 1:2 * h + 2]),
                        waits=[(ST, C["v_st"]), (SV, C["pbuf_sv"][kc % 3])],
                        inc=SS, dbg=f"ex{h}{qb}{kc}")
                    brel(stb, SS, v_exp)
                    C["w_pv"] = [(SS, v_exp)]
                    C["pbuf_sv"][kc % 3] = 0
                    if kc >= 4 * qb:
                        v_mk = kb.op("vector", lambda e, p=pbuf, m=kc - 4 * qb: e.tensor_mul(
                            p[:, :], p[:, :], CM[:, m * 512:(m + 1) * 512]),
                            waits=[(SS, v_exp), (SV, v_cm)], inc=SV, dbg=f"msk{h}{qb}{kc}")
                        C["w_pv"] = [(SV, v_mk)]
                        C["pbuf_sv"][kc % 3] = v_mk
                for C, h in chains:
                    pbuf = C["pbuf"][kc % 3]
                    kb.op("tensor", lambda e, p=pbuf, h=h, kc=kc, b=C["pvb"]: e.matmul(
                        pb[b][0:65, 0:512],
                        VA1[:, kc * 520 + h * 65:kc * 520 + (h + 1) * 65],
                        p[:, :], start=(kc == 0), stop=(kc == nkc - 1),
                        skip_group_check=True),
                        waits=C["w_pv"] + (bw(C["pvb"]) if kc == 0 else []),
                        inc=ST, dbg=f"pv{h}{qb}{kc}")
                    C["v_pv"] = ST.v
            for C, h in chains:
                cc, par = h // 2, h % 2
                C["pv_prev"] = C["v_pv"]
                v_rl = kb.op("vector", lambda e, C=C: e.reciprocal(
                    C["rlrow"][0:1, :], pb[C["pvb"]][64:65, 0:512]),
                    waits=[(ST, C["v_pv"])], inc=SV, dbg=f"rl{h}{qb}")
                v_rep = kb.mm(pb[C["rlb"]][0:64, 0:512], ones1r[0:1, 0:64],
                              C["rlrow"][0:1, :], True, True,
                              waits=[(SV, v_rl)] + bw(C["rlb"]), inc=ST, dbg=f"rlr{h}{qb}")
                v_repc = kb.op("vector", lambda e, C=C: e.tensor_copy(
                    C["rlrep"][:, :], pb[C["rlb"]][0:64, 0:512]),
                    waits=[(ST, v_rep)], inc=SV, dbg=f"rlc{h}{qb}")
                brel(C["rlb"], SV, v_repc)
                ycol = cc * 4096 + qb * 512
                if par == 0:
                    v_y1 = kb.op("vector", lambda e, ycol=ycol, C=C: e.tensor_mul(
                        Y1[0:64, ycol:ycol + 512], pb[C["pvb"]][0:64, 0:512],
                        C["rlrep"][:, :]),
                        waits=[(SV, v_repc)], inc=SV, dbg=f"y{h}{qb}")
                    brel(C["pvb"], SV, v_y1)
                else:
                    v_ys = kb.op("vector", lambda e, C=C: e.tensor_mul(
                        YS[:, :], pb[C["pvb"]][0:64, 0:512], C["rlrep"][:, :]),
                        waits=[(SV, v_repc)] + ([sb_toks[-1]] if sb_toks else []),
                        inc=SV, dbg=f"ys{h}{qb}")
                    brel(C["pvb"], SV, v_ys)
                    sb_toks.append(GD.dma(
                        bass.AP(Y1, 64 * (4 * 4096) + ycol, [[4 * 4096, 64], [1, 512]]),
                        YS[0:64, :], waits=[(SV, v_ys)], dbg=f"ysc{h}{qb}"))
    v_flash = SV.v
    tap("y1", Y1[:, 0:4 * 4096], [128, 4 * 4096], [(SV, v_flash)] + sb_toks[-1:], BF16)

    if nseg < 11:
        return _finish(kb, nc)

    WO1 = AR.alloc("WO1", [128, 4 * 1024], BF16)
    RSB = AR.alloc("RSB", [128, 1024], BF16)
    XF0 = AR.alloc("XF0", [128, 512], F32)
    GF0 = AR.alloc("GF0", [128, 512], BF16)
    SF0 = AR.alloc("SF0", [128, 512], BF16)
    OUTB = AR.alloc("OUTB", [128, 512], F32)

    k_wo1 = None
    for cc in range(4):
        k_wo1 = LD.dma(WO1[:, cc * 1024:(cc + 1) * 1024],
                       d_l1wo.ap()[cc * 128:(cc + 1) * 128, :],
                       waits=([k_cast1] if cc == 0 else []), dbg="ldwo1", group=(cc > 0))
    rs_w_toks = [[] for _ in range(4)]
    v_rsb_free = 0
    last_tk = None
    for tcg in range(32):
        kbuf = (tcg % 16) // 4
        arow = (tcg % 4) * 128 + (512 if tcg >= 16 else 0)
        for db in range(2):
            bank = (tcg * 2 + db) % 4
            for cc in range(4):
                kb.mm(pb[bank][0:128, 0:512],
                      Y1[:, cc * 4096 + tcg * 128:cc * 4096 + (tcg + 1) * 128],
                      WO1[:, cc * 1024 + db * 512:cc * 1024 + (db + 1) * 512],
                      cc == 0, cc == 3,
                      waits=(([k_wo1, (SV, v_flash)] + sb_toks[-1:] + bw(bank))
                             if cc == 0 else []),
                      dbg=f"wp{tcg}{db}")
            v_mmw = ST.v
            v_ev = kb.op("vector", lambda e, b=bank, db=db: e.tensor_copy(
                RSB[:, db * 512:(db + 1) * 512], pb[b][0:128, 0:512]),
                waits=[(ST, v_mmw)] + ([last_tk] if (db == 0 and last_tk) else []),
                dbg=f"wpe{tcg}{db}")
            brel(bank, SV, v_ev)
        v_rsb = SV.v
        tk = GD.dma(c_rs_i[kbuf].ap()[arow:arow + 128, :], RSB[:, :],
                    waits=[(SV, v_rsb)], dbg=f"rsw{tcg}")
        rs_w_toks[kbuf].append(tk)
        last_tk = tk
    v_ccrs = {}
    for kbuf in range(4):
        v_ccrs[kbuf] = kb.op("gpsimd", lambda e, k=kbuf: e.collective_compute(
            "ReduceScatter", ALU.add, replica_groups=PAIRS,
            ins=[c_rs_i[k].ap().opt()], outs=[c_rs_o[k].ap().opt()]),
            waits=list(rs_w_toks[kbuf]), inc=SC, dbg=f"RS{kbuf}")

    out_toks = []
    v_fc = 0
    for dk in range(8):
        for kbuf in range(4):
            k_x1 = LD.dma(XF0[:, :], d_x1.ap()[dk * 128:(dk + 1) * 128,
                                               kbuf * 512:(kbuf + 1) * 512],
                          waits=[(SV, v_fc)] + list(x1_toks[-1:]), dbg=f"fx{dk}{kbuf}")
            k_g1 = LD.dma(GF0[:, :], d_g1.ap()[dk * 128:(dk + 1) * 128,
                                               kbuf * 512:(kbuf + 1) * 512],
                          waits=[tok_g1, (SV, v_fc)], dbg=f"fg{dk}{kbuf}")
            k_s1 = TP.dma(SF0[:, :], c_rs_o[kbuf].ap()[:, dk * 128:(dk + 1) * 128],
                          waits=[(SC, v_ccrs[kbuf]), (SV, v_fc)] + sb_toks[-1:], transpose=True,
                          dbg=f"fs{dk}{kbuf}")
            kb.op("vector", lambda e: e.tensor_mul(OUTB[:, :], SF0[:, :], GF0[:, :]),
                  waits=[k_x1, k_g1, k_s1] + out_toks[-1:], dbg=f"fc{dk}{kbuf}a")
            v_fc = kb.op("vector", lambda e: e.tensor_add(OUTB[:, :], OUTB[:, :], XF0[:, :]),
                         inc=SV, dbg=f"fc{dk}{kbuf}b")
            out_toks.append(LD.dma(out_T.ap()[dk * 128:(dk + 1) * 128,
                                              kbuf * 512:(kbuf + 1) * 512],
                                   OUTB[:, :], waits=[(SV, v_fc)], dbg=f"fo{dk}{kbuf}"))

    return _finish(kb, nc)


def _finish(kb, nc):
    waits = [(s, s.v) for s in kb.sems.values() if s.v > 0]
    kb.wait_only("sync", waits, dbg="final_wait")
    kb.check()
    with nc.Block() as block:
        for eng_name in KB.ENGS:
            def make(en):
                def emit(eng):
                    for e2, cl in kb.prog:
                        if e2 == en:
                            cl(eng)
                return emit
            getattr(block, eng_name)(make(eng_name))
    return nc, kb



def _host_inputs(inputs, core):
    x = np.asarray(inputs["L0_attn_x"])
    xf = np.ascontiguousarray(x.reshape(B * S, D))
    rows = slice(core * TOK, (core + 1) * TOK)
    hh = core % 2
    d = {}
    d["xT"] = np.ascontiguousarray(xf[rows].T)
    dl = np.asarray(inputs["L0_attn_decay_log"]).reshape(8)
    d["dlog_c"] = np.ascontiguousarray(dl.reshape(8, 1))
    d["dlog_r"] = np.ascontiguousarray(dl.reshape(1, 8))
    d["wqT"] = np.ascontiguousarray(np.asarray(inputs["L0_attn_W_Q"]).T)
    d["wkT"] = np.ascontiguousarray(np.asarray(inputs["L0_attn_W_K"]).T)
    d["wvT"] = np.ascontiguousarray(np.asarray(inputs["L0_attn_W_V"]).T)
    d["wgT"] = np.ascontiguousarray(np.asarray(inputs["L0_attn_W_g_w"]).T)
    d["wgb_row"] = np.ascontiguousarray(np.asarray(inputs["L0_attn_W_g_b"]).reshape(1, CH0))
    d["woT"] = np.ascontiguousarray(np.asarray(inputs["L0_attn_W_O"]).T)
    d["wup"] = np.ascontiguousarray(np.asarray(inputs["L0_ffn_w_up"]))
    d["wgate"] = np.ascontiguousarray(np.asarray(inputs["L0_ffn_w_gate"]))
    d["wdown"] = np.ascontiguousarray(np.asarray(inputs["L0_ffn_w_down"]))
    sl = slice(hh * CH1, (hh + 1) * CH1)
    d["l1_wqT"] = np.ascontiguousarray(np.asarray(inputs["L1_attn_q_heads"])[sl].T)
    d["l1_wkT"] = np.ascontiguousarray(np.asarray(inputs["L1_attn_k_heads"])[sl].T)
    d["l1_wvT"] = np.ascontiguousarray(np.asarray(inputs["L1_attn_v_heads"])[sl].T)
    d["l1_woT"] = np.ascontiguousarray(np.asarray(inputs["L1_attn_output_proj"]).T[sl])
    d["l1_wfT"] = np.ascontiguousarray(np.asarray(inputs["L1_ffn_weights"]).T)
    sc = np.asarray(inputs["L1_ffn_scale"]).reshape(8, 128)
    bi = np.asarray(inputs["L1_ffn_bias"]).reshape(8, 128)
    sb = np.empty((128, 16), np.float32)
    sb[:, 0::2] = sc.T
    sb[:, 1::2] = bi.T
    d["l1_sb"] = sb
    fl = np.zeros((1, 8), np.float32)
    fl[0, 0] = 1.0 if core % 2 == 0 else 0.0
    fl[0, 1] = 0.0 if core % 2 == 0 else 1.0
    d["flags"] = fl
    hs = np.zeros((8, CH0), np.float32)
    for ch in range(CH0):
        hs[ch // DK0, ch] = 1.0
    d["hsel_ch"] = hs
    h65 = np.zeros((8, 4 * 65), np.float32)
    ps = np.zeros((8, 128), np.float32)
    for h in range(H0):
        h65[h, (h // 2) * 65:(h // 2) * 65 + 65] = 1.0
        ps[h, (h % 2) * 64:(h % 2) * 64 + 64] = 1.0
    d["hsel65"] = h65
    d["psel"] = ps
    for k in d:
        if d[k].dtype == np.float64:
            d[k] = d[k].astype(np.float32)
    return d


def kernel(**inputs):
    nc, kb = build()
    in_maps = [_host_inputs(inputs, c) for c in range(NCORES)]
    res = run_bass_kernel_spmd(nc, in_maps, core_ids=list(range(NCORES)))
    outs = []
    for c in range(NCORES):
        outs.append(np.asarray(res.results[c]["out_T"]).T)
    return np.ascontiguousarray(
        np.concatenate(outs, axis=0).reshape(B, S, D)).astype(np.float32)
